# version 01
# baseline (speedup 1.0000x reference)
import contextlib
import numpy as np
import ml_dtypes
import concourse.bass as bass
import concourse.mybir as mybir
from concourse.bass_utils import run_bass_kernel_spmd

F32 = mybir.dt.float32
BF16 = mybir.dt.bfloat16
I32 = mybir.dt.int32
AF = mybir.ActivationFunctionType
ALU = mybir.AluOpType
AX = mybir.AxisListType
BF = ml_dtypes.bfloat16

DM = 1024
DIN = 2048
NH = 32
HD = 64
NF = 4096
CH = 64
TP = 2048
TO = 2048
SEG = 256
NPAIR = SEG // 128
NCHUNK = SEG // CH
NSEG = TP // SEG
CONVC = 18
EPS = 1e-5

ENGS = ("tensor", "vector", "scalar", "gpsimd", "sync")

_TAPS = ()
_LAST_RES = None
_BARRIER_SEGS = False
_BMODE = 0
_ABLATE = set()
_NOSILU = False
_DMASER = 3


class Op:
    __slots__ = ("eng", "fn", "waits", "marked", "idx", "inc_amt", "tag")
    def __init__(self, eng, fn, idx, inc_amt, tag):
        self.eng = eng; self.fn = fn; self.idx = idx
        self.waits = []; self.marked = False
        self.inc_amt = inc_amt; self.tag = tag


class Buf:
    __slots__ = ("writer", "readers", "name")
    def __init__(self, name=""):
        self.writer = None; self.readers = []; self.name = name


class Sched:
    def __init__(self):
        self.ops = []
        self.per_eng = {e: [] for e in ENGS}
        self.last_dma = {e: None for e in ENGS}

    def op(self, eng, fn, reads=(), writes=(), extra_waits=(), dma=False, tag=""):
        import kernel as _K
        if dma and _K._DMASER & (1 if eng == "gpsimd" else 2 if eng == "sync" else 0):
            extra_waits = list(extra_waits)
            if self.last_dma[eng] is not None:
                extra_waits.append(self.last_dma[eng])
        o = Op(eng, fn, len(self.per_eng[eng]), 16 if dma else 1, tag)
        if dma:
            self.last_dma[eng] = o
        seen = set()
        def add_wait(w):
            if w is not None and id(w) not in seen and w is not o:
                seen.add(id(w)); o.waits.append(w)
        for b in reads:
            add_wait(b.writer)
        for b in writes:
            add_wait(b.writer)
            for r in b.readers:
                add_wait(r)
        for w in extra_waits:
            add_wait(w)
        for b in reads:
            b.readers.append(o)
        for b in writes:
            b.writer = o; b.readers = []
        self.ops.append(o)
        self.per_eng[eng].append(o)
        return o

    def barrier(self):
        lasts = [self.per_eng[e][-1] for e in ENGS if self.per_eng[e]]
        for e in ENGS:
            if self.per_eng[e]:
                self.op(e, lambda eng: eng.nop(), extra_waits=lasts, tag="barrier_" + e)

    def emit(self, nc, block, sems):
        for o in self.ops:
            if o.inc_amt == 16:
                o.marked = True
            for w in o.waits:
                w.marked = True
        val = {}
        for e in ENGS:
            c = 0
            for o in self.per_eng[e]:
                if o.marked:
                    c += o.inc_amt
                val[id(o)] = (e, c)
        def emit_engine(e, eng):
            highest = {x: 0 for x in ENGS}
            for o in self.per_eng[e]:
                needed = {}
                for w in o.waits:
                    we, wv = val[id(w)]
                    if we == e == "tensor" and w.inc_amt == 1 and o.inc_amt == 1:
                        continue
                    if wv > needed.get(we, 0):
                        needed[we] = wv
                for we, wv in sorted(needed.items()):
                    if wv > highest[we]:
                        eng.wait_ge(sems[we], wv)
                        highest[we] = wv
                instr = o.fn(eng)
                if o.marked:
                    if instr is None:
                        raise RuntimeError(f"marked op {o.tag} returned None")
                    instr.then_inc(sems[e], o.inc_amt)
        for e in ENGS:
            if self.per_eng[e]:
                getattr(block, e)(lambda eng, e=e: emit_engine(e, eng))


def build_program(nc, taps=()):
    import kernel as _K
    es = contextlib.ExitStack()
    tapset = set(taps)

    def din(name, shape, dtype=F32):
        return nc.dram_tensor(name, list(shape), dtype, kind="ExternalInput").ap()

    xpre_d = din("xpre", [TP, DM])
    xown_d = din("xown", [TO, DM])
    win_d = din("w_in", [DM, 2336])
    convw_d = din("convw", [128, 72])
    convb_d = din("convb", [128, CONVC])
    dtb_d = [din("dtb0", [16, 1]), din("dtb1", [16, 1])]
    alog_d = [din("alog0", [16, 1]), din("alog1", [16, 1])]
    drep_d = din("drep", [128, 16])
    wout_d = din("w_out", [DIN, DM])
    w1_d = din("w1", [NF, DM])
    b1_d = din("b1r", [1, NF])
    w2_d = din("w2", [DM, NF])
    b2_d = din("b2r", [1, DM])
    gate_d = din("gate", [128, 1])
    identb_d = din("identb", [128, 128], BF16)
    tri_d = din("tri", [128, CH], BF16)
    ind2_d = din("ind2", [2, 128], BF16)
    indn_d = din("indn16", [16, 1024], BF16)
    maska_d = din("maska", [16, SEG])
    onesb_d = din("onesb", [1, 128], BF16)
    epsc_d = din("epsc", [128, 1])
    tinyc_d = din("tinyc", [16, 1])

    out_d = nc.dram_tensor("out", [TO, DM], F32, kind="ExternalOutput").ap()
    tap_d = {}

    def scr(name, shape, dtype):
        return nc.dram_tensor(name, list(shape), dtype, kind="Internal").ap()

    xscr_d = [scr("xpscr", [TP, DM], BF16), scr("xoscr", [TO, DM], BF16)]
    xhscr = [scr(f"xhscr{i}", [DIN, SEG], BF16) for i in range(2)]
    yscr = [scr(f"yscr{i}", [SEG, DIN], BF16) for i in range(2)]
    xlnscr = scr("xlnscr", [TO, DM], BF16)
    xq1scr = scr("xq1scr", [TO, DM], BF16)
    wq1scr = scr("wq1scr", [NF, DM], BF16)
    wq2scr = scr("wq2scr", [DM, NF], BF16)
    xq2scr = [scr(f"xq2scr{i}", [SEG, NF], BF16) for i in range(2)]
    accscr = scr("accscr", [64, SEG], BF16)
    ecscr = scr("ecscr", [32, NCHUNK], BF16)

    def TT_(stack, name, shape, dtype):
        return stack.enter_context(nc.sbuf_tensor(name, list(shape), dtype))[:]
    T = lambda n, s, d: TT_(es, n, s, d)

    ident_sb = T("ident_sb", [128, 128], BF16)
    ones128_sb = T("ones128_sb", [128, 128], BF16)
    tri_sb = T("tri_sb", [128, CH], BF16)
    ind2_sb = T("ind2_sb", [2, 128], BF16)
    indn_sb = T("indn_sb", [16, 1024], BF16)
    maska_sb = T("maska_sb", [16, SEG], F32)
    onesb_sb = T("onesb_sb", [1, 128], BF16)
    convb_sb = T("convb_sb", [128, CONVC], F32)
    drep_sb = T("drep_sb", [128, 16], F32)
    gate_sb = T("gate_sb", [128, 1], F32)
    epsc_sb = T("epsc_sb", [128, 1], F32)
    tinyc_sb = T("tinyc_sb", [16, 1], F32)
    convw_sb = T("convw_sb", [128, 72], F32)
    acol_sb = [T(f"acol{g}", [16, 1], F32) for g in range(2)]
    dtb_sb = [T(f"dtbs{g}", [16, 1], F32) for g in range(2)]
    S_sb = [T(f"S{g}", [128, 1024], BF16) for g in range(2)]
    halo_sb = T("halo_sb", [128, CONVC * 3], BF16)
    sw_scale = {k: T(k + "_scale", [128, 1], F32) for k in ("w1", "w2")}
    sw_mean = {k: T(k + "_mean", [128, 1], F32) for k in ("w1", "w2")}
    sw_acc = {k: T(k + "_acc", [128, 32], F32) for k in ("w1", "w2")}
    sw_accb = T("sw_accb", [128, 32], BF16)
    b1row_sb = T("b1row_sb", [1, NF], BF16)
    b2row_sb = T("b2row_sb", [1, DM], BF16)
    st_sb = T("st_sb", [128, 16], F32)
    pln1_sb = T("pln1_sb", [128, DM], F32)
    xln1_sb = T("xln1_sb", [128, DM], F32)
    xln1b_sb = T("xln1b_sb", [128, DM], BF16)

    PSY = es.enter_context(nc.psum_tensor("psy", [128, 1024], F32))[:]
    PSO = es.enter_context(nc.psum_tensor("pso", [128, 1024], F32))[:]
    PST = es.enter_context(nc.psum_tensor("pst", [128, 1024], F32))[:]
    PG = [es.enter_context(nc.psum_tensor(f"pg{i}", [128, 512], F32))[:] for i in range(2)]
    PT = PSY.bitcast(BF16)

    sems = {e: es.enter_context(nc.semaphore("sem_" + e)) for e in ENGS}
    block = es.enter_context(nc.Block())

    s = Sched()
    bufs = {}
    def B(name):
        if name not in bufs:
            bufs[name] = Buf(name)
        return bufs[name]

    def tap(name, src_ap, shape, dtype, rbufs):
        if name in tapset:
            tap_d[name] = nc.dram_tensor("tap_" + name, list(shape), dtype,
                                         kind="ExternalOutput").ap()
            s.op("gpsimd", lambda e: e.dma_start(out=tap_d[name], in_=src_ap),
                 reads=[B(r) for r in rbufs], dma=True, tag="tap_" + name)

    aes = contextlib.ExitStack()
    A = lambda n, s_, d: TT_(aes, n, s_, d)
    win_sb = A("win_sb", [128, 8 * 2336], BF16)
    wout_sb = A("wout_sb", [128, 16 * DM], BF16)
    diagw_sb = A("diagw_sb", [128, 72 * 128], BF16)
    xtm_sb = A("xtm_sb", [128, DM], BF16)
    xfm_sb = A("xfm_sb", [128, 8 * SEG], BF16)
    xbc_sb = A("xbc_sb", [128, 2 * (SEG + 3)], BF16)
    xhs_sb = A("xhs_sb", [128, 16 * SEG], BF16)
    bfm_sb = A("bfm_sb", [128, SEG], BF16)
    cfm_sb = A("cfm_sb", [128, SEG], BF16)
    xT_sb = A("xT_sb", [128, NPAIR * DIN], BF16)
    xddt_sb = A("xddt_sb", [128, NPAIR * DIN], BF16)
    bt_sb = A("bt_sb", [128, NPAIR * 128], BF16)
    dtT_sb = [A(f"dtT{g}", [16, SEG], F32) for g in range(2)]
    accp_sb = [A(f"accp{g}", [16, SEG], F32) for g in range(2)]
    acchi_sb = [A(f"acchi{g}", [16, SEG], BF16) for g in range(2)]
    acclo_sb = [A(f"acclo{g}", [16, SEG], BF16) for g in range(2)]
    ea_sb = [A(f"ea{g}", [16, SEG], BF16) for g in range(2)]
    ddt_sb = [A(f"ddt{g}", [16, SEG], BF16) for g in range(2)]
    dtb16_sb = [A(f"dtb16_{g}", [16, SEG], BF16) for g in range(2)]
    sp1_sb = A("sp1_sb", [16, SEG], F32)
    sp2_sb = A("sp2_sb", [16, SEG], F32)
    kt_sb = [A(f"kt{g}", [2, 2 * NPAIR * 1024], BF16) for g in range(2)]
    eat_sb = A("eat_sb", [128, NPAIR * NH], BF16)
    dttm_sb = A("dttm_sb", [128, NPAIR * NH], BF16)
    ddttm_sb = A("ddttm_sb", [128, NPAIR * NH], BF16)
    eb_sb = A("eb_sb", [128, NCHUNK * NH], BF16)
    ecrow_sb = A("ecrow_sb", [1, NCHUNK * NH], BF16)
    ecc_sb = A("ecc_sb", [16, 2 * NCHUNK], BF16)
    lt_sb = A("lt_sb", [128, 1024], BF16)
    m2t_sb = A("m2t_sb", [128, 1024], BF16)
    bcm_sb = A("bcm_sb", [128, CH], BF16)
    ytm_sb = A("ytm_sb", [128, NPAIR * DIN], BF16)
    yfm_sb = A("yfm_sb", [128, 16 * SEG], BF16)
    wslab_sb = [A("wslab0", [128, DM], F32)] * 2
    WSLN = ["wsl0", "wsl0"]
    wqb_sb = A("wqb_sb", [128, DM], BF16)
    wint_sb = A("wint_sb", [128, DM], I32)
    q1o_sb = wqb_sb

    def ld(dst, src, bufname, q="sync"):
        return s.op(q, lambda e: e.dma_start(out=dst, in_=src),
                    writes=[B(bufname)], dma=True, tag="ld_" + bufname)

    ld(ident_sb, identb_d, "ident")
    ld(tri_sb, tri_d, "tri")
    ld(ind2_sb, ind2_d, "ind2")
    ld(indn_sb, indn_d, "indn")
    ld(maska_sb, maska_d, "maska")
    ld(onesb_sb, onesb_d, "onesb")
    ld(convb_sb, convb_d, "convb")
    ld(drep_sb, drep_d, "drep")
    ld(gate_sb, gate_d, "gate")
    ld(epsc_sb, epsc_d, "epsc")
    ld(tinyc_sb, tinyc_d, "tinyc")
    ld(convw_sb, convw_d, "convw")
    ld(b1row_sb, b1_d, "b1row", q="gpsimd")
    ld(b2row_sb, b2_d, "b2row", q="gpsimd")
    s.op("gpsimd", lambda e: e.dma_start(
        out=win_sb.rearrange("p (k c) -> p k c", k=8),
        in_=win_d.rearrange("(k p) c -> p k c", p=128)),
        writes=[B("win")], dma=True, tag="ld_win")
    s.op("gpsimd", lambda e: e.dma_start(
        out=wout_sb.rearrange("p (k c) -> p k c", k=16),
        in_=wout_d.rearrange("(k p) c -> p k c", p=128)),
        writes=[B("wout")], dma=True, tag="ld_wout")
    s.op("vector", lambda e: e.memset(ones128_sb, 1.0), writes=[B("ones128")], tag="ones128")
    for wf in ((AF.Exp,) if _K._NOSILU else (AF.Exp, AF.Silu, AF.Gelu, AF.Exp)):
        s.op("scalar", lambda e, wf=wf: e.activation(out=sp1_sb[0:16, 0:16],
                                                     in_=maska_sb[0:16, 0:16], func=wf),
             reads=[B("maska")], writes=[B("sp1")], tag=f"warm_{wf.name}")
    for j in range(72):
        s.op("vector", lambda e, j=j: e.tensor_scalar_mul(
            out=diagw_sb.rearrange("p (j c) -> p j c", j=72)[:, j, :],
            in0=ident_sb, scalar1=convw_sb[:, j:j + 1]),
            reads=[B("ident"), B("convw")], writes=[B("diagw")], tag=f"dw{j}")
    s.op("vector", lambda e: e.memset(halo_sb, 0.0), writes=[B("halo")], tag="halo0")
    for g in range(2):
        s.op("vector", lambda e, g=g: e.memset(S_sb[g], 0.0), writes=[B(f"S{g}")], tag="S0")
    for g in range(2):
        ld(acol_sb[g], alog_d[g], f"acol{g}")
        ld(dtb_sb[g], dtb_d[g], f"dtb{g}")

    def bcast_row(row_ap, rowbuf, n, dst, dstbuf):
        nb = (n + 511) // 512
        for i in range(nb):
            w = min(512, n - i * 512)
            s.op("tensor", lambda e, i=i, w=w: e.matmul(
                PG[0][:, 0:w], onesb_sb[0:1, :], row_ap[0:1, i * 512:i * 512 + w],
                start=True, stop=True),
                reads=[B("onesb"), B(rowbuf)], writes=[B("pg0")], tag="bcmm")
            s.op("vector", lambda e, i=i, w=w: e.tensor_copy(
                out=dst[:, i * 512:i * 512 + w], in_=PG[0][:, 0:w]),
                reads=[B("pg0")], writes=[B(dstbuf)], tag="bcev")

    def wq_prep(wd, rows, cols, scrd, key, scrbuf):
        tiles = [(it, jc) for it in range(rows // 128) for jc in range(cols // DM)]
        for idx, (it, jc) in enumerate(tiles):
            si = idx % 2
            sl = wslab_sb[si]
            s.op("sync", lambda e, it=it, jc=jc, sl=sl: e.dma_start(
                out=sl, in_=wd[it * 128:(it + 1) * 128, jc * DM:(jc + 1) * DM]),
                writes=[B(WSLN[si])], dma=True, tag=f"{key}l1_{idx}")
            s.op("scalar", lambda e, idx=idx, sl=sl: e.activation(
                out=sl, in_=sl, func=AF.Abs, accum_out=sw_acc[key][:, idx:idx + 1]),
                reads=[B(WSLN[si])], writes=[B(WSLN[si]), B(key + "acc")], tag=f"{key}a{idx}")
        ntot = len(tiles)
        s.op("vector", lambda e, ntot=ntot: e.tensor_copy(
            out=sw_accb[:, 0:ntot], in_=sw_acc[key][:, 0:ntot]),
            reads=[B(key + "acc")], writes=[B("accb")], tag=f"{key}cb")
        s.op("tensor", lambda e, ntot=ntot: e.matmul(
            PG[0][:, 0:ntot], ones128_sb, sw_accb[:, 0:ntot], start=True, stop=True),
            reads=[B("ones128"), B("accb")], writes=[B("pg0")], tag=f"{key}rm")
        s.op("vector", lambda e, ntot=ntot: e.tensor_reduce(
            out=sw_mean[key], in_=PG[0][:, 0:ntot], axis=AX.X, op=ALU.add),
            reads=[B("pg0")], writes=[B(key + "mean")], tag=f"{key}rd")
        N = float(rows * cols)
        s.op("vector", lambda e: e.tensor_scalar_mul(out=sw_mean[key], in0=sw_mean[key],
                                                     scalar1=1.0 / N),
             reads=[B(key + "mean")], writes=[B(key + "mean")], tag=f"{key}mn")
        s.op("vector", lambda e: e.tensor_scalar_max(out=sw_scale[key], in0=sw_mean[key],
                                                     scalar1=1e-5),
             reads=[B(key + "mean")], writes=[B(key + "sc")], tag=f"{key}mx")
        s.op("vector", lambda e: e.reciprocal(out=sw_scale[key], in_=sw_scale[key]),
             reads=[B(key + "sc")], writes=[B(key + "sc")], tag=f"{key}rc")
        for idx, (it, jc) in enumerate(tiles):
            si = idx % 2
            sl = wslab_sb[si]
            s.op("sync", lambda e, it=it, jc=jc, sl=sl: e.dma_start(
                out=sl, in_=wd[it * 128:(it + 1) * 128, jc * DM:(jc + 1) * DM]),
                writes=[B(WSLN[si])], dma=True, tag=f"{key}l2_{idx}")
            s.op("scalar", lambda e, sl=sl: e.activation(
                out=wint_sb, in_=sl, func=AF.Copy, scale=sw_scale[key][:, 0:1]),
                reads=[B(WSLN[si]), B(key + "sc")], writes=[B("wint")], tag=f"{key}r{idx}")
            s.op("vector", lambda e: e.tensor_scalar(
                out=wqb_sb, in0=wint_sb, scalar1=1.0, scalar2=-1.0,
                op0=ALU.min, op1=ALU.max),
                reads=[B("wint")], writes=[B("wqb")], tag=f"{key}c{idx}")
            s.op("gpsimd", lambda e, it=it, jc=jc: e.dma_start(
                out=scrd[it * 128:(it + 1) * 128, jc * DM:(jc + 1) * DM], in_=wqb_sb),
                reads=[B("wqb")], writes=[B(scrbuf)], dma=True, tag=f"{key}s{idx}")

    def attn_segment(seg, phase):
        full = (phase == 2)
        t0 = seg * SEG
        xd = xown_d if full else xpre_d
        xscrd = xscr_d[1] if full else xscr_d[0]
        si = seg % 2
        pfx = f"p{phase}s{seg}_"
        istap = full and seg == 0

        for pp in range(NPAIR):
            s.op("gpsimd", lambda e, pp=pp: e.dma_start(
                out=xtm_sb,
                in_=xd[t0 + pp * 128:t0 + (pp + 1) * 128, :]),
                writes=[B("xtm")], dma=True, tag=pfx + f"xtm{pp}")
            s.op("gpsimd", lambda e, pp=pp: e.dma_start(
                out=xscrd[t0 + pp * 128:t0 + (pp + 1) * 128, :], in_=xtm_sb),
                reads=[B("xtm")], writes=[B("xscr")], dma=True, tag=pfx + f"xscrw{pp}")
        for k in range(8):
            s.op("sync", lambda e, k=k: e.dma_start(
                out=xfm_sb[:, k * SEG:(k + 1) * SEG],
                in_=xscrd[t0:t0 + SEG, k * 128:(k + 1) * 128], transpose=True),
                reads=[B("xscr")], writes=[B("xfm")], dma=True, tag=pfx + f"xfm{k}")

        mtiles = list(range(16)) + [16] + ([17] if (full or seg == NSEG - 1) else [])
        xbcv = xbc_sb.rearrange("p (c q) -> p c q", c=2)
        halov = halo_sb.rearrange("p (c q) -> p c q", q=3)
        for cc in mtiles:
            halo_only = (cc == 17 and not full)
            pid = cc % 2
            for k in range(8):
                s.op("tensor", lambda e, cc=cc, k=k, pid=pid: e.matmul(
                    PG[pid][:, 0:SEG],
                    win_sb.rearrange("p (k c) -> p k c", k=8)[:, k, cc * 128:(cc + 1) * 128],
                    xfm_sb[:, k * SEG:(k + 1) * SEG],
                    start=(k == 0), stop=(k == 7)),
                    reads=[B("win"), B("xfm")], writes=[B(f"pg{pid}")], tag=pfx + f"ip{cc}_{k}")
            s.op("vector", lambda e, cc=cc, pid=pid: e.tensor_copy(
                out=xbcv[:, pid, 3:3 + SEG], in_=PG[pid][:, 0:SEG]),
                reads=[B(f"pg{pid}")], writes=[B(f"xbcr{pid}")], tag=pfx + f"ipe{cc}")
            if not halo_only:
                s.op("vector", lambda e, cc=cc, pid=pid: e.tensor_copy(
                    out=xbcv[:, pid, 0:3], in_=halov[:, cc, :]),
                    reads=[B("halo")], writes=[B(f"xbcr{pid}")], tag=pfx + f"hin{cc}")
            s.op("vector", lambda e, cc=cc, pid=pid: e.tensor_copy(
                out=halov[:, cc, :], in_=xbcv[:, pid, SEG:SEG + 3]),
                reads=[B(f"xbcr{pid}")], writes=[B("halo")], tag=pfx + f"hsv{cc}")
            if halo_only:
                continue
            for k in range(4):
                s.op("tensor", lambda e, cc=cc, k=k, pid=pid: e.matmul(
                    PG[pid][:, 0:SEG],
                    diagw_sb.rearrange("p (j c) -> p j c", j=72)[:, cc * 4 + k, :],
                    xbcv[:, pid, k:k + SEG],
                    start=(k == 0), stop=(k == 3)),
                    reads=[B("diagw"), B(f"xbcr{pid}")],
                    writes=[B(f"pg{pid}")], tag=pfx + f"cv{cc}_{k}")
            if cc < 16:
                dst = xhs_sb[:, cc * SEG:(cc + 1) * SEG]; dbuf = "xhs"
            elif cc == 16:
                dst = bfm_sb; dbuf = "bfm"
            else:
                dst = cfm_sb; dbuf = "cfm"
            s.op("scalar", lambda e, cc=cc, pid=pid, dst=dst: e.activation(
                out=dst, in_=PG[pid][:, 0:SEG], func=(AF.Relu if _K._NOSILU else AF.Silu), bias=convb_sb[:, cc:cc + 1]),
                reads=[B(f"pg{pid}"), B("convb")], writes=[B(dbuf)], tag=pfx + f"si{cc}")
        for g in range(2):
            for k in range(8):
                s.op("tensor", lambda e, g=g, k=k: e.matmul(
                    PG[0][0:16, 0:SEG],
                    win_sb.rearrange("p (k c) -> p k c", k=8)[:, k, 2304 + 16 * g:2304 + 16 * (g + 1)],
                    xfm_sb[:, k * SEG:(k + 1) * SEG],
                    start=(k == 0), stop=(k == 7)),
                    reads=[B("win"), B("xfm")], writes=[B("pg0")], tag=pfx + f"dt{g}_{k}")
            s.op("scalar", lambda e, g=g: e.activation(
                out=sp1_sb, in_=PG[0][0:16, 0:SEG], func=AF.Abs, bias=dtb_sb[g][:, 0:1]),
                reads=[B("pg0"), B(f"dtb{g}")], writes=[B("sp1")], tag=pfx + f"spa{g}")
            s.op("scalar", lambda e: e.activation(out=sp2_sb, in_=sp1_sb, func=AF.Exp, scale=-1.0),
                 reads=[B("sp1")], writes=[B("sp2")], tag=pfx + f"spe{g}")
            s.op("scalar", lambda e: e.activation(out=sp2_sb, in_=sp2_sb, func=AF.Ln, bias=1.0),
                 reads=[B("sp2")], writes=[B("sp2")], tag=pfx + f"spl{g}")
            s.op("scalar", lambda e, g=g: e.activation(
                out=sp1_sb, in_=PG[0][0:16, 0:SEG], func=AF.Relu, bias=dtb_sb[g][:, 0:1]),
                reads=[B("pg0"), B(f"dtb{g}")], writes=[B("sp1")], tag=pfx + f"spr{g}")
            s.op("vector", lambda e, g=g: e.tensor_add(out=dtT_sb[g], in0=sp1_sb, in1=sp2_sb),
                 reads=[B("sp1"), B("sp2")], writes=[B(f"dtT{g}")], tag=pfx + f"spadd{g}")
            s.op("scalar", lambda e, g=g: e.activation(out=sp1_sb, in_=dtT_sb[g], func=AF.Ln,
                                                        bias=tinyc_sb[:, 0:1]),
                reads=[B(f"dtT{g}"), B("tinyc")], writes=[B("sp1")], tag=pfx + f"adtln{g}")
            s.op("scalar", lambda e, g=g: e.activation(
                out=sp2_sb, in_=sp1_sb, func=AF.Exp, bias=acol_sb[g][:, 0:1]),
                reads=[B("sp1"), B(f"acol{g}")], writes=[B("sp2")], tag=pfx + f"adtex{g}")
            s.op("vector", lambda e, g=g: e.tensor_tensor_scan(
                out=accp_sb[g], data0=maska_sb, data1=sp2_sb,
                initial=0.0, op0=ALU.mult, op1=ALU.add),
                reads=[B("maska"), B("sp2")], writes=[B(f"accp{g}")], tag=pfx + f"scan{g}")
            s.op("vector", lambda e, g=g: e.tensor_scalar_mul(
                out=acchi_sb[g], in0=accp_sb[g], scalar1=-1.0),
                reads=[B(f"accp{g}")], writes=[B(f"acchi{g}")], tag=pfx + f"hi{g}")
            s.op("vector", lambda e, g=g: e.scalar_tensor_tensor(
                out=acclo_sb[g], in0=accp_sb[g], scalar=-1.0, in1=acchi_sb[g],
                op0=ALU.mult, op1=ALU.subtract),
                reads=[B(f"accp{g}"), B(f"acchi{g}")], writes=[B(f"acclo{g}")], tag=pfx + f"lo{g}")
            s.op("scalar", lambda e, g=g: e.activation(out=ea_sb[g], in_=accp_sb[g], func=AF.Exp,
                                                        scale=-1.0),
                 reads=[B(f"accp{g}")], writes=[B(f"ea{g}")], tag=pfx + f"ea{g}")
            s.op("vector", lambda e, g=g: e.tensor_tensor(
                out=sp1_sb.rearrange("h (c l) -> h c l", l=CH),
                in0=accp_sb[g].rearrange("h (c l) -> h c l", l=CH),
                in1=accp_sb[g].rearrange("h (c l) -> h c l", l=CH)[:, :, CH - 1:CH]
                    .broadcast_to([16, NCHUNK, CH]), op=ALU.subtract),
                reads=[B(f"accp{g}")], writes=[B("sp1")], tag=pfx + f"dda{g}")
            s.op("scalar", lambda e: e.activation(out=sp2_sb, in_=sp1_sb, func=AF.Exp),
                 reads=[B("sp1")], writes=[B("sp2")], tag=pfx + f"dde{g}")
            s.op("vector", lambda e, g=g: e.tensor_tensor(
                out=ddt_sb[g], in0=sp2_sb, in1=dtT_sb[g], op=ALU.mult),
                reads=[B("sp2"), B(f"dtT{g}")], writes=[B(f"ddt{g}")], tag=pfx + f"ddm{g}")
            if full:
                s.op("vector", lambda e, g=g: e.tensor_copy(out=dtb16_sb[g], in_=dtT_sb[g]),
                     reads=[B(f"dtT{g}")], writes=[B(f"dtb16{g}")], tag=pfx + f"d16{g}")

        if istap:
            tap("xfm", xfm_sb, [128, 8 * SEG], BF16, ["xfm"])
            tap("xhs", xhs_sb, [128, 16 * SEG], BF16, ["xhs"])
            tap("bfm", bfm_sb, [128, SEG], BF16, ["bfm"])
            tap("cfm", cfm_sb, [128, SEG], BF16, ["cfm"])
            tap("dtT0", dtT_sb[0], [16, SEG], F32, ["dtT0"])
            tap("accp0", accp_sb[0], [16, SEG], F32, ["accp0"])
            tap("ea0", ea_sb[0], [16, SEG], BF16, ["ea0"])
            tap("ddt0", ddt_sb[0], [16, SEG], BF16, ["ddt0"])
            tap("ddt1", ddt_sb[1], [16, SEG], BF16, ["ddt1"])
            tap("dtT1", dtT_sb[1], [16, SEG], F32, ["dtT1"])
            tap("accp1", accp_sb[1], [16, SEG], F32, ["accp1"])

        tp_jobs = [(ddt_sb, ddttm_sb, "ddt", "ddttm")]
        if full:
            tp_jobs += [(ea_sb, eat_sb, "ea", "eat"), (dtb16_sb, dttm_sb, "dtb16", "dttm")]
        for (srcs, dst, sname, dname) in tp_jobs:
            for p in range(NPAIR):
                for g in range(2):
                    s.op("tensor", lambda e, srcs=srcs, p=p, g=g: e.transpose(
                        PT[:, (p * NH + g * 16):(p * NH + g * 16) + 16],
                        srcs[g][0:16, p * 128:(p + 1) * 128], ident_sb[0:16, 0:16]),
                        reads=[B(f"{sname}{g}"), B("ident")], writes=[B("psy")],
                        tag=pfx + f"t{sname}{p}{g}")
            s.op("vector", lambda e, dst=dst: e.tensor_copy(out=dst, in_=PT[:, 0:NPAIR * NH]),
                 reads=[B("psy")], writes=[B(dname)], tag=pfx + f"te{sname}")
        for p in range(NPAIR):
            s.op("tensor", lambda e, p=p: e.transpose(
                PT[:, p * 128:(p + 1) * 128], bfm_sb[:, p * 128:(p + 1) * 128], ident_sb),
                reads=[B("bfm"), B("ident")], writes=[B("psy")], tag=pfx + f"tb{p}")
        s.op("vector", lambda e: e.tensor_copy(out=bt_sb, in_=PT[:, 0:NPAIR * 128]),
             reads=[B("psy")], writes=[B("bt")], tag=pfx + "tbe")

        for g in range(2):
            s.op("vector", lambda e, g=g: e.tensor_copy(
                out=ecc_sb[:, g * NCHUNK:(g + 1) * NCHUNK],
                in_=ea_sb[g].rearrange("h (c l) -> h c l", l=CH)[:, :, CH - 1]),
                reads=[B(f"ea{g}")], writes=[B("ecc")], tag=pfx + f"ecc{g}")
        s.op("gpsimd", lambda e: e.dma_start(
            out=ecscr.rearrange("(g h) c -> h g c", g=2),
            in_=ecc_sb.rearrange("h (g c) -> h g c", g=2)),
            reads=[B("ecc")], writes=[B("ecscr")], dma=True, tag=pfx + "ecw")
        s.op("sync", lambda e: e.dma_start(
            out=ecrow_sb,
            in_=ecscr.rearrange("h c -> (h c)")[None, :]),
            reads=[B("ecscr")], writes=[B("ecrow")], dma=True, tag=pfx + "ec")
        s.op("tensor", lambda e: e.matmul(
            PG[0][:, 0:NCHUNK * NH], onesb_sb[0:1, :], ecrow_sb[0:1, :],
            start=True, stop=True),
            reads=[B("ecrow"), B("onesb")], writes=[B("pg0")], tag=pfx + "ebm")
        s.op("vector", lambda e: e.tensor_copy(out=eb_sb, in_=PG[0][:, 0:NCHUNK * NH]),
             reads=[B("pg0")], writes=[B("eb")], tag=pfx + "ebe")

        s.op("gpsimd", lambda e, si=si: e.dma_start(
            out=xhscr[si].rearrange("(c p) t -> p c t", p=128),
            in_=xhs_sb.rearrange("p (c t) -> p c t", c=16)),
            reads=[B("xhs")], writes=[B(f"xhscr{si}")], dma=True, tag=pfx + "xhw")
        for p in range(NPAIR):
            s.op("sync", lambda e, p=p, si=si: e.dma_start(
                out=xT_sb[:, p * DIN:(p + 1) * DIN],
                in_=xhscr[si][:, p * 128:(p + 1) * 128], transpose=True),
                reads=[B(f"xhscr{si}")], writes=[B("xT")], dma=True, tag=pfx + f"xT{p}")
        for p in range(NPAIR):
            s.op("vector", lambda e, p=p: e.tensor_tensor(
                out=xddt_sb[:, p * DIN:(p + 1) * DIN].rearrange("p (h q) -> p h q", h=NH),
                in0=xT_sb[:, p * DIN:(p + 1) * DIN].rearrange("p (h q) -> p h q", h=NH),
                in1=ddttm_sb.rearrange("p (r h) -> p r h", r=NPAIR)[:, p, :, None]
                    .broadcast_to([128, NH, HD]),
                op=ALU.mult),
                reads=[B("xT"), B("ddttm")], writes=[B("xddt")], tag=pfx + f"xd{p}")
        if istap:
            tap("eat", eat_sb, [128, NPAIR * NH], BF16, ["eat"])
            tap("dttm", dttm_sb, [128, NPAIR * NH], BF16, ["dttm"])
            tap("ddttm", ddttm_sb, [128, NPAIR * NH], BF16, ["ddttm"])
            tap("xT", xT_sb, [128, NPAIR * DIN], BF16, ["xT"])
            tap("xddt", xddt_sb, [128, NPAIR * DIN], BF16, ["xddt"])
            tap("bt", bt_sb, [128, NPAIR * 128], BF16, ["bt"])
            tap("eb", eb_sb, [128, NCHUNK * NH], BF16, ["eb"])

        if full:
            for g in range(2):
                for hl, src, sname in ((0, acchi_sb[g], f"acchi{g}"), (1, acclo_sb[g], f"acclo{g}")):
                    s.op("gpsimd", lambda e, g=g, hl=hl, src=src: e.dma_start(
                        out=accscr[g * 32 + hl * 16:g * 32 + hl * 16 + 16, :], in_=src),
                        reads=[B(sname)], writes=[B("accscr")], dma=True, tag=pfx + f"ktw{g}{hl}")
                for hl in range(2):
                    for pp in range(NPAIR):
                        s.op("sync", lambda e, g=g, hl=hl, pp=pp: e.dma_start(
                            out=kt_sb[g].rearrange("r (x p h l) -> r x p h l", x=2, p=NPAIR, h=16)
                                [:, hl, pp, :, :],
                            in_=accscr[g * 32 + hl * 16:g * 32 + hl * 16 + 16, :]
                                .rearrange("h (p r l) -> r p h l", p=NPAIR, r=2)[:, pp, :, :]),
                            reads=[B("accscr")], writes=[B(f"kt{g}")], dma=True,
                            tag=pfx + f"kt{g}{hl}{pp}")

        ablate_pair = (f"pair{phase}" in _K._ABLATE)
        for p in range(NPAIR) if not ablate_pair else []:
            if full:
                for ci in range(2):
                    s.op("tensor", lambda e, p=p, ci=ci: e.matmul(
                        PST[ci * 64:(ci + 1) * 64, 0:CH],
                        bfm_sb[:, (p * 2 + ci) * CH:(p * 2 + ci + 1) * CH],
                        cfm_sb[:, (p * 2 + ci) * CH:(p * 2 + ci + 1) * CH],
                        start=True, stop=True, skip_group_check=True),
                        reads=[B("bfm"), B("cfm")], writes=[B("pst")], tag=pfx + f"cb{p}{ci}")
                s.op("vector", lambda e: e.tensor_tensor(
                    out=bcm_sb, in0=PST[:, 0:CH], in1=tri_sb, op=ALU.mult),
                    reads=[B("pst"), B("tri")], writes=[B("bcm")], tag=pfx + f"bcm{p}")
            for g in range(2):
                sg = f"{p}{g}"
                if full:
                    ktv = kt_sb[g].rearrange("r (x p h l) -> r x p (h l)", x=2, p=NPAIR, h=16)
                    for half in range(2):
                        cs = slice(half * 512, (half + 1) * 512)
                        s.op("tensor", lambda e, p=p, cs=cs, ktv=ktv: e.matmul(
                            PSY[:, cs], ind2_sb, ktv[:, 0, p, cs],
                            start=True, stop=False, skip_group_check=True),
                            reads=[B(f"kt{g}"), B("ind2")], writes=[B("psy")], tag=pfx + "d1" + sg)
                        s.op("tensor", lambda e, p=p, cs=cs, ktv=ktv: e.matmul(
                            PSY[:, cs], ind2_sb, ktv[:, 1, p, cs],
                            start=False, stop=False, skip_group_check=True),
                            reads=[B(f"kt{g}")], writes=[B("psy")], tag=pfx + "d2" + sg)
                        s.op("tensor", lambda e, p=p, g=g, cs=cs: e.matmul(
                            PSY[:, cs], acchi_sb[g][0:16, p * 128:(p + 1) * 128],
                            indn_sb[:, cs],
                            start=False, stop=False, skip_group_check=True),
                            reads=[B(f"acchi{g}"), B("indn")], writes=[B("psy")], tag=pfx + "d3" + sg)
                        s.op("tensor", lambda e, p=p, g=g, cs=cs: e.matmul(
                            PSY[:, cs], acclo_sb[g][0:16, p * 128:(p + 1) * 128],
                            indn_sb[:, cs],
                            start=False, stop=True, skip_group_check=True),
                            reads=[B(f"acclo{g}"), B("indn")], writes=[B("psy")], tag=pfx + "d4" + sg)
                    s.op("vector", lambda e: e.tensor_scalar_min(out=m2t_sb, in0=PSY, scalar1=0.0),
                         reads=[B("psy")], writes=[B("m2t")], tag=pfx + "dm" + sg)
                    s.op("scalar", lambda e: e.activation(out=lt_sb, in_=m2t_sb, func=AF.Exp),
                         reads=[B("m2t")], writes=[B("lt")], tag=pfx + "lx" + sg)
                    s.op("vector", lambda e, p=p, g=g: e.tensor_tensor(
                        out=lt_sb.rearrange("p (h l) -> p h l", h=16),
                        in0=lt_sb.rearrange("p (h l) -> p h l", h=16),
                        in1=dttm_sb.rearrange("p (r h) -> p r h", r=NPAIR)
                            [:, p, g * 16:(g + 1) * 16, None].broadcast_to([128, 16, CH]),
                        op=ALU.mult),
                        reads=[B("lt"), B("dttm")], writes=[B("lt")], tag=pfx + "ld" + sg)
                    s.op("vector", lambda e: e.tensor_tensor(
                        out=m2t_sb.rearrange("p (h l) -> p h l", h=16),
                        in0=lt_sb.rearrange("p (h l) -> p h l", h=16),
                        in1=bcm_sb[:, None, :].broadcast_to([128, 16, CH]),
                        op=ALU.mult),
                        reads=[B("lt"), B("bcm")], writes=[B("m2t")], tag=pfx + "m2" + sg)
                    if istap and p == 0 and g == 0:
                        tap("m2t", m2t_sb, [128, 1024], BF16, ["m2t"])
                    if istap and p == 0 and g == 1:
                        tap("lt1", lt_sb, [128, 1024], BF16, ["lt"])
                        tap("m2t1", m2t_sb, [128, 1024], BF16, ["m2t"])
                    for h in range(16):
                        hh = g * 16 + h
                        for ci in range(2):
                            s.op("tensor", lambda e, p=p, h=h, hh=hh, ci=ci: e.matmul(
                                PSY[ci * 64:(ci + 1) * 64, h * 64:(h + 1) * 64],
                                m2t_sb.rearrange("p (h l) -> p h l", h=16)[ci * 64:(ci + 1) * 64, h, :],
                                xT_sb[ci * 64:(ci + 1) * 64,
                                      p * DIN + hh * 64:p * DIN + (hh + 1) * 64],
                                start=True, stop=True, skip_group_check=True),
                                reads=[B("m2t"), B("xT")], writes=[B("psy")],
                                tag=pfx + f"yd{h}{ci}" + sg)
                for ci in range(2):
                    cloc = p * 2 + ci
                    if full:
                        for half in range(2):
                            s.op("tensor", lambda e, cloc=cloc, ci=ci, g=g, half=half: e.matmul(
                                PSO[ci * 64:(ci + 1) * 64, half * 512:(half + 1) * 512],
                                cfm_sb[:, cloc * CH:(cloc + 1) * CH],
                                S_sb[g][:, half * 512:(half + 1) * 512],
                                start=True, stop=True, skip_group_check=True),
                                reads=[B("cfm"), B(f"S{g}")], writes=[B("pso")],
                                tag=pfx + f"yo{ci}{half}" + sg)
                    for half in range(2):
                        s.op("tensor", lambda e, p=p, ci=ci, g=g, half=half: e.matmul(
                            PST[:, half * 512:(half + 1) * 512],
                            bt_sb[ci * 64:(ci + 1) * 64, p * 128:(p + 1) * 128],
                            xddt_sb[ci * 64:(ci + 1) * 64,
                                    p * DIN + g * 1024 + half * 512:
                                    p * DIN + g * 1024 + (half + 1) * 512],
                            start=True, stop=True, skip_group_check=True),
                            reads=[B("bt"), B("xddt")], writes=[B("pst")],
                            tag=pfx + f"st{ci}{half}" + sg)
                    s.op("vector", lambda e, cloc=cloc, g=g: e.tensor_tensor(
                        out=S_sb[g].rearrange("p (h q) -> p h q", h=16),
                        in0=S_sb[g].rearrange("p (h q) -> p h q", h=16),
                        in1=eb_sb.rearrange("p (g h c) -> p g h c", g=2, h=16)
                            [:, g, :, cloc, None].broadcast_to([128, 16, HD]),
                        op=ALU.mult),
                        reads=[B("eb")], writes=[B(f"S{g}")], tag=pfx + f"sm{ci}" + sg)
                    s.op("vector", lambda e, g=g: e.tensor_tensor(
                        out=S_sb[g], in0=S_sb[g], in1=PST, op=ALU.add),
                        reads=[B("pst")], writes=[B(f"S{g}")], tag=pfx + f"sa{ci}" + sg)
                if full:
                    s.op("vector", lambda e, p=p, g=g: e.tensor_tensor(
                        out=lt_sb.rearrange("p (h q) -> p h q", h=16),
                        in0=PSO.rearrange("p (h q) -> p h q", h=16),
                        in1=eat_sb.rearrange("p (r h) -> p r h", r=NPAIR)
                            [:, p, g * 16:(g + 1) * 16, None].broadcast_to([128, 16, HD]),
                        op=ALU.mult),
                        reads=[B("pso"), B("eat")], writes=[B("lt")], tag=pfx + "ym" + sg)
                    s.op("vector", lambda e, p=p, g=g: e.tensor_tensor(
                        out=ytm_sb[:, p * DIN + g * 1024:p * DIN + (g + 1) * 1024],
                        in0=lt_sb, in1=PSY, op=ALU.add),
                        reads=[B("lt"), B("psy")], writes=[B("ytm")], tag=pfx + "ya" + sg)
                if _K._BMODE >= 3:
                    s.barrier()

        if not full:
            return
        if istap:
            tap("ytm", ytm_sb, [128, NPAIR * DIN], BF16, ["ytm"])

        s.op("gpsimd", lambda e, si=si: e.dma_start(
            out=yscr[si].rearrange("(r p) c -> p r c", p=128),
            in_=ytm_sb.rearrange("p (r c) -> p r c", r=NPAIR)),
            reads=[B("ytm")], writes=[B(f"yscr{si}")], dma=True, tag=pfx + "yw")
        for cc in range(16):
            s.op("sync", lambda e, cc=cc, si=si: e.dma_start(
                out=yfm_sb[:, cc * SEG:(cc + 1) * SEG],
                in_=yscr[si][:, cc * 128:(cc + 1) * 128], transpose=True),
                reads=[B(f"yscr{si}")], writes=[B("yfmr")], dma=True, tag=pfx + f"yf{cc}")
        for cc in range(16):
            s.op("vector", lambda e, cc=cc: e.scalar_tensor_tensor(
                out=yfm_sb[:, cc * SEG:(cc + 1) * SEG],
                in0=xhs_sb[:, cc * SEG:(cc + 1) * SEG],
                scalar=drep_sb[:, cc:cc + 1],
                in1=yfm_sb[:, cc * SEG:(cc + 1) * SEG],
                op0=ALU.mult, op1=ALU.add),
                reads=[B("yfmr"), B("xhs"), B("drep")], writes=[B("yfm")], tag=pfx + f"yD{cc}")
        if istap:
            tap("yfm", yfm_sb, [128, 16 * SEG], BF16, ["yfm"])

        for tt in range(NPAIR):
            for cc in range(16):
                for half in range(2):
                    s.op("tensor", lambda e, tt=tt, cc=cc, half=half: e.matmul(
                        PSY[:, half * 512:(half + 1) * 512],
                        yfm_sb[:, cc * SEG + tt * 128:cc * SEG + (tt + 1) * 128],
                        wout_sb.rearrange("p (c q) -> p c q", c=16)[:, cc, half * 512:(half + 1) * 512],
                        start=(cc == 0), stop=(cc == 15), skip_group_check=True),
                        reads=[B("yfm"), B("wout")], writes=[B("psy")], tag=pfx + f"op{tt}{cc}{half}")
            s.op("gpsimd", lambda e, tt=tt: e.dma_start(
                out=xtm_sb, in_=xscrd[t0 + tt * 128:t0 + (tt + 1) * 128, :]),
                reads=[B("xscr")], writes=[B("xtm")], dma=True, tag=pfx + f"xrl{tt}")
            s.op("vector", lambda e, tt=tt: e.tensor_tensor(
                out=pln1_sb, in0=PSY, in1=xtm_sb, op=ALU.add),
                reads=[B("psy"), B("xtm")], writes=[B("pln1")], tag=pfx + f"res{tt}")
            _ln_quant(pfx, tt, t0, istap)

    def _ln_quant(pfx, tt, t0, istap):
        s.op("vector", lambda e: e.tensor_reduce(
            out=st_sb[:, 0:1], in_=pln1_sb, axis=AX.X, op=ALU.add),
            reads=[B("pln1")], writes=[B("st0")], tag=pfx + f"mu{tt}")
        s.op("vector", lambda e: e.tensor_scalar_mul(out=st_sb[:, 0:1], in0=st_sb[:, 0:1],
                                                     scalar1=1.0 / DM),
             reads=[B("st0")], writes=[B("st0")], tag=pfx + f"mu2{tt}")
        s.op("scalar", lambda e: e.activation(
            out=xln1b_sb, in_=pln1_sb, func=AF.Square, accum_out=st_sb[:, 1:2]),
            reads=[B("pln1")], writes=[B("st1"), B("xlnb")], tag=pfx + f"sq{tt}")
        s.op("vector", lambda e: e.tensor_scalar_mul(out=st_sb[:, 1:2], in0=st_sb[:, 1:2],
                                                     scalar1=1.0 / DM),
             reads=[B("st1")], writes=[B("st1")], tag=pfx + f"ex2{tt}")
        s.op("vector", lambda e: e.tensor_tensor(
            out=st_sb[:, 2:3], in0=st_sb[:, 0:1], in1=st_sb[:, 0:1], op=ALU.mult),
            reads=[B("st0")], writes=[B("st2")], tag=pfx + f"m2{tt}")
        s.op("vector", lambda e: e.tensor_sub(
            out=st_sb[:, 2:3], in0=st_sb[:, 1:2], in1=st_sb[:, 2:3]),
            reads=[B("st1"), B("st2")], writes=[B("st2")], tag=pfx + f"var{tt}")
        s.op("scalar", lambda e: e.activation(
            out=st_sb[:, 3:4], in_=st_sb[:, 2:3], func=AF.Ln, bias=epsc_sb[:, 0:1]),
            reads=[B("st2"), B("epsc")], writes=[B("st3")], tag=pfx + f"lns{tt}")
        s.op("scalar", lambda e: e.activation(
            out=st_sb[:, 3:4], in_=st_sb[:, 3:4], func=AF.Exp, scale=-0.5),
            reads=[B("st3")], writes=[B("st3")], tag=pfx + f"rstd{tt}")
        s.op("vector", lambda e: e.tensor_tensor(
            out=st_sb[:, 4:5], in0=st_sb[:, 0:1], in1=st_sb[:, 3:4], op=ALU.mult),
            reads=[B("st0"), B("st3")], writes=[B("st4")], tag=pfx + f"mr{tt}")
        s.op("vector", lambda e: e.tensor_scalar_mul(out=st_sb[:, 4:5], in0=st_sb[:, 4:5],
                                                     scalar1=-1.0),
             reads=[B("st4")], writes=[B("st4")], tag=pfx + f"nmr{tt}")
        s.op("vector", lambda e: e.tensor_scalar(
            out=xln1_sb, in0=pln1_sb, scalar1=st_sb[:, 3:4], scalar2=st_sb[:, 4:5],
            op0=ALU.mult, op1=ALU.add),
            reads=[B("pln1"), B("st3"), B("st4")], writes=[B("xln1")], tag=pfx + f"lna{tt}")
        s.op("vector", lambda e: e.tensor_copy(out=xln1b_sb, in_=xln1_sb),
             reads=[B("xln1")], writes=[B("xlnb")], tag=pfx + f"lnb{tt}")
        s.op("gpsimd", lambda e: e.dma_start(
            out=xlnscr[t0 + tt * 128:t0 + (tt + 1) * 128, :], in_=xln1b_sb),
            reads=[B("xlnb")], writes=[B("xlnscr")], dma=True, tag=pfx + f"lnw{tt}")
        s.op("vector", lambda e: e.tensor_tensor(
            out=st_sb[:, 5:6], in0=st_sb[:, 3:4], in1=st_sb[:, 3:4], op=ALU.mult),
            reads=[B("st3")], writes=[B("st5")], tag=pfx + f"r2a{tt}")
        s.op("vector", lambda e: e.tensor_tensor(
            out=st_sb[:, 5:6], in0=st_sb[:, 5:6], in1=st_sb[:, 2:3], op=ALU.mult),
            reads=[B("st5"), B("st2")], writes=[B("st5")], tag=pfx + f"r2b{tt}")
        s.op("scalar", lambda e: e.activation(
            out=st_sb[:, 5:6], in_=st_sb[:, 5:6], func=AF.Ln, bias=epsc_sb[:, 0:1]),
            reads=[B("st5"), B("epsc")], writes=[B("st5")], tag=pfx + f"r2c{tt}")
        s.op("scalar", lambda e: e.activation(
            out=st_sb[:, 5:6], in_=st_sb[:, 5:6], func=AF.Exp, scale=-0.5),
            reads=[B("st5")], writes=[B("st5")], tag=pfx + f"r2d{tt}")
        s.op("vector", lambda e: e.reduce_max(
            out=st_sb[:, 6:7], in_=xln1_sb, axis=AX.X, apply_absolute_value=True),
            reads=[B("xln1")], writes=[B("st6")], tag=pfx + f"am{tt}")
        s.op("vector", lambda e: e.tensor_tensor(
            out=st_sb[:, 6:7], in0=st_sb[:, 6:7], in1=st_sb[:, 5:6], op=ALU.mult),
            reads=[B("st6"), B("st5")], writes=[B("st6")], tag=pfx + f"am2{tt}")
        s.op("vector", lambda e: e.tensor_scalar_max(
            out=st_sb[:, 6:7], in0=st_sb[:, 6:7], scalar1=1e-5),
            reads=[B("st6")], writes=[B("st6")], tag=pfx + f"am3{tt}")
        s.op("vector", lambda e: e.reciprocal(out=st_sb[:, 7:8], in_=st_sb[:, 6:7]),
             reads=[B("st6")], writes=[B("st7")], tag=pfx + f"rs{tt}")
        s.op("vector", lambda e: e.tensor_scalar_mul(
            out=st_sb[:, 7:8], in0=st_sb[:, 7:8], scalar1=127.0),
            reads=[B("st7")], writes=[B("st7")], tag=pfx + f"rs2{tt}")
        s.op("vector", lambda e: e.tensor_tensor(
            out=st_sb[:, 8:9], in0=st_sb[:, 7:8], in1=st_sb[:, 5:6], op=ALU.mult),
            reads=[B("st7"), B("st5")], writes=[B("st8")], tag=pfx + f"rs3{tt}")
        s.op("vector", lambda e: e.tensor_scalar_mul(
            out=st_sb[:, 9:10], in0=st_sb[:, 6:7], scalar1=1.0 / 127.0),
            reads=[B("st6")], writes=[B("st9")], tag=pfx + f"inv{tt}")
        s.op("scalar", lambda e: e.activation(
            out=wint_sb, in_=xln1_sb, func=AF.Copy, scale=st_sb[:, 8:9]),
            reads=[B("xln1"), B("st8")], writes=[B("wint")], tag=pfx + f"q1i{tt}")
        s.op("vector", lambda e: e.tensor_scalar_mul(
            out=q1o_sb, in0=wint_sb, scalar1=st_sb[:, 9:10]),
            reads=[B("wint"), B("st9")], writes=[B("wqb")], tag=pfx + f"q1o{tt}")
        s.op("gpsimd", lambda e: e.dma_start(
            out=xq1scr[t0 + tt * 128:t0 + (tt + 1) * 128, :], in_=q1o_sb),
            reads=[B("wqb")], writes=[B("xq1scr")], dma=True, tag=pfx + f"q1w{tt}")
        if istap and tt == 0:
            tap("st", st_sb, [128, 16], F32, ["st0", "st1", "st2", "st3", "st4", "st5", "st6", "st7", "st8", "st9"])
            tap("pln1", pln1_sb, [128, DM], F32, ["pln1"])
            tap("xln1", xln1_sb, [128, DM], F32, ["xln1"])
            tap("q1o", q1o_sb, [128, DM], BF16, ["wqb"])

    for seg in range(NSEG):
        attn_segment(seg, 1)
        if _K._BARRIER_SEGS:
            s.barrier()
    wq_prep(w1_d, NF, DM, wq1scr, "w1", "wq1scr")
    for g in range(2):
        s.op("vector", lambda e, g=g: e.tensor_scalar_mul(
            out=S_sb[g], in0=S_sb[g], scalar1=gate_sb[:, 0:1]),
            reads=[B(f"S{g}"), B("gate")], writes=[B(f"S{g}")], tag=f"gateS{g}")
    s.op("vector", lambda e: e.tensor_scalar_mul(
        out=halo_sb, in0=halo_sb, scalar1=gate_sb[:, 0:1]),
        reads=[B("halo"), B("gate")], writes=[B("halo")], tag="gateH")
    tap("Sg0", S_sb[0], [128, 1024], BF16, ["S0"])
    tap("acol0", acol_sb[0], [16, 1], F32, ["acol0"])
    tap("acol1", acol_sb[1], [16, 1], F32, ["acol1"])
    for seg in range(NSEG):
        attn_segment(seg, 2)
        if _K._BARRIER_SEGS:
            s.barrier()
    wq_prep(w2_d, DM, NF, wq2scr, "w2", "wq2scr")

    aes.close()
    s.barrier()
    P3 = lambda n, s_, d: TT_(es, n, s_, d)
    wq1t_sb = P3("wq1t_sb", [128, 8 * NF], BF16)
    b1bc_sb = P3("b1bc_sb", [128, NF], BF16)
    b2bc_sb = P3("b2bc_sb", [128, DM], BF16)
    bcast_row(b1row_sb, "b1row", NF, b1bc_sb, "b1bc")
    bcast_row(b2row_sb, "b2row", DM, b2bc_sb, "b2bc")
    wq2t_sb = P3("wq2t_sb", [128, 32 * DM], BF16)
    xq1f_sb = P3("xq1f_sb", [128, 8 * SEG], BF16)
    xlnt_sb = P3("xlnt_sb", [128, NPAIR * DM], BF16)
    hpre_sb = P3("hpre_sb", [128, NF], BF16)
    q2i_sb = P3("q2i_sb", [128, 1024], I32)
    q2o_sb = P3("q2o_sb", [128, 1024], BF16)
    xq2f_sb = P3("xq2f_sb", [128, 32 * SEG], BF16)
    pln2_sb = pln1_sb
    rb2_sb = xln1b_sb
    outr_sb = xln1_sb

    tap("wq1scr", wq1scr, [NF, DM], BF16, ["wq1scr"])
    for k in range(8):
        s.op("sync", lambda e, k=k: e.dma_start(
            out=wq1t_sb.rearrange("p (k c) -> p k c", k=8)[:, k, :],
            in_=wq1scr[:, k * 128:(k + 1) * 128], transpose=True),
            reads=[B("wq1scr")], writes=[B("wq1t")], dma=True, tag=f"wq1t{k}")
    for k in range(32):
        s.op("sync", lambda e, k=k: e.dma_start(
            out=wq2t_sb.rearrange("p (k c) -> p k c", k=32)[:, k, :],
            in_=wq2scr[:, k * 128:(k + 1) * 128], transpose=True),
            reads=[B("wq2scr")], writes=[B("wq2t")], dma=True, tag=f"wq2t{k}")

    def ffn_segment(seg):
        t0 = seg * SEG
        si = seg % 2
        pfx = f"p3s{seg}_"
        istap = seg == 0
        for k in range(8):
            s.op("sync", lambda e, k=k: e.dma_start(
                out=xq1f_sb[:, k * SEG:(k + 1) * SEG],
                in_=xq1scr[t0:t0 + SEG, k * 128:(k + 1) * 128], transpose=True),
                reads=[B("xq1scr")], writes=[B("xq1f")], dma=True, tag=pfx + f"q1f{k}")
        s.op("gpsimd", lambda e: e.dma_start(
            out=xlnt_sb.rearrange("p (r c) -> p r c", r=NPAIR),
            in_=xlnscr.rearrange("(r p) c -> p r c", p=128)[:, NPAIR * seg:NPAIR * (seg + 1), :]),
            reads=[B("xlnscr")], writes=[B("xlnt")], dma=True, tag=pfx + "xlnt")
        for tt in range(NPAIR):
            for fc in range(8):
                pid = fc % 2
                for k in range(8):
                    s.op("tensor", lambda e, tt=tt, fc=fc, k=k, pid=pid: e.matmul(
                        PG[pid][:, 0:512],
                        xq1f_sb[:, k * SEG + tt * 128:k * SEG + (tt + 1) * 128],
                        wq1t_sb.rearrange("p (k c) -> p k c", k=8)[:, k, fc * 512:(fc + 1) * 512],
                        start=(k == 0), stop=(k == 7)),
                        reads=[B("xq1f"), B("wq1t")], writes=[B(f"pg{pid}")],
                        tag=pfx + f"m1_{tt}{fc}{k}")
                s.op("vector", lambda e, fc=fc, pid=pid: e.scalar_tensor_tensor(
                    out=hpre_sb[:, fc * 512:(fc + 1) * 512],
                    in0=PG[pid][:, 0:512], scalar=sw_mean["w1"][:, 0:1],
                    in1=b1bc_sb[:, fc * 512:(fc + 1) * 512],
                    op0=ALU.mult, op1=ALU.add),
                    reads=[B(f"pg{pid}"), B("w1mean"), B("b1bc")], writes=[B("hpre")],
                    tag=pfx + f"e1_{tt}{fc}")
            if istap and tt == 0:
                tap("hpre", hpre_sb, [128, NF], BF16, ["hpre"])
                tap("xq1f", xq1f_sb, [128, 8 * SEG], BF16, ["xq1f"])
                tap("b1bc", b1bc_sb, [128, NF], BF16, ["b1bc"])
                tap("swm1", sw_mean["w1"], [128, 1], F32, ["w1mean"])
                tap("sws1", sw_scale["w1"], [128, 1], F32, ["w1sc"])
                tap("wq1t", wq1t_sb, [128, 8 * NF], BF16, ["wq1t"])
            s.op("scalar", lambda e: e.activation(out=hpre_sb, in_=hpre_sb, func=(AF.Relu if _K._NOSILU else AF.Gelu)),
                 reads=[B("hpre")], writes=[B("hpre")], tag=pfx + f"gel{tt}")
            s.op("scalar", lambda e: e.activation(
                out=xq2f_sb[:, 0:NF], in_=hpre_sb, func=AF.Square, accum_out=st_sb[:, 1:2]),
                reads=[B("hpre")], writes=[B("st1"), B("xq2f")], tag=pfx + f"sq{tt}")
            s.op("vector", lambda e: e.reduce_max(
                out=st_sb[:, 6:7], in_=hpre_sb, axis=AX.X, apply_absolute_value=True),
                reads=[B("hpre")], writes=[B("st6")], tag=pfx + f"am{tt}")
            s.op("vector", lambda e: e.tensor_scalar_mul(
                out=st_sb[:, 1:2], in0=st_sb[:, 1:2], scalar1=1.0 / NF),
                reads=[B("st1")], writes=[B("st1")], tag=pfx + f"ms{tt}")
            s.op("scalar", lambda e: e.activation(
                out=st_sb[:, 5:6], in_=st_sb[:, 1:2], func=AF.Ln, bias=epsc_sb[:, 0:1]),
                reads=[B("st1"), B("epsc")], writes=[B("st5")], tag=pfx + f"r2a{tt}")
            s.op("scalar", lambda e: e.activation(
                out=st_sb[:, 5:6], in_=st_sb[:, 5:6], func=AF.Exp, scale=-0.5),
                reads=[B("st5")], writes=[B("st5")], tag=pfx + f"r2b{tt}")
            s.op("vector", lambda e: e.tensor_tensor(
                out=st_sb[:, 6:7], in0=st_sb[:, 6:7], in1=st_sb[:, 5:6], op=ALU.mult),
                reads=[B("st6"), B("st5")], writes=[B("st6")], tag=pfx + f"am2{tt}")
            s.op("vector", lambda e: e.tensor_scalar_max(
                out=st_sb[:, 6:7], in0=st_sb[:, 6:7], scalar1=1e-5),
                reads=[B("st6")], writes=[B("st6")], tag=pfx + f"am3{tt}")
            s.op("vector", lambda e: e.reciprocal(out=st_sb[:, 7:8], in_=st_sb[:, 6:7]),
                 reads=[B("st6")], writes=[B("st7")], tag=pfx + f"rs{tt}")
            s.op("vector", lambda e: e.tensor_scalar_mul(
                out=st_sb[:, 7:8], in0=st_sb[:, 7:8], scalar1=127.0),
                reads=[B("st7")], writes=[B("st7")], tag=pfx + f"rs2{tt}")
            s.op("vector", lambda e: e.tensor_tensor(
                out=st_sb[:, 8:9], in0=st_sb[:, 7:8], in1=st_sb[:, 5:6], op=ALU.mult),
                reads=[B("st7"), B("st5")], writes=[B("st8")], tag=pfx + f"rs3{tt}")
            s.op("vector", lambda e: e.tensor_scalar_mul(
                out=st_sb[:, 9:10], in0=st_sb[:, 6:7], scalar1=1.0 / 127.0),
                reads=[B("st6")], writes=[B("st9")], tag=pfx + f"inv{tt}")
            for fh in range(4):
                s.op("scalar", lambda e, fh=fh: e.activation(
                    out=q2i_sb, in_=hpre_sb[:, fh * 1024:(fh + 1) * 1024],
                    func=AF.Copy, scale=st_sb[:, 8:9]),
                    reads=[B("hpre"), B("st8")], writes=[B("q2i")], tag=pfx + f"q2i{tt}{fh}")
                s.op("vector", lambda e, fh=fh: e.tensor_scalar_mul(
                    out=q2o_sb, in0=q2i_sb, scalar1=st_sb[:, 9:10]),
                    reads=[B("q2i"), B("st9")], writes=[B("q2o")], tag=pfx + f"q2o{tt}{fh}")
                s.op("gpsimd", lambda e, tt=tt, si=si, fh=fh: e.dma_start(
                    out=xq2scr[si][tt * 128:(tt + 1) * 128, fh * 1024:(fh + 1) * 1024],
                    in_=q2o_sb),
                    reads=[B("q2o")], writes=[B(f"xq2scr{si}")], dma=True, tag=pfx + f"q2w{tt}{fh}")
            if istap and tt == 0:
                tap("h", hpre_sb, [128, NF], BF16, ["hpre"])
                tap("q2o", q2o_sb, [128, 1024], BF16, ["q2o"])
        for fc in range(32):
            s.op("sync", lambda e, fc=fc, si=si: e.dma_start(
                out=xq2f_sb[:, fc * SEG:(fc + 1) * SEG],
                in_=xq2scr[si][:, fc * 128:(fc + 1) * 128], transpose=True),
                reads=[B(f"xq2scr{si}")], writes=[B("xq2f")], dma=True, tag=pfx + f"q2f{fc}")
        for tt in range(NPAIR):
            for half in range(2):
                for k in range(32):
                    s.op("tensor", lambda e, tt=tt, half=half, k=k: e.matmul(
                        PSY[:, half * 512:(half + 1) * 512],
                        xq2f_sb[:, k * SEG + tt * 128:k * SEG + (tt + 1) * 128],
                        wq2t_sb.rearrange("p (k c) -> p k c", k=32)[:, k, half * 512:(half + 1) * 512],
                        start=(k == 0), stop=(k == 31), skip_group_check=True),
                        reads=[B("xq2f"), B("wq2t")], writes=[B("psy")],
                        tag=pfx + f"m2_{tt}{half}{k}")
            s.op("vector", lambda e, tt=tt: e.tensor_tensor(
                out=rb2_sb, in0=xlnt_sb[:, tt * DM:(tt + 1) * DM], in1=b2bc_sb, op=ALU.add),
                reads=[B("xlnt"), B("b2bc")], writes=[B("rb2")], tag=pfx + f"rb{tt}")
            s.op("vector", lambda e: e.scalar_tensor_tensor(
                out=pln2_sb, in0=PSY, scalar=sw_mean["w2"][:, 0:1], in1=rb2_sb,
                op0=ALU.mult, op1=ALU.add),
                reads=[B("psy"), B("w2mean"), B("rb2")], writes=[B("pln2")], tag=pfx + f"e2{tt}")
            s.op("vector", lambda e: e.tensor_reduce(
                out=st_sb[:, 0:1], in_=pln2_sb, axis=AX.X, op=ALU.add),
                reads=[B("pln2")], writes=[B("st0")], tag=pfx + f"mu{tt}")
            s.op("vector", lambda e: e.tensor_scalar_mul(
                out=st_sb[:, 0:1], in0=st_sb[:, 0:1], scalar1=1.0 / DM),
                reads=[B("st0")], writes=[B("st0")], tag=pfx + f"mu2{tt}")
            s.op("scalar", lambda e: e.activation(
                out=rb2_sb, in_=pln2_sb, func=AF.Square, accum_out=st_sb[:, 1:2]),
                reads=[B("pln2")], writes=[B("st1"), B("rb2")], tag=pfx + f"sq2{tt}")
            s.op("vector", lambda e: e.tensor_scalar_mul(
                out=st_sb[:, 1:2], in0=st_sb[:, 1:2], scalar1=1.0 / DM),
                reads=[B("st1")], writes=[B("st1")], tag=pfx + f"ex2{tt}")
            s.op("vector", lambda e: e.tensor_tensor(
                out=st_sb[:, 2:3], in0=st_sb[:, 0:1], in1=st_sb[:, 0:1], op=ALU.mult),
                reads=[B("st0")], writes=[B("st2")], tag=pfx + f"m22{tt}")
            s.op("vector", lambda e: e.tensor_sub(
                out=st_sb[:, 2:3], in0=st_sb[:, 1:2], in1=st_sb[:, 2:3]),
                reads=[B("st1"), B("st2")], writes=[B("st2")], tag=pfx + f"var{tt}")
            s.op("scalar", lambda e: e.activation(
                out=st_sb[:, 3:4], in_=st_sb[:, 2:3], func=AF.Ln, bias=epsc_sb[:, 0:1]),
                reads=[B("st2"), B("epsc")], writes=[B("st3")], tag=pfx + f"ln2{tt}")
            s.op("scalar", lambda e: e.activation(
                out=st_sb[:, 3:4], in_=st_sb[:, 3:4], func=AF.Exp, scale=-0.5),
                reads=[B("st3")], writes=[B("st3")], tag=pfx + f"rstd{tt}")
            s.op("vector", lambda e: e.tensor_tensor(
                out=st_sb[:, 4:5], in0=st_sb[:, 0:1], in1=st_sb[:, 3:4], op=ALU.mult),
                reads=[B("st0"), B("st3")], writes=[B("st4")], tag=pfx + f"mr{tt}")
            s.op("vector", lambda e: e.tensor_scalar_mul(
                out=st_sb[:, 4:5], in0=st_sb[:, 4:5], scalar1=-1.0),
                reads=[B("st4")], writes=[B("st4")], tag=pfx + f"nmr{tt}")
            s.op("vector", lambda e: e.tensor_scalar(
                out=outr_sb, in0=pln2_sb, scalar1=st_sb[:, 3:4], scalar2=st_sb[:, 4:5],
                op0=ALU.mult, op1=ALU.add),
                reads=[B("pln2"), B("st3"), B("st4")], writes=[B("outr")], tag=pfx + f"lna{tt}")
            s.op("gpsimd", lambda e, tt=tt: e.dma_start(
                out=out_d[t0 + tt * 128:t0 + (tt + 1) * 128, :], in_=outr_sb),
                reads=[B("outr")], writes=[B("outd")], dma=True, tag=pfx + f"ow{tt}")

    for seg in range(NSEG):
        ffn_segment(seg)
        if _K._BARRIER_SEGS:
            s.barrier()
    s.op("gpsimd", lambda e: e.nop(), extra_waits=[B("outd").writer], tag="final")

    s.emit(nc, block, sems)
    es.close()
    return tap_d


def _host_inputs(inputs):
    x = np.asarray(inputs["L0_attn_x"])
    win = np.ascontiguousarray(np.asarray(inputs["L0_attn_in_proj_w"])[:, 2048:4384])
    convw = np.asarray(inputs["L0_attn_conv_w"])[:, 0, :]
    convw_r = np.zeros((128, 72), np.float32)
    for cc in range(CONVC):
        for k in range(4):
            convw_r[:, cc * 4 + k] = convw[k, cc * 128:(cc + 1) * 128]
    convb = np.asarray(inputs["L0_attn_conv_b"]).reshape(CONVC, 128).T.copy()
    dtb = np.asarray(inputs["L0_attn_dt_bias"])
    alog = np.asarray(inputs["L0_attn_A_log"])
    Dv = np.asarray(inputs["L0_attn_D"])
    drep = np.repeat(Dv, HD).reshape(16, 128).T.copy()
    wout = np.asarray(inputs["L0_attn_out_proj_w"])
    w1 = np.asarray(inputs["L0_ffn_w1"])
    b1 = np.asarray(inputs["L0_ffn_b1"]).reshape(1, NF)
    w2 = np.asarray(inputs["L0_ffn_w2"])
    b2 = np.asarray(inputs["L0_ffn_b2"]).reshape(1, DM)

    ident = np.eye(128, dtype=np.float32).astype(BF)
    tri64 = (np.arange(CH)[None, :] >= np.arange(CH)[:, None]).astype(np.float32)
    tri = np.concatenate([tri64, tri64], axis=0).astype(BF)
    ind2 = np.zeros((2, 128), np.float32)
    ind2[0, 0:64] = 1.0; ind2[1, 64:128] = 1.0
    ind2 = ind2.astype(BF)
    indn = np.zeros((16, 1024), np.float32)
    for h in range(16):
        indn[h, h * 64:(h + 1) * 64] = -1.0
    indn = indn.astype(BF)
    maska = np.ones((16, SEG), np.float32)
    maska[:, ::CH] = 0.0
    onesb = np.ones((1, 128), np.float32).astype(BF)

    common = dict(
        w_in=np.ascontiguousarray(win, np.float32), convw=convw_r,
        convb=np.ascontiguousarray(convb, np.float32),
        dtb0=dtb[0:16].reshape(16, 1).astype(np.float32),
        dtb1=dtb[16:32].reshape(16, 1).astype(np.float32),
        alog0=alog[0:16].reshape(16, 1).astype(np.float32),
        alog1=alog[16:32].reshape(16, 1).astype(np.float32),
        drep=np.ascontiguousarray(drep, np.float32),
        w_out=np.ascontiguousarray(wout, np.float32),
        w1=np.ascontiguousarray(w1, np.float32), b1r=np.ascontiguousarray(b1, np.float32),
        w2=np.ascontiguousarray(w2, np.float32), b2r=np.ascontiguousarray(b2, np.float32),
        identb=ident, tri=tri, ind2=ind2, indn16=indn,
        maska=maska, onesb=onesb, epsc=np.full((128, 1), EPS, np.float32), tinyc=np.full((16, 1), 1e-30, np.float32),
    )
    in_maps = []
    for c in range(8):
        b, r = c // 2, c % 2
        m = dict(common)
        m["xpre"] = np.ascontiguousarray(x[b, 0:TP], np.float32)
        m["xown"] = np.ascontiguousarray(x[b, r * TP:(r + 1) * TP], np.float32)
        m["gate"] = np.full((128, 1), float(r), np.float32)
        in_maps.append(m)
    return in_maps


def kernel(**inputs):
    global _LAST_RES
    in_maps = _host_inputs(inputs)
    nc = bass.Bass("TRN2", target_bir_lowering=False)
    build_program(nc, taps=_TAPS)
    res = run_bass_kernel_spmd(nc, in_maps, core_ids=list(range(8)))
    _LAST_RES = res
    out = np.zeros((4, 4096, DM), np.float32)
    for c in range(8):
        b, r = c // 2, c % 2
        out[b, r * TP:(r + 1) * TP] = res.results[c]["out"]
    return out


# version 02
# speedup vs baseline: 1.0007x; 1.0007x over previous
import contextlib
import numpy as np
import ml_dtypes
import concourse.bass as bass
import concourse.mybir as mybir
from concourse.bass_utils import run_bass_kernel_spmd

F32 = mybir.dt.float32
BF16 = mybir.dt.bfloat16
I32 = mybir.dt.int32
AF = mybir.ActivationFunctionType
ALU = mybir.AluOpType
AX = mybir.AxisListType
BF = ml_dtypes.bfloat16

DM = 1024
DIN = 2048
NH = 32
HD = 64
NF = 4096
CH = 64
TP = 2048
TO = 2048
SEG = 256
NPAIR = SEG // 128
NCHUNK = SEG // CH
NSEG = TP // SEG
CONVC = 18
EPS = 1e-5

ENGS = ("tensor", "vector", "scalar", "gpsimd", "sync")

_TAPS = ()
_LAST_RES = None
_BARRIER_SEGS = False
_BMODE = 0
_ABLATE = set()
_NOSILU = False
_DMASER = 0


class Op:
    __slots__ = ("eng", "fn", "waits", "marked", "idx", "inc_amt", "tag")
    def __init__(self, eng, fn, idx, inc_amt, tag):
        self.eng = eng; self.fn = fn; self.idx = idx
        self.waits = []; self.marked = False
        self.inc_amt = inc_amt; self.tag = tag


class Buf:
    __slots__ = ("writer", "readers", "name")
    def __init__(self, name=""):
        self.writer = None; self.readers = []; self.name = name


class Sched:
    def __init__(self):
        self.ops = []
        self.per_eng = {e: [] for e in ENGS}
        self.last_dma = {e: None for e in ENGS}
        self.pend_max = {e: 0 for e in ENGS}

    def op(self, eng, fn, reads=(), writes=(), extra_waits=(), dma=False, sz=None, tag=""):
        import kernel as _K
        if dma:
            full = _K._DMASER & (1 if eng == "gpsimd" else 2 if eng == "sync" else 0)
            cost = 0 if sz is None else sz
            if self.last_dma[eng] is not None and (full or sz is None
                                                   or cost < self.pend_max[eng]):
                extra_waits = list(extra_waits) + [self.last_dma[eng]]
                self.pend_max[eng] = cost
            else:
                self.pend_max[eng] = max(self.pend_max[eng], cost)
        o = Op(eng, fn, len(self.per_eng[eng]), 16 if dma else 1, tag)
        if dma:
            self.last_dma[eng] = o
        seen = set()
        def add_wait(w):
            if w is not None and id(w) not in seen and w is not o:
                seen.add(id(w)); o.waits.append(w)
        for b in reads:
            add_wait(b.writer)
        for b in writes:
            add_wait(b.writer)
            for r in b.readers:
                add_wait(r)
        for w in extra_waits:
            add_wait(w)
        for b in reads:
            b.readers.append(o)
        for b in writes:
            b.writer = o; b.readers = []
        self.ops.append(o)
        self.per_eng[eng].append(o)
        return o

    def barrier(self):
        lasts = [self.per_eng[e][-1] for e in ENGS if self.per_eng[e]]
        for e in ENGS:
            if self.per_eng[e]:
                self.op(e, lambda eng: eng.nop(), extra_waits=lasts, tag="barrier_" + e)

    def emit(self, nc, block, sems):
        for o in self.ops:
            if o.inc_amt == 16:
                o.marked = True
            for w in o.waits:
                w.marked = True
        val = {}
        for e in ENGS:
            c = 0
            for o in self.per_eng[e]:
                if o.marked:
                    c += o.inc_amt
                val[id(o)] = (e, c)
        def emit_engine(e, eng):
            highest = {x: 0 for x in ENGS}
            for o in self.per_eng[e]:
                needed = {}
                for w in o.waits:
                    we, wv = val[id(w)]
                    if we == e == "tensor" and w.inc_amt == 1 and o.inc_amt == 1:
                        continue
                    if wv > needed.get(we, 0):
                        needed[we] = wv
                for we, wv in sorted(needed.items()):
                    if wv > highest[we]:
                        eng.wait_ge(sems[we], wv)
                        highest[we] = wv
                instr = o.fn(eng)
                if o.marked:
                    if instr is None:
                        raise RuntimeError(f"marked op {o.tag} returned None")
                    instr.then_inc(sems[e], o.inc_amt)
        for e in ENGS:
            if self.per_eng[e]:
                getattr(block, e)(lambda eng, e=e: emit_engine(e, eng))


def build_program(nc, taps=()):
    import kernel as _K
    es = contextlib.ExitStack()
    tapset = set(taps)

    def din(name, shape, dtype=F32):
        return nc.dram_tensor(name, list(shape), dtype, kind="ExternalInput").ap()

    xpre_d = din("xpre", [TP, DM])
    xown_d = din("xown", [TO, DM])
    win_d = din("w_in", [DM, 2336])
    convw_d = din("convw", [128, 72])
    convb_d = din("convb", [128, CONVC])
    dtb_d = [din("dtb0", [16, 1]), din("dtb1", [16, 1])]
    alog_d = [din("alog0", [16, 1]), din("alog1", [16, 1])]
    drep_d = din("drep", [128, 16])
    wout_d = din("w_out", [DIN, DM])
    w1_d = din("w1", [NF, DM])
    b1_d = din("b1r", [1, NF])
    w2_d = din("w2", [DM, NF])
    b2_d = din("b2r", [1, DM])
    gate_d = din("gate", [128, 1])
    identb_d = din("identb", [128, 128], BF16)
    tri_d = din("tri", [128, CH], BF16)
    ind2_d = din("ind2", [2, 128], BF16)
    indn_d = din("indn16", [16, 1024], BF16)
    maska_d = din("maska", [16, SEG])
    onesb_d = din("onesb", [1, 128], BF16)
    epsc_d = din("epsc", [128, 1])
    tinyc_d = din("tinyc", [16, 1])

    out_d = nc.dram_tensor("out", [TO, DM], F32, kind="ExternalOutput").ap()
    tap_d = {}

    def scr(name, shape, dtype):
        return nc.dram_tensor(name, list(shape), dtype, kind="Internal").ap()

    xscr_d = [scr("xpscr", [TP, DM], BF16), scr("xoscr", [TO, DM], BF16)]
    xhscr = [scr(f"xhscr{i}", [DIN, SEG], BF16) for i in range(2)]
    yscr = [scr(f"yscr{i}", [SEG, DIN], BF16) for i in range(2)]
    xlnscr = scr("xlnscr", [TO, DM], BF16)
    xq1scr = scr("xq1scr", [TO, DM], BF16)
    wq1scr = scr("wq1scr", [NF, DM], BF16)
    wq2scr = scr("wq2scr", [DM, NF], BF16)
    xq2scr = [scr(f"xq2scr{i}", [SEG, NF], BF16) for i in range(2)]
    accscr = scr("accscr", [64, SEG], BF16)
    ecscr = scr("ecscr", [32, NCHUNK], BF16)

    def TT_(stack, name, shape, dtype):
        return stack.enter_context(nc.sbuf_tensor(name, list(shape), dtype))[:]
    T = lambda n, s, d: TT_(es, n, s, d)

    ident_sb = T("ident_sb", [128, 128], BF16)
    ones128_sb = T("ones128_sb", [128, 128], BF16)
    tri_sb = T("tri_sb", [128, CH], BF16)
    ind2_sb = T("ind2_sb", [2, 128], BF16)
    indn_sb = T("indn_sb", [16, 1024], BF16)
    maska_sb = T("maska_sb", [16, SEG], F32)
    onesb_sb = T("onesb_sb", [1, 128], BF16)
    convb_sb = T("convb_sb", [128, CONVC], F32)
    drep_sb = T("drep_sb", [128, 16], F32)
    gate_sb = T("gate_sb", [128, 1], F32)
    epsc_sb = T("epsc_sb", [128, 1], F32)
    tinyc_sb = T("tinyc_sb", [16, 1], F32)
    convw_sb = T("convw_sb", [128, 72], F32)
    acol_sb = [T(f"acol{g}", [16, 1], F32) for g in range(2)]
    dtb_sb = [T(f"dtbs{g}", [16, 1], F32) for g in range(2)]
    S_sb = [T(f"S{g}", [128, 1024], BF16) for g in range(2)]
    halo_sb = T("halo_sb", [128, CONVC * 3], BF16)
    sw_scale = {k: T(k + "_scale", [128, 1], F32) for k in ("w1", "w2")}
    sw_mean = {k: T(k + "_mean", [128, 1], F32) for k in ("w1", "w2")}
    sw_acc = {k: T(k + "_acc", [128, 32], F32) for k in ("w1", "w2")}
    sw_accb = T("sw_accb", [128, 32], BF16)
    b1row_sb = T("b1row_sb", [1, NF], BF16)
    b2row_sb = T("b2row_sb", [1, DM], BF16)
    st_sb = T("st_sb", [128, 16], F32)
    pln1_sb = T("pln1_sb", [128, DM], F32)
    xln1_sb = T("xln1_sb", [128, DM], F32)
    xln1b_sb = T("xln1b_sb", [128, DM], BF16)

    PSY = es.enter_context(nc.psum_tensor("psy", [128, 1024], F32))[:]
    PSO = es.enter_context(nc.psum_tensor("pso", [128, 1024], F32))[:]
    PST = es.enter_context(nc.psum_tensor("pst", [128, 1024], F32))[:]
    PG = [es.enter_context(nc.psum_tensor(f"pg{i}", [128, 512], F32))[:] for i in range(2)]
    PT = PSY.bitcast(BF16)

    sems = {e: es.enter_context(nc.semaphore("sem_" + e)) for e in ENGS}
    block = es.enter_context(nc.Block())

    s = Sched()
    bufs = {}
    def B(name):
        if name not in bufs:
            bufs[name] = Buf(name)
        return bufs[name]

    def tap(name, src_ap, shape, dtype, rbufs):
        if name in tapset:
            tap_d[name] = nc.dram_tensor("tap_" + name, list(shape), dtype,
                                         kind="ExternalOutput").ap()
            s.op("gpsimd", lambda e: e.dma_start(out=tap_d[name], in_=src_ap),
                 reads=[B(r) for r in rbufs], dma=True, tag="tap_" + name)

    aes = contextlib.ExitStack()
    A = lambda n, s_, d: TT_(aes, n, s_, d)
    win_sb = A("win_sb", [128, 8 * 2336], BF16)
    wout_sb = A("wout_sb", [128, 16 * DM], BF16)
    diagw_sb = A("diagw_sb", [128, 72 * 128], BF16)
    xtm_sb = A("xtm_sb", [128, DM], BF16)
    xfm_sb = A("xfm_sb", [128, 8 * SEG], BF16)
    xbc_sb = A("xbc_sb", [128, 2 * (SEG + 3)], BF16)
    xhs_sb = A("xhs_sb", [128, 16 * SEG], BF16)
    bfm_sb = A("bfm_sb", [128, SEG], BF16)
    cfm_sb = A("cfm_sb", [128, SEG], BF16)
    xT_sb = A("xT_sb", [128, NPAIR * DIN], BF16)
    xddt_sb = A("xddt_sb", [128, NPAIR * DIN], BF16)
    bt_sb = A("bt_sb", [128, NPAIR * 128], BF16)
    dtT_sb = [A(f"dtT{g}", [16, SEG], F32) for g in range(2)]
    accp_sb = [A(f"accp{g}", [16, SEG], F32) for g in range(2)]
    acchi_sb = [A(f"acchi{g}", [16, SEG], BF16) for g in range(2)]
    acclo_sb = [A(f"acclo{g}", [16, SEG], BF16) for g in range(2)]
    ea_sb = [A(f"ea{g}", [16, SEG], BF16) for g in range(2)]
    ddt_sb = [A(f"ddt{g}", [16, SEG], BF16) for g in range(2)]
    dtb16_sb = [A(f"dtb16_{g}", [16, SEG], BF16) for g in range(2)]
    sp1_sb = A("sp1_sb", [16, SEG], F32)
    sp2_sb = A("sp2_sb", [16, SEG], F32)
    kt_sb = [A(f"kt{g}", [2, 2 * NPAIR * 1024], BF16) for g in range(2)]
    eat_sb = A("eat_sb", [128, NPAIR * NH], BF16)
    dttm_sb = A("dttm_sb", [128, NPAIR * NH], BF16)
    ddttm_sb = A("ddttm_sb", [128, NPAIR * NH], BF16)
    eb_sb = A("eb_sb", [128, NCHUNK * NH], BF16)
    ecrow_sb = A("ecrow_sb", [1, NCHUNK * NH], BF16)
    ecc_sb = A("ecc_sb", [16, 2 * NCHUNK], BF16)
    lt_sb = A("lt_sb", [128, 1024], BF16)
    m2t_sb = A("m2t_sb", [128, 1024], BF16)
    bcm_sb = A("bcm_sb", [128, CH], BF16)
    ytm_sb = A("ytm_sb", [128, NPAIR * DIN], BF16)
    yfm_sb = A("yfm_sb", [128, 16 * SEG], BF16)
    wslab_sb = [A("wslab0", [128, DM], F32)] * 2
    WSLN = ["wsl0", "wsl0"]
    wqb_sb = A("wqb_sb", [128, DM], BF16)
    wint_sb = A("wint_sb", [128, DM], I32)
    q1o_sb = wqb_sb

    def ld(dst, src, bufname, q="sync"):
        return s.op(q, lambda e: e.dma_start(out=dst, in_=src),
                    writes=[B(bufname)], dma=True, tag="ld_" + bufname)

    ld(ident_sb, identb_d, "ident")
    ld(tri_sb, tri_d, "tri")
    ld(ind2_sb, ind2_d, "ind2")
    ld(indn_sb, indn_d, "indn")
    ld(maska_sb, maska_d, "maska")
    ld(onesb_sb, onesb_d, "onesb")
    ld(convb_sb, convb_d, "convb")
    ld(drep_sb, drep_d, "drep")
    ld(gate_sb, gate_d, "gate")
    ld(epsc_sb, epsc_d, "epsc")
    ld(tinyc_sb, tinyc_d, "tinyc")
    ld(convw_sb, convw_d, "convw")
    ld(b1row_sb, b1_d, "b1row", q="gpsimd")
    ld(b2row_sb, b2_d, "b2row", q="gpsimd")
    s.op("gpsimd", lambda e: e.dma_start(
        out=win_sb.rearrange("p (k c) -> p k c", k=8),
        in_=win_d.rearrange("(k p) c -> p k c", p=128)),
        writes=[B("win")], dma=True, sz=10 * 1024 * 1024, tag="ld_win")
    s.op("gpsimd", lambda e: e.dma_start(
        out=wout_sb.rearrange("p (k c) -> p k c", k=16),
        in_=wout_d.rearrange("(k p) c -> p k c", p=128)),
        writes=[B("wout")], dma=True, sz=8 * 1024 * 1024, tag="ld_wout")
    s.op("vector", lambda e: e.memset(ones128_sb, 1.0), writes=[B("ones128")], tag="ones128")
    for wf in ((AF.Exp,) if _K._NOSILU else (AF.Exp, AF.Silu, AF.Gelu, AF.Exp)):
        s.op("scalar", lambda e, wf=wf: e.activation(out=sp1_sb[0:16, 0:16],
                                                     in_=maska_sb[0:16, 0:16], func=wf),
             reads=[B("maska")], writes=[B("sp1")], tag=f"warm_{wf.name}")
    for j in range(72):
        s.op("vector", lambda e, j=j: e.tensor_scalar_mul(
            out=diagw_sb.rearrange("p (j c) -> p j c", j=72)[:, j, :],
            in0=ident_sb, scalar1=convw_sb[:, j:j + 1]),
            reads=[B("ident"), B("convw")], writes=[B("diagw")], tag=f"dw{j}")
    s.op("vector", lambda e: e.memset(halo_sb, 0.0), writes=[B("halo")], tag="halo0")
    for g in range(2):
        s.op("vector", lambda e, g=g: e.memset(S_sb[g], 0.0), writes=[B(f"S{g}")], tag="S0")
    for g in range(2):
        ld(acol_sb[g], alog_d[g], f"acol{g}")
        ld(dtb_sb[g], dtb_d[g], f"dtb{g}")

    def bcast_row(row_ap, rowbuf, n, dst, dstbuf):
        nb = (n + 511) // 512
        for i in range(nb):
            w = min(512, n - i * 512)
            s.op("tensor", lambda e, i=i, w=w: e.matmul(
                PG[0][:, 0:w], onesb_sb[0:1, :], row_ap[0:1, i * 512:i * 512 + w],
                start=True, stop=True),
                reads=[B("onesb"), B(rowbuf)], writes=[B("pg0")], tag="bcmm")
            s.op("vector", lambda e, i=i, w=w: e.tensor_copy(
                out=dst[:, i * 512:i * 512 + w], in_=PG[0][:, 0:w]),
                reads=[B("pg0")], writes=[B(dstbuf)], tag="bcev")

    def wq_prep(wd, rows, cols, scrd, key, scrbuf):
        tiles = [(it, jc) for it in range(rows // 128) for jc in range(cols // DM)]
        for idx, (it, jc) in enumerate(tiles):
            si = idx % 2
            sl = wslab_sb[si]
            s.op("sync" if si == 0 else "gpsimd", lambda e, it=it, jc=jc, sl=sl: e.dma_start(
                out=sl, in_=wd[it * 128:(it + 1) * 128, jc * DM:(jc + 1) * DM]),
                writes=[B(WSLN[si])], dma=True, sz=512 * 1024, tag=f"{key}l1_{idx}")
            s.op("scalar", lambda e, idx=idx, sl=sl: e.activation(
                out=sl, in_=sl, func=AF.Abs, accum_out=sw_acc[key][:, idx:idx + 1]),
                reads=[B(WSLN[si])], writes=[B(WSLN[si]), B(key + "acc")], tag=f"{key}a{idx}")
        ntot = len(tiles)
        s.op("vector", lambda e, ntot=ntot: e.tensor_copy(
            out=sw_accb[:, 0:ntot], in_=sw_acc[key][:, 0:ntot]),
            reads=[B(key + "acc")], writes=[B("accb")], tag=f"{key}cb")
        s.op("tensor", lambda e, ntot=ntot: e.matmul(
            PG[0][:, 0:ntot], ones128_sb, sw_accb[:, 0:ntot], start=True, stop=True),
            reads=[B("ones128"), B("accb")], writes=[B("pg0")], tag=f"{key}rm")
        s.op("vector", lambda e, ntot=ntot: e.tensor_reduce(
            out=sw_mean[key], in_=PG[0][:, 0:ntot], axis=AX.X, op=ALU.add),
            reads=[B("pg0")], writes=[B(key + "mean")], tag=f"{key}rd")
        N = float(rows * cols)
        s.op("vector", lambda e: e.tensor_scalar_mul(out=sw_mean[key], in0=sw_mean[key],
                                                     scalar1=1.0 / N),
             reads=[B(key + "mean")], writes=[B(key + "mean")], tag=f"{key}mn")
        s.op("vector", lambda e: e.tensor_scalar_max(out=sw_scale[key], in0=sw_mean[key],
                                                     scalar1=1e-5),
             reads=[B(key + "mean")], writes=[B(key + "sc")], tag=f"{key}mx")
        s.op("vector", lambda e: e.reciprocal(out=sw_scale[key], in_=sw_scale[key]),
             reads=[B(key + "sc")], writes=[B(key + "sc")], tag=f"{key}rc")
        for idx, (it, jc) in enumerate(tiles):
            si = idx % 2
            sl = wslab_sb[si]
            s.op("sync" if si == 0 else "gpsimd", lambda e, it=it, jc=jc, sl=sl: e.dma_start(
                out=sl, in_=wd[it * 128:(it + 1) * 128, jc * DM:(jc + 1) * DM]),
                writes=[B(WSLN[si])], dma=True, sz=512 * 1024, tag=f"{key}l2_{idx}")
            s.op("scalar", lambda e, sl=sl: e.activation(
                out=wint_sb, in_=sl, func=AF.Copy, scale=sw_scale[key][:, 0:1]),
                reads=[B(WSLN[si]), B(key + "sc")], writes=[B("wint")], tag=f"{key}r{idx}")
            s.op("vector", lambda e: e.tensor_scalar(
                out=wqb_sb, in0=wint_sb, scalar1=1.0, scalar2=-1.0,
                op0=ALU.min, op1=ALU.max),
                reads=[B("wint")], writes=[B("wqb")], tag=f"{key}c{idx}")
            s.op("gpsimd", lambda e, it=it, jc=jc: e.dma_start(
                out=scrd[it * 128:(it + 1) * 128, jc * DM:(jc + 1) * DM], in_=wqb_sb),
                reads=[B("wqb")], writes=[B(scrbuf)], dma=True, sz=256 * 1024, tag=f"{key}s{idx}")

    def attn_segment(seg, phase):
        full = (phase == 2)
        t0 = seg * SEG
        xd = xown_d if full else xpre_d
        xscrd = xscr_d[1] if full else xscr_d[0]
        si = seg % 2
        pfx = f"p{phase}s{seg}_"
        istap = full and seg == 0

        for pp in range(NPAIR):
            s.op("gpsimd", lambda e, pp=pp: e.dma_start(
                out=xtm_sb,
                in_=xd[t0 + pp * 128:t0 + (pp + 1) * 128, :]),
                writes=[B("xtm")], dma=True, sz=512 * 1024, tag=pfx + f"xtm{pp}")
            s.op("gpsimd", lambda e, pp=pp: e.dma_start(
                out=xscrd[t0 + pp * 128:t0 + (pp + 1) * 128, :], in_=xtm_sb),
                reads=[B("xtm")], writes=[B("xscr")], dma=True, sz=256 * 1024, tag=pfx + f"xscrw{pp}")
        for k in range(8):
            s.op("sync", lambda e, k=k: e.dma_start(
                out=xfm_sb[:, k * SEG:(k + 1) * SEG],
                in_=xscrd[t0:t0 + SEG, k * 128:(k + 1) * 128], transpose=True),
                reads=[B("xscr")], writes=[B("xfm")], dma=True, sz=256 * 1024, tag=pfx + f"xfm{k}")

        mtiles = list(range(16)) + [16] + ([17] if (full or seg == NSEG - 1) else [])
        xbcv = xbc_sb.rearrange("p (c q) -> p c q", c=2)
        halov = halo_sb.rearrange("p (c q) -> p c q", q=3)
        for cc in mtiles:
            halo_only = (cc == 17 and not full)
            pid = cc % 2
            for k in range(8):
                s.op("tensor", lambda e, cc=cc, k=k, pid=pid: e.matmul(
                    PG[pid][:, 0:SEG],
                    win_sb.rearrange("p (k c) -> p k c", k=8)[:, k, cc * 128:(cc + 1) * 128],
                    xfm_sb[:, k * SEG:(k + 1) * SEG],
                    start=(k == 0), stop=(k == 7)),
                    reads=[B("win"), B("xfm")], writes=[B(f"pg{pid}")], tag=pfx + f"ip{cc}_{k}")
            s.op("vector", lambda e, cc=cc, pid=pid: e.tensor_copy(
                out=xbcv[:, pid, 3:3 + SEG], in_=PG[pid][:, 0:SEG]),
                reads=[B(f"pg{pid}")], writes=[B(f"xbcr{pid}")], tag=pfx + f"ipe{cc}")
            if not halo_only:
                s.op("vector", lambda e, cc=cc, pid=pid: e.tensor_copy(
                    out=xbcv[:, pid, 0:3], in_=halov[:, cc, :]),
                    reads=[B("halo")], writes=[B(f"xbcr{pid}")], tag=pfx + f"hin{cc}")
            s.op("vector", lambda e, cc=cc, pid=pid: e.tensor_copy(
                out=halov[:, cc, :], in_=xbcv[:, pid, SEG:SEG + 3]),
                reads=[B(f"xbcr{pid}")], writes=[B("halo")], tag=pfx + f"hsv{cc}")
            if halo_only:
                continue
            for k in range(4):
                s.op("tensor", lambda e, cc=cc, k=k, pid=pid: e.matmul(
                    PG[pid][:, 0:SEG],
                    diagw_sb.rearrange("p (j c) -> p j c", j=72)[:, cc * 4 + k, :],
                    xbcv[:, pid, k:k + SEG],
                    start=(k == 0), stop=(k == 3)),
                    reads=[B("diagw"), B(f"xbcr{pid}")],
                    writes=[B(f"pg{pid}")], tag=pfx + f"cv{cc}_{k}")
            if cc < 16:
                dst = xhs_sb[:, cc * SEG:(cc + 1) * SEG]; dbuf = "xhs"
            elif cc == 16:
                dst = bfm_sb; dbuf = "bfm"
            else:
                dst = cfm_sb; dbuf = "cfm"
            s.op("scalar", lambda e, cc=cc, pid=pid, dst=dst: e.activation(
                out=dst, in_=PG[pid][:, 0:SEG], func=(AF.Relu if _K._NOSILU else AF.Silu), bias=convb_sb[:, cc:cc + 1]),
                reads=[B(f"pg{pid}"), B("convb")], writes=[B(dbuf)], tag=pfx + f"si{cc}")
        for g in range(2):
            for k in range(8):
                s.op("tensor", lambda e, g=g, k=k: e.matmul(
                    PG[0][0:16, 0:SEG],
                    win_sb.rearrange("p (k c) -> p k c", k=8)[:, k, 2304 + 16 * g:2304 + 16 * (g + 1)],
                    xfm_sb[:, k * SEG:(k + 1) * SEG],
                    start=(k == 0), stop=(k == 7)),
                    reads=[B("win"), B("xfm")], writes=[B("pg0")], tag=pfx + f"dt{g}_{k}")
            s.op("scalar", lambda e, g=g: e.activation(
                out=sp1_sb, in_=PG[0][0:16, 0:SEG], func=AF.Abs, bias=dtb_sb[g][:, 0:1]),
                reads=[B("pg0"), B(f"dtb{g}")], writes=[B("sp1")], tag=pfx + f"spa{g}")
            s.op("scalar", lambda e: e.activation(out=sp2_sb, in_=sp1_sb, func=AF.Exp, scale=-1.0),
                 reads=[B("sp1")], writes=[B("sp2")], tag=pfx + f"spe{g}")
            s.op("scalar", lambda e: e.activation(out=sp2_sb, in_=sp2_sb, func=AF.Ln, bias=1.0),
                 reads=[B("sp2")], writes=[B("sp2")], tag=pfx + f"spl{g}")
            s.op("scalar", lambda e, g=g: e.activation(
                out=sp1_sb, in_=PG[0][0:16, 0:SEG], func=AF.Relu, bias=dtb_sb[g][:, 0:1]),
                reads=[B("pg0"), B(f"dtb{g}")], writes=[B("sp1")], tag=pfx + f"spr{g}")
            s.op("vector", lambda e, g=g: e.tensor_add(out=dtT_sb[g], in0=sp1_sb, in1=sp2_sb),
                 reads=[B("sp1"), B("sp2")], writes=[B(f"dtT{g}")], tag=pfx + f"spadd{g}")
            s.op("scalar", lambda e, g=g: e.activation(out=sp1_sb, in_=dtT_sb[g], func=AF.Ln,
                                                        bias=tinyc_sb[:, 0:1]),
                reads=[B(f"dtT{g}"), B("tinyc")], writes=[B("sp1")], tag=pfx + f"adtln{g}")
            s.op("scalar", lambda e, g=g: e.activation(
                out=sp2_sb, in_=sp1_sb, func=AF.Exp, bias=acol_sb[g][:, 0:1]),
                reads=[B("sp1"), B(f"acol{g}")], writes=[B("sp2")], tag=pfx + f"adtex{g}")
            s.op("vector", lambda e, g=g: e.tensor_tensor_scan(
                out=accp_sb[g], data0=maska_sb, data1=sp2_sb,
                initial=0.0, op0=ALU.mult, op1=ALU.add),
                reads=[B("maska"), B("sp2")], writes=[B(f"accp{g}")], tag=pfx + f"scan{g}")
            s.op("vector", lambda e, g=g: e.tensor_scalar_mul(
                out=acchi_sb[g], in0=accp_sb[g], scalar1=-1.0),
                reads=[B(f"accp{g}")], writes=[B(f"acchi{g}")], tag=pfx + f"hi{g}")
            s.op("vector", lambda e, g=g: e.scalar_tensor_tensor(
                out=acclo_sb[g], in0=accp_sb[g], scalar=-1.0, in1=acchi_sb[g],
                op0=ALU.mult, op1=ALU.subtract),
                reads=[B(f"accp{g}"), B(f"acchi{g}")], writes=[B(f"acclo{g}")], tag=pfx + f"lo{g}")
            s.op("scalar", lambda e, g=g: e.activation(out=ea_sb[g], in_=accp_sb[g], func=AF.Exp,
                                                        scale=-1.0),
                 reads=[B(f"accp{g}")], writes=[B(f"ea{g}")], tag=pfx + f"ea{g}")
            s.op("vector", lambda e, g=g: e.tensor_tensor(
                out=sp1_sb.rearrange("h (c l) -> h c l", l=CH),
                in0=accp_sb[g].rearrange("h (c l) -> h c l", l=CH),
                in1=accp_sb[g].rearrange("h (c l) -> h c l", l=CH)[:, :, CH - 1:CH]
                    .broadcast_to([16, NCHUNK, CH]), op=ALU.subtract),
                reads=[B(f"accp{g}")], writes=[B("sp1")], tag=pfx + f"dda{g}")
            s.op("scalar", lambda e: e.activation(out=sp2_sb, in_=sp1_sb, func=AF.Exp),
                 reads=[B("sp1")], writes=[B("sp2")], tag=pfx + f"dde{g}")
            s.op("vector", lambda e, g=g: e.tensor_tensor(
                out=ddt_sb[g], in0=sp2_sb, in1=dtT_sb[g], op=ALU.mult),
                reads=[B("sp2"), B(f"dtT{g}")], writes=[B(f"ddt{g}")], tag=pfx + f"ddm{g}")
            if full:
                s.op("vector", lambda e, g=g: e.tensor_copy(out=dtb16_sb[g], in_=dtT_sb[g]),
                     reads=[B(f"dtT{g}")], writes=[B(f"dtb16{g}")], tag=pfx + f"d16{g}")

        if istap:
            tap("xfm", xfm_sb, [128, 8 * SEG], BF16, ["xfm"])
            tap("xhs", xhs_sb, [128, 16 * SEG], BF16, ["xhs"])
            tap("bfm", bfm_sb, [128, SEG], BF16, ["bfm"])
            tap("cfm", cfm_sb, [128, SEG], BF16, ["cfm"])
            tap("dtT0", dtT_sb[0], [16, SEG], F32, ["dtT0"])
            tap("accp0", accp_sb[0], [16, SEG], F32, ["accp0"])
            tap("ea0", ea_sb[0], [16, SEG], BF16, ["ea0"])
            tap("ddt0", ddt_sb[0], [16, SEG], BF16, ["ddt0"])
            tap("ddt1", ddt_sb[1], [16, SEG], BF16, ["ddt1"])
            tap("dtT1", dtT_sb[1], [16, SEG], F32, ["dtT1"])
            tap("accp1", accp_sb[1], [16, SEG], F32, ["accp1"])

        tp_jobs = [(ddt_sb, ddttm_sb, "ddt", "ddttm")]
        if full:
            tp_jobs += [(ea_sb, eat_sb, "ea", "eat"), (dtb16_sb, dttm_sb, "dtb16", "dttm")]
        for (srcs, dst, sname, dname) in tp_jobs:
            for p in range(NPAIR):
                for g in range(2):
                    s.op("tensor", lambda e, srcs=srcs, p=p, g=g: e.transpose(
                        PT[:, (p * NH + g * 16):(p * NH + g * 16) + 16],
                        srcs[g][0:16, p * 128:(p + 1) * 128], ident_sb[0:16, 0:16]),
                        reads=[B(f"{sname}{g}"), B("ident")], writes=[B("psy")],
                        tag=pfx + f"t{sname}{p}{g}")
            s.op("vector", lambda e, dst=dst: e.tensor_copy(out=dst, in_=PT[:, 0:NPAIR * NH]),
                 reads=[B("psy")], writes=[B(dname)], tag=pfx + f"te{sname}")
        for p in range(NPAIR):
            s.op("tensor", lambda e, p=p: e.transpose(
                PT[:, p * 128:(p + 1) * 128], bfm_sb[:, p * 128:(p + 1) * 128], ident_sb),
                reads=[B("bfm"), B("ident")], writes=[B("psy")], tag=pfx + f"tb{p}")
        s.op("vector", lambda e: e.tensor_copy(out=bt_sb, in_=PT[:, 0:NPAIR * 128]),
             reads=[B("psy")], writes=[B("bt")], tag=pfx + "tbe")

        for g in range(2):
            s.op("vector", lambda e, g=g: e.tensor_copy(
                out=ecc_sb[:, g * NCHUNK:(g + 1) * NCHUNK],
                in_=ea_sb[g].rearrange("h (c l) -> h c l", l=CH)[:, :, CH - 1]),
                reads=[B(f"ea{g}")], writes=[B("ecc")], tag=pfx + f"ecc{g}")
        s.op("gpsimd", lambda e: e.dma_start(
            out=ecscr.rearrange("(g h) c -> h g c", g=2),
            in_=ecc_sb.rearrange("h (g c) -> h g c", g=2)),
            reads=[B("ecc")], writes=[B("ecscr")], dma=True, sz=256, tag=pfx + "ecw")
        s.op("sync", lambda e: e.dma_start(
            out=ecrow_sb,
            in_=ecscr.rearrange("h c -> (h c)")[None, :]),
            reads=[B("ecscr")], writes=[B("ecrow")], dma=True, sz=256, tag=pfx + "ec")
        s.op("tensor", lambda e: e.matmul(
            PG[0][:, 0:NCHUNK * NH], onesb_sb[0:1, :], ecrow_sb[0:1, :],
            start=True, stop=True),
            reads=[B("ecrow"), B("onesb")], writes=[B("pg0")], tag=pfx + "ebm")
        s.op("vector", lambda e: e.tensor_copy(out=eb_sb, in_=PG[0][:, 0:NCHUNK * NH]),
             reads=[B("pg0")], writes=[B("eb")], tag=pfx + "ebe")

        s.op("gpsimd", lambda e, si=si: e.dma_start(
            out=xhscr[si].rearrange("(c p) t -> p c t", p=128),
            in_=xhs_sb.rearrange("p (c t) -> p c t", c=16)),
            reads=[B("xhs")], writes=[B(f"xhscr{si}")], dma=True, sz=1024 * 1024, tag=pfx + "xhw")
        for p in range(NPAIR):
            s.op("sync", lambda e, p=p, si=si: e.dma_start(
                out=xT_sb[:, p * DIN:(p + 1) * DIN],
                in_=xhscr[si][:, p * 128:(p + 1) * 128], transpose=True),
                reads=[B(f"xhscr{si}")], writes=[B("xT")], dma=True, sz=2048 * 1024, tag=pfx + f"xT{p}")
        for p in range(NPAIR):
            s.op("vector", lambda e, p=p: e.tensor_tensor(
                out=xddt_sb[:, p * DIN:(p + 1) * DIN].rearrange("p (h q) -> p h q", h=NH),
                in0=xT_sb[:, p * DIN:(p + 1) * DIN].rearrange("p (h q) -> p h q", h=NH),
                in1=ddttm_sb.rearrange("p (r h) -> p r h", r=NPAIR)[:, p, :, None]
                    .broadcast_to([128, NH, HD]),
                op=ALU.mult),
                reads=[B("xT"), B("ddttm")], writes=[B("xddt")], tag=pfx + f"xd{p}")
        if istap:
            tap("eat", eat_sb, [128, NPAIR * NH], BF16, ["eat"])
            tap("dttm", dttm_sb, [128, NPAIR * NH], BF16, ["dttm"])
            tap("ddttm", ddttm_sb, [128, NPAIR * NH], BF16, ["ddttm"])
            tap("xT", xT_sb, [128, NPAIR * DIN], BF16, ["xT"])
            tap("xddt", xddt_sb, [128, NPAIR * DIN], BF16, ["xddt"])
            tap("bt", bt_sb, [128, NPAIR * 128], BF16, ["bt"])
            tap("eb", eb_sb, [128, NCHUNK * NH], BF16, ["eb"])

        if full:
            for g in range(2):
                for hl, src, sname in ((0, acchi_sb[g], f"acchi{g}"), (1, acclo_sb[g], f"acclo{g}")):
                    s.op("gpsimd", lambda e, g=g, hl=hl, src=src: e.dma_start(
                        out=accscr[g * 32 + hl * 16:g * 32 + hl * 16 + 16, :], in_=src),
                        reads=[B(sname)], writes=[B("accscr")], dma=True, sz=16 * 1024, tag=pfx + f"ktw{g}{hl}")
                for hl in range(2):
                    for pp in range(NPAIR):
                        s.op("sync", lambda e, g=g, hl=hl, pp=pp: e.dma_start(
                            out=kt_sb[g].rearrange("r (x p h l) -> r x p h l", x=2, p=NPAIR, h=16)
                                [:, hl, pp, :, :],
                            in_=accscr[g * 32 + hl * 16:g * 32 + hl * 16 + 16, :]
                                .rearrange("h (p r l) -> r p h l", p=NPAIR, r=2)[:, pp, :, :]),
                            reads=[B("accscr")], writes=[B(f"kt{g}")], dma=True,
                            sz=8 * 1024, tag=pfx + f"kt{g}{hl}{pp}")

        ablate_pair = (f"pair{phase}" in _K._ABLATE)
        for p in range(NPAIR) if not ablate_pair else []:
            if full:
                for ci in range(2):
                    s.op("tensor", lambda e, p=p, ci=ci: e.matmul(
                        PST[ci * 64:(ci + 1) * 64, 0:CH],
                        bfm_sb[:, (p * 2 + ci) * CH:(p * 2 + ci + 1) * CH],
                        cfm_sb[:, (p * 2 + ci) * CH:(p * 2 + ci + 1) * CH],
                        start=True, stop=True, skip_group_check=True),
                        reads=[B("bfm"), B("cfm")], writes=[B("pst")], tag=pfx + f"cb{p}{ci}")
                s.op("vector", lambda e: e.tensor_tensor(
                    out=bcm_sb, in0=PST[:, 0:CH], in1=tri_sb, op=ALU.mult),
                    reads=[B("pst"), B("tri")], writes=[B("bcm")], tag=pfx + f"bcm{p}")
            for g in range(2):
                sg = f"{p}{g}"
                if full:
                    ktv = kt_sb[g].rearrange("r (x p h l) -> r x p (h l)", x=2, p=NPAIR, h=16)
                    for half in range(2):
                        cs = slice(half * 512, (half + 1) * 512)
                        s.op("tensor", lambda e, p=p, cs=cs, ktv=ktv: e.matmul(
                            PSY[:, cs], ind2_sb, ktv[:, 0, p, cs],
                            start=True, stop=False, skip_group_check=True),
                            reads=[B(f"kt{g}"), B("ind2")], writes=[B("psy")], tag=pfx + "d1" + sg)
                        s.op("tensor", lambda e, p=p, cs=cs, ktv=ktv: e.matmul(
                            PSY[:, cs], ind2_sb, ktv[:, 1, p, cs],
                            start=False, stop=False, skip_group_check=True),
                            reads=[B(f"kt{g}")], writes=[B("psy")], tag=pfx + "d2" + sg)
                        s.op("tensor", lambda e, p=p, g=g, cs=cs: e.matmul(
                            PSY[:, cs], acchi_sb[g][0:16, p * 128:(p + 1) * 128],
                            indn_sb[:, cs],
                            start=False, stop=False, skip_group_check=True),
                            reads=[B(f"acchi{g}"), B("indn")], writes=[B("psy")], tag=pfx + "d3" + sg)
                        s.op("tensor", lambda e, p=p, g=g, cs=cs: e.matmul(
                            PSY[:, cs], acclo_sb[g][0:16, p * 128:(p + 1) * 128],
                            indn_sb[:, cs],
                            start=False, stop=True, skip_group_check=True),
                            reads=[B(f"acclo{g}"), B("indn")], writes=[B("psy")], tag=pfx + "d4" + sg)
                    s.op("vector", lambda e: e.tensor_scalar_min(out=m2t_sb, in0=PSY, scalar1=0.0),
                         reads=[B("psy")], writes=[B("m2t")], tag=pfx + "dm" + sg)
                    s.op("scalar", lambda e: e.activation(out=lt_sb, in_=m2t_sb, func=AF.Exp),
                         reads=[B("m2t")], writes=[B("lt")], tag=pfx + "lx" + sg)
                    s.op("vector", lambda e, p=p, g=g: e.tensor_tensor(
                        out=lt_sb.rearrange("p (h l) -> p h l", h=16),
                        in0=lt_sb.rearrange("p (h l) -> p h l", h=16),
                        in1=dttm_sb.rearrange("p (r h) -> p r h", r=NPAIR)
                            [:, p, g * 16:(g + 1) * 16, None].broadcast_to([128, 16, CH]),
                        op=ALU.mult),
                        reads=[B("lt"), B("dttm")], writes=[B("lt")], tag=pfx + "ld" + sg)
                    s.op("vector", lambda e: e.tensor_tensor(
                        out=m2t_sb.rearrange("p (h l) -> p h l", h=16),
                        in0=lt_sb.rearrange("p (h l) -> p h l", h=16),
                        in1=bcm_sb[:, None, :].broadcast_to([128, 16, CH]),
                        op=ALU.mult),
                        reads=[B("lt"), B("bcm")], writes=[B("m2t")], tag=pfx + "m2" + sg)
                    if istap and p == 0 and g == 0:
                        tap("m2t", m2t_sb, [128, 1024], BF16, ["m2t"])
                    if istap and p == 0 and g == 1:
                        tap("lt1", lt_sb, [128, 1024], BF16, ["lt"])
                        tap("m2t1", m2t_sb, [128, 1024], BF16, ["m2t"])
                    for h in range(16):
                        hh = g * 16 + h
                        for ci in range(2):
                            s.op("tensor", lambda e, p=p, h=h, hh=hh, ci=ci: e.matmul(
                                PSY[ci * 64:(ci + 1) * 64, h * 64:(h + 1) * 64],
                                m2t_sb.rearrange("p (h l) -> p h l", h=16)[ci * 64:(ci + 1) * 64, h, :],
                                xT_sb[ci * 64:(ci + 1) * 64,
                                      p * DIN + hh * 64:p * DIN + (hh + 1) * 64],
                                start=True, stop=True, skip_group_check=True),
                                reads=[B("m2t"), B("xT")], writes=[B("psy")],
                                tag=pfx + f"yd{h}{ci}" + sg)
                for ci in range(2):
                    cloc = p * 2 + ci
                    if full:
                        for half in range(2):
                            s.op("tensor", lambda e, cloc=cloc, ci=ci, g=g, half=half: e.matmul(
                                PSO[ci * 64:(ci + 1) * 64, half * 512:(half + 1) * 512],
                                cfm_sb[:, cloc * CH:(cloc + 1) * CH],
                                S_sb[g][:, half * 512:(half + 1) * 512],
                                start=True, stop=True, skip_group_check=True),
                                reads=[B("cfm"), B(f"S{g}")], writes=[B("pso")],
                                tag=pfx + f"yo{ci}{half}" + sg)
                    for half in range(2):
                        s.op("tensor", lambda e, p=p, ci=ci, g=g, half=half: e.matmul(
                            PST[:, half * 512:(half + 1) * 512],
                            bt_sb[ci * 64:(ci + 1) * 64, p * 128:(p + 1) * 128],
                            xddt_sb[ci * 64:(ci + 1) * 64,
                                    p * DIN + g * 1024 + half * 512:
                                    p * DIN + g * 1024 + (half + 1) * 512],
                            start=True, stop=True, skip_group_check=True),
                            reads=[B("bt"), B("xddt")], writes=[B("pst")],
                            tag=pfx + f"st{ci}{half}" + sg)
                    s.op("vector", lambda e, cloc=cloc, g=g: e.tensor_tensor(
                        out=S_sb[g].rearrange("p (h q) -> p h q", h=16),
                        in0=S_sb[g].rearrange("p (h q) -> p h q", h=16),
                        in1=eb_sb.rearrange("p (g h c) -> p g h c", g=2, h=16)
                            [:, g, :, cloc, None].broadcast_to([128, 16, HD]),
                        op=ALU.mult),
                        reads=[B("eb")], writes=[B(f"S{g}")], tag=pfx + f"sm{ci}" + sg)
                    s.op("vector", lambda e, g=g: e.tensor_tensor(
                        out=S_sb[g], in0=S_sb[g], in1=PST, op=ALU.add),
                        reads=[B("pst")], writes=[B(f"S{g}")], tag=pfx + f"sa{ci}" + sg)
                if full:
                    s.op("vector", lambda e, p=p, g=g: e.tensor_tensor(
                        out=lt_sb.rearrange("p (h q) -> p h q", h=16),
                        in0=PSO.rearrange("p (h q) -> p h q", h=16),
                        in1=eat_sb.rearrange("p (r h) -> p r h", r=NPAIR)
                            [:, p, g * 16:(g + 1) * 16, None].broadcast_to([128, 16, HD]),
                        op=ALU.mult),
                        reads=[B("pso"), B("eat")], writes=[B("lt")], tag=pfx + "ym" + sg)
                    s.op("vector", lambda e, p=p, g=g: e.tensor_tensor(
                        out=ytm_sb[:, p * DIN + g * 1024:p * DIN + (g + 1) * 1024],
                        in0=lt_sb, in1=PSY, op=ALU.add),
                        reads=[B("lt"), B("psy")], writes=[B("ytm")], tag=pfx + "ya" + sg)
                if _K._BMODE >= 3:
                    s.barrier()

        if not full:
            return
        if istap:
            tap("ytm", ytm_sb, [128, NPAIR * DIN], BF16, ["ytm"])

        s.op("gpsimd", lambda e, si=si: e.dma_start(
            out=yscr[si].rearrange("(r p) c -> p r c", p=128),
            in_=ytm_sb.rearrange("p (r c) -> p r c", r=NPAIR)),
            reads=[B("ytm")], writes=[B(f"yscr{si}")], dma=True, sz=1024 * 1024, tag=pfx + "yw")
        for cc in range(16):
            s.op("sync", lambda e, cc=cc, si=si: e.dma_start(
                out=yfm_sb[:, cc * SEG:(cc + 1) * SEG],
                in_=yscr[si][:, cc * 128:(cc + 1) * 128], transpose=True),
                reads=[B(f"yscr{si}")], writes=[B("yfmr")], dma=True, sz=256 * 1024, tag=pfx + f"yf{cc}")
        for cc in range(16):
            s.op("vector", lambda e, cc=cc: e.scalar_tensor_tensor(
                out=yfm_sb[:, cc * SEG:(cc + 1) * SEG],
                in0=xhs_sb[:, cc * SEG:(cc + 1) * SEG],
                scalar=drep_sb[:, cc:cc + 1],
                in1=yfm_sb[:, cc * SEG:(cc + 1) * SEG],
                op0=ALU.mult, op1=ALU.add),
                reads=[B("yfmr"), B("xhs"), B("drep")], writes=[B("yfm")], tag=pfx + f"yD{cc}")
        if istap:
            tap("yfm", yfm_sb, [128, 16 * SEG], BF16, ["yfm"])

        for tt in range(NPAIR):
            for cc in range(16):
                for half in range(2):
                    s.op("tensor", lambda e, tt=tt, cc=cc, half=half: e.matmul(
                        PSY[:, half * 512:(half + 1) * 512],
                        yfm_sb[:, cc * SEG + tt * 128:cc * SEG + (tt + 1) * 128],
                        wout_sb.rearrange("p (c q) -> p c q", c=16)[:, cc, half * 512:(half + 1) * 512],
                        start=(cc == 0), stop=(cc == 15), skip_group_check=True),
                        reads=[B("yfm"), B("wout")], writes=[B("psy")], tag=pfx + f"op{tt}{cc}{half}")
            s.op("gpsimd", lambda e, tt=tt: e.dma_start(
                out=xtm_sb, in_=xscrd[t0 + tt * 128:t0 + (tt + 1) * 128, :]),
                reads=[B("xscr")], writes=[B("xtm")], dma=True, sz=256 * 1024, tag=pfx + f"xrl{tt}")
            s.op("vector", lambda e, tt=tt: e.tensor_tensor(
                out=pln1_sb, in0=PSY, in1=xtm_sb, op=ALU.add),
                reads=[B("psy"), B("xtm")], writes=[B("pln1")], tag=pfx + f"res{tt}")
            _ln_quant(pfx, tt, t0, istap)

    def _ln_quant(pfx, tt, t0, istap):
        s.op("vector", lambda e: e.tensor_reduce(
            out=st_sb[:, 0:1], in_=pln1_sb, axis=AX.X, op=ALU.add),
            reads=[B("pln1")], writes=[B("st0")], tag=pfx + f"mu{tt}")
        s.op("vector", lambda e: e.tensor_scalar_mul(out=st_sb[:, 0:1], in0=st_sb[:, 0:1],
                                                     scalar1=1.0 / DM),
             reads=[B("st0")], writes=[B("st0")], tag=pfx + f"mu2{tt}")
        s.op("scalar", lambda e: e.activation(
            out=xln1b_sb, in_=pln1_sb, func=AF.Square, accum_out=st_sb[:, 1:2]),
            reads=[B("pln1")], writes=[B("st1"), B("xlnb")], tag=pfx + f"sq{tt}")
        s.op("vector", lambda e: e.tensor_scalar_mul(out=st_sb[:, 1:2], in0=st_sb[:, 1:2],
                                                     scalar1=1.0 / DM),
             reads=[B("st1")], writes=[B("st1")], tag=pfx + f"ex2{tt}")
        s.op("vector", lambda e: e.tensor_tensor(
            out=st_sb[:, 2:3], in0=st_sb[:, 0:1], in1=st_sb[:, 0:1], op=ALU.mult),
            reads=[B("st0")], writes=[B("st2")], tag=pfx + f"m2{tt}")
        s.op("vector", lambda e: e.tensor_sub(
            out=st_sb[:, 2:3], in0=st_sb[:, 1:2], in1=st_sb[:, 2:3]),
            reads=[B("st1"), B("st2")], writes=[B("st2")], tag=pfx + f"var{tt}")
        s.op("scalar", lambda e: e.activation(
            out=st_sb[:, 3:4], in_=st_sb[:, 2:3], func=AF.Ln, bias=epsc_sb[:, 0:1]),
            reads=[B("st2"), B("epsc")], writes=[B("st3")], tag=pfx + f"lns{tt}")
        s.op("scalar", lambda e: e.activation(
            out=st_sb[:, 3:4], in_=st_sb[:, 3:4], func=AF.Exp, scale=-0.5),
            reads=[B("st3")], writes=[B("st3")], tag=pfx + f"rstd{tt}")
        s.op("vector", lambda e: e.tensor_tensor(
            out=st_sb[:, 4:5], in0=st_sb[:, 0:1], in1=st_sb[:, 3:4], op=ALU.mult),
            reads=[B("st0"), B("st3")], writes=[B("st4")], tag=pfx + f"mr{tt}")
        s.op("vector", lambda e: e.tensor_scalar_mul(out=st_sb[:, 4:5], in0=st_sb[:, 4:5],
                                                     scalar1=-1.0),
             reads=[B("st4")], writes=[B("st4")], tag=pfx + f"nmr{tt}")
        s.op("vector", lambda e: e.tensor_scalar(
            out=xln1_sb, in0=pln1_sb, scalar1=st_sb[:, 3:4], scalar2=st_sb[:, 4:5],
            op0=ALU.mult, op1=ALU.add),
            reads=[B("pln1"), B("st3"), B("st4")], writes=[B("xln1")], tag=pfx + f"lna{tt}")
        s.op("vector", lambda e: e.tensor_copy(out=xln1b_sb, in_=xln1_sb),
             reads=[B("xln1")], writes=[B("xlnb")], tag=pfx + f"lnb{tt}")
        s.op("gpsimd", lambda e: e.dma_start(
            out=xlnscr[t0 + tt * 128:t0 + (tt + 1) * 128, :], in_=xln1b_sb),
            reads=[B("xlnb")], writes=[B("xlnscr")], dma=True, sz=256 * 1024, tag=pfx + f"lnw{tt}")
        s.op("vector", lambda e: e.tensor_tensor(
            out=st_sb[:, 5:6], in0=st_sb[:, 3:4], in1=st_sb[:, 3:4], op=ALU.mult),
            reads=[B("st3")], writes=[B("st5")], tag=pfx + f"r2a{tt}")
        s.op("vector", lambda e: e.tensor_tensor(
            out=st_sb[:, 5:6], in0=st_sb[:, 5:6], in1=st_sb[:, 2:3], op=ALU.mult),
            reads=[B("st5"), B("st2")], writes=[B("st5")], tag=pfx + f"r2b{tt}")
        s.op("scalar", lambda e: e.activation(
            out=st_sb[:, 5:6], in_=st_sb[:, 5:6], func=AF.Ln, bias=epsc_sb[:, 0:1]),
            reads=[B("st5"), B("epsc")], writes=[B("st5")], tag=pfx + f"r2c{tt}")
        s.op("scalar", lambda e: e.activation(
            out=st_sb[:, 5:6], in_=st_sb[:, 5:6], func=AF.Exp, scale=-0.5),
            reads=[B("st5")], writes=[B("st5")], tag=pfx + f"r2d{tt}")
        s.op("vector", lambda e: e.reduce_max(
            out=st_sb[:, 6:7], in_=xln1_sb, axis=AX.X, apply_absolute_value=True),
            reads=[B("xln1")], writes=[B("st6")], tag=pfx + f"am{tt}")
        s.op("vector", lambda e: e.tensor_tensor(
            out=st_sb[:, 6:7], in0=st_sb[:, 6:7], in1=st_sb[:, 5:6], op=ALU.mult),
            reads=[B("st6"), B("st5")], writes=[B("st6")], tag=pfx + f"am2{tt}")
        s.op("vector", lambda e: e.tensor_scalar_max(
            out=st_sb[:, 6:7], in0=st_sb[:, 6:7], scalar1=1e-5),
            reads=[B("st6")], writes=[B("st6")], tag=pfx + f"am3{tt}")
        s.op("vector", lambda e: e.reciprocal(out=st_sb[:, 7:8], in_=st_sb[:, 6:7]),
             reads=[B("st6")], writes=[B("st7")], tag=pfx + f"rs{tt}")
        s.op("vector", lambda e: e.tensor_scalar_mul(
            out=st_sb[:, 7:8], in0=st_sb[:, 7:8], scalar1=127.0),
            reads=[B("st7")], writes=[B("st7")], tag=pfx + f"rs2{tt}")
        s.op("vector", lambda e: e.tensor_tensor(
            out=st_sb[:, 8:9], in0=st_sb[:, 7:8], in1=st_sb[:, 5:6], op=ALU.mult),
            reads=[B("st7"), B("st5")], writes=[B("st8")], tag=pfx + f"rs3{tt}")
        s.op("vector", lambda e: e.tensor_scalar_mul(
            out=st_sb[:, 9:10], in0=st_sb[:, 6:7], scalar1=1.0 / 127.0),
            reads=[B("st6")], writes=[B("st9")], tag=pfx + f"inv{tt}")
        s.op("scalar", lambda e: e.activation(
            out=wint_sb, in_=xln1_sb, func=AF.Copy, scale=st_sb[:, 8:9]),
            reads=[B("xln1"), B("st8")], writes=[B("wint")], tag=pfx + f"q1i{tt}")
        s.op("vector", lambda e: e.tensor_scalar_mul(
            out=q1o_sb, in0=wint_sb, scalar1=st_sb[:, 9:10]),
            reads=[B("wint"), B("st9")], writes=[B("wqb")], tag=pfx + f"q1o{tt}")
        s.op("gpsimd", lambda e: e.dma_start(
            out=xq1scr[t0 + tt * 128:t0 + (tt + 1) * 128, :], in_=q1o_sb),
            reads=[B("wqb")], writes=[B("xq1scr")], dma=True, sz=256 * 1024, tag=pfx + f"q1w{tt}")
        if istap and tt == 0:
            tap("st", st_sb, [128, 16], F32, ["st0", "st1", "st2", "st3", "st4", "st5", "st6", "st7", "st8", "st9"])
            tap("pln1", pln1_sb, [128, DM], F32, ["pln1"])
            tap("xln1", xln1_sb, [128, DM], F32, ["xln1"])
            tap("q1o", q1o_sb, [128, DM], BF16, ["wqb"])

    for seg in range(NSEG):
        attn_segment(seg, 1)
        if _K._BARRIER_SEGS:
            s.barrier()
    wq_prep(w1_d, NF, DM, wq1scr, "w1", "wq1scr")
    for g in range(2):
        s.op("vector", lambda e, g=g: e.tensor_scalar_mul(
            out=S_sb[g], in0=S_sb[g], scalar1=gate_sb[:, 0:1]),
            reads=[B(f"S{g}"), B("gate")], writes=[B(f"S{g}")], tag=f"gateS{g}")
    s.op("vector", lambda e: e.tensor_scalar_mul(
        out=halo_sb, in0=halo_sb, scalar1=gate_sb[:, 0:1]),
        reads=[B("halo"), B("gate")], writes=[B("halo")], tag="gateH")
    tap("Sg0", S_sb[0], [128, 1024], BF16, ["S0"])
    tap("acol0", acol_sb[0], [16, 1], F32, ["acol0"])
    tap("acol1", acol_sb[1], [16, 1], F32, ["acol1"])
    for seg in range(NSEG):
        attn_segment(seg, 2)
        if _K._BARRIER_SEGS:
            s.barrier()
    wq_prep(w2_d, DM, NF, wq2scr, "w2", "wq2scr")

    aes.close()
    s.barrier()
    P3 = lambda n, s_, d: TT_(es, n, s_, d)
    wq1t_sb = P3("wq1t_sb", [128, 8 * NF], BF16)
    b1bc_sb = P3("b1bc_sb", [128, NF], BF16)
    b2bc_sb = P3("b2bc_sb", [128, DM], BF16)
    bcast_row(b1row_sb, "b1row", NF, b1bc_sb, "b1bc")
    bcast_row(b2row_sb, "b2row", DM, b2bc_sb, "b2bc")
    wq2t_sb = P3("wq2t_sb", [128, 32 * DM], BF16)
    xq1f_sb = P3("xq1f_sb", [128, 8 * SEG], BF16)
    xlnt_sb = P3("xlnt_sb", [128, NPAIR * DM], BF16)
    hpre_sb = P3("hpre_sb", [128, NF], BF16)
    q2i_sb = P3("q2i_sb", [128, 1024], I32)
    q2o_sb = P3("q2o_sb", [128, 1024], BF16)
    xq2f_sb = P3("xq2f_sb", [128, 32 * SEG], BF16)
    pln2_sb = pln1_sb
    rb2_sb = xln1b_sb
    outr_sb = xln1_sb

    tap("wq1scr", wq1scr, [NF, DM], BF16, ["wq1scr"])
    for k in range(8):
        s.op("sync", lambda e, k=k: e.dma_start(
            out=wq1t_sb.rearrange("p (k c) -> p k c", k=8)[:, k, :],
            in_=wq1scr[:, k * 128:(k + 1) * 128], transpose=True),
            reads=[B("wq1scr")], writes=[B("wq1t")], dma=True, sz=4096 * 1024, tag=f"wq1t{k}")
    for k in range(32):
        s.op("sync", lambda e, k=k: e.dma_start(
            out=wq2t_sb.rearrange("p (k c) -> p k c", k=32)[:, k, :],
            in_=wq2scr[:, k * 128:(k + 1) * 128], transpose=True),
            reads=[B("wq2scr")], writes=[B("wq2t")], dma=True, sz=4096 * 1024, tag=f"wq2t{k}")

    def ffn_segment(seg):
        t0 = seg * SEG
        si = seg % 2
        pfx = f"p3s{seg}_"
        istap = seg == 0
        for k in range(8):
            s.op("sync", lambda e, k=k: e.dma_start(
                out=xq1f_sb[:, k * SEG:(k + 1) * SEG],
                in_=xq1scr[t0:t0 + SEG, k * 128:(k + 1) * 128], transpose=True),
                reads=[B("xq1scr")], writes=[B("xq1f")], dma=True, sz=256 * 1024, tag=pfx + f"q1f{k}")
        s.op("gpsimd", lambda e: e.dma_start(
            out=xlnt_sb.rearrange("p (r c) -> p r c", r=NPAIR),
            in_=xlnscr.rearrange("(r p) c -> p r c", p=128)[:, NPAIR * seg:NPAIR * (seg + 1), :]),
            reads=[B("xlnscr")], writes=[B("xlnt")], dma=True, sz=2048 * 1024, tag=pfx + "xlnt")
        for tt in range(NPAIR):
            for fc in range(8):
                pid = fc % 2
                for k in range(8):
                    s.op("tensor", lambda e, tt=tt, fc=fc, k=k, pid=pid: e.matmul(
                        PG[pid][:, 0:512],
                        xq1f_sb[:, k * SEG + tt * 128:k * SEG + (tt + 1) * 128],
                        wq1t_sb.rearrange("p (k c) -> p k c", k=8)[:, k, fc * 512:(fc + 1) * 512],
                        start=(k == 0), stop=(k == 7)),
                        reads=[B("xq1f"), B("wq1t")], writes=[B(f"pg{pid}")],
                        tag=pfx + f"m1_{tt}{fc}{k}")
                s.op("vector", lambda e, fc=fc, pid=pid: e.scalar_tensor_tensor(
                    out=hpre_sb[:, fc * 512:(fc + 1) * 512],
                    in0=PG[pid][:, 0:512], scalar=sw_mean["w1"][:, 0:1],
                    in1=b1bc_sb[:, fc * 512:(fc + 1) * 512],
                    op0=ALU.mult, op1=ALU.add),
                    reads=[B(f"pg{pid}"), B("w1mean"), B("b1bc")], writes=[B("hpre")],
                    tag=pfx + f"e1_{tt}{fc}")
            if istap and tt == 0:
                tap("hpre", hpre_sb, [128, NF], BF16, ["hpre"])
                tap("xq1f", xq1f_sb, [128, 8 * SEG], BF16, ["xq1f"])
                tap("b1bc", b1bc_sb, [128, NF], BF16, ["b1bc"])
                tap("swm1", sw_mean["w1"], [128, 1], F32, ["w1mean"])
                tap("sws1", sw_scale["w1"], [128, 1], F32, ["w1sc"])
                tap("wq1t", wq1t_sb, [128, 8 * NF], BF16, ["wq1t"])
            s.op("scalar", lambda e: e.activation(out=hpre_sb, in_=hpre_sb, func=(AF.Relu if _K._NOSILU else AF.Gelu)),
                 reads=[B("hpre")], writes=[B("hpre")], tag=pfx + f"gel{tt}")
            s.op("scalar", lambda e: e.activation(
                out=xq2f_sb[:, 0:NF], in_=hpre_sb, func=AF.Square, accum_out=st_sb[:, 1:2]),
                reads=[B("hpre")], writes=[B("st1"), B("xq2f")], tag=pfx + f"sq{tt}")
            s.op("vector", lambda e: e.reduce_max(
                out=st_sb[:, 6:7], in_=hpre_sb, axis=AX.X, apply_absolute_value=True),
                reads=[B("hpre")], writes=[B("st6")], tag=pfx + f"am{tt}")
            s.op("vector", lambda e: e.tensor_scalar_mul(
                out=st_sb[:, 1:2], in0=st_sb[:, 1:2], scalar1=1.0 / NF),
                reads=[B("st1")], writes=[B("st1")], tag=pfx + f"ms{tt}")
            s.op("scalar", lambda e: e.activation(
                out=st_sb[:, 5:6], in_=st_sb[:, 1:2], func=AF.Ln, bias=epsc_sb[:, 0:1]),
                reads=[B("st1"), B("epsc")], writes=[B("st5")], tag=pfx + f"r2a{tt}")
            s.op("scalar", lambda e: e.activation(
                out=st_sb[:, 5:6], in_=st_sb[:, 5:6], func=AF.Exp, scale=-0.5),
                reads=[B("st5")], writes=[B("st5")], tag=pfx + f"r2b{tt}")
            s.op("vector", lambda e: e.tensor_tensor(
                out=st_sb[:, 6:7], in0=st_sb[:, 6:7], in1=st_sb[:, 5:6], op=ALU.mult),
                reads=[B("st6"), B("st5")], writes=[B("st6")], tag=pfx + f"am2{tt}")
            s.op("vector", lambda e: e.tensor_scalar_max(
                out=st_sb[:, 6:7], in0=st_sb[:, 6:7], scalar1=1e-5),
                reads=[B("st6")], writes=[B("st6")], tag=pfx + f"am3{tt}")
            s.op("vector", lambda e: e.reciprocal(out=st_sb[:, 7:8], in_=st_sb[:, 6:7]),
                 reads=[B("st6")], writes=[B("st7")], tag=pfx + f"rs{tt}")
            s.op("vector", lambda e: e.tensor_scalar_mul(
                out=st_sb[:, 7:8], in0=st_sb[:, 7:8], scalar1=127.0),
                reads=[B("st7")], writes=[B("st7")], tag=pfx + f"rs2{tt}")
            s.op("vector", lambda e: e.tensor_tensor(
                out=st_sb[:, 8:9], in0=st_sb[:, 7:8], in1=st_sb[:, 5:6], op=ALU.mult),
                reads=[B("st7"), B("st5")], writes=[B("st8")], tag=pfx + f"rs3{tt}")
            s.op("vector", lambda e: e.tensor_scalar_mul(
                out=st_sb[:, 9:10], in0=st_sb[:, 6:7], scalar1=1.0 / 127.0),
                reads=[B("st6")], writes=[B("st9")], tag=pfx + f"inv{tt}")
            for fh in range(4):
                s.op("scalar", lambda e, fh=fh: e.activation(
                    out=q2i_sb, in_=hpre_sb[:, fh * 1024:(fh + 1) * 1024],
                    func=AF.Copy, scale=st_sb[:, 8:9]),
                    reads=[B("hpre"), B("st8")], writes=[B("q2i")], tag=pfx + f"q2i{tt}{fh}")
                s.op("vector", lambda e, fh=fh: e.tensor_scalar_mul(
                    out=q2o_sb, in0=q2i_sb, scalar1=st_sb[:, 9:10]),
                    reads=[B("q2i"), B("st9")], writes=[B("q2o")], tag=pfx + f"q2o{tt}{fh}")
                s.op("gpsimd", lambda e, tt=tt, si=si, fh=fh: e.dma_start(
                    out=xq2scr[si][tt * 128:(tt + 1) * 128, fh * 1024:(fh + 1) * 1024],
                    in_=q2o_sb),
                    reads=[B("q2o")], writes=[B(f"xq2scr{si}")], dma=True, sz=256 * 1024, tag=pfx + f"q2w{tt}{fh}")
            if istap and tt == 0:
                tap("h", hpre_sb, [128, NF], BF16, ["hpre"])
                tap("q2o", q2o_sb, [128, 1024], BF16, ["q2o"])
        for fc in range(32):
            s.op("sync", lambda e, fc=fc, si=si: e.dma_start(
                out=xq2f_sb[:, fc * SEG:(fc + 1) * SEG],
                in_=xq2scr[si][:, fc * 128:(fc + 1) * 128], transpose=True),
                reads=[B(f"xq2scr{si}")], writes=[B("xq2f")], dma=True, sz=256 * 1024, tag=pfx + f"q2f{fc}")
        for tt in range(NPAIR):
            for half in range(2):
                for k in range(32):
                    s.op("tensor", lambda e, tt=tt, half=half, k=k: e.matmul(
                        PSY[:, half * 512:(half + 1) * 512],
                        xq2f_sb[:, k * SEG + tt * 128:k * SEG + (tt + 1) * 128],
                        wq2t_sb.rearrange("p (k c) -> p k c", k=32)[:, k, half * 512:(half + 1) * 512],
                        start=(k == 0), stop=(k == 31), skip_group_check=True),
                        reads=[B("xq2f"), B("wq2t")], writes=[B("psy")],
                        tag=pfx + f"m2_{tt}{half}{k}")
            s.op("vector", lambda e, tt=tt: e.tensor_tensor(
                out=rb2_sb, in0=xlnt_sb[:, tt * DM:(tt + 1) * DM], in1=b2bc_sb, op=ALU.add),
                reads=[B("xlnt"), B("b2bc")], writes=[B("rb2")], tag=pfx + f"rb{tt}")
            s.op("vector", lambda e: e.scalar_tensor_tensor(
                out=pln2_sb, in0=PSY, scalar=sw_mean["w2"][:, 0:1], in1=rb2_sb,
                op0=ALU.mult, op1=ALU.add),
                reads=[B("psy"), B("w2mean"), B("rb2")], writes=[B("pln2")], tag=pfx + f"e2{tt}")
            s.op("vector", lambda e: e.tensor_reduce(
                out=st_sb[:, 0:1], in_=pln2_sb, axis=AX.X, op=ALU.add),
                reads=[B("pln2")], writes=[B("st0")], tag=pfx + f"mu{tt}")
            s.op("vector", lambda e: e.tensor_scalar_mul(
                out=st_sb[:, 0:1], in0=st_sb[:, 0:1], scalar1=1.0 / DM),
                reads=[B("st0")], writes=[B("st0")], tag=pfx + f"mu2{tt}")
            s.op("scalar", lambda e: e.activation(
                out=rb2_sb, in_=pln2_sb, func=AF.Square, accum_out=st_sb[:, 1:2]),
                reads=[B("pln2")], writes=[B("st1"), B("rb2")], tag=pfx + f"sq2{tt}")
            s.op("vector", lambda e: e.tensor_scalar_mul(
                out=st_sb[:, 1:2], in0=st_sb[:, 1:2], scalar1=1.0 / DM),
                reads=[B("st1")], writes=[B("st1")], tag=pfx + f"ex2{tt}")
            s.op("vector", lambda e: e.tensor_tensor(
                out=st_sb[:, 2:3], in0=st_sb[:, 0:1], in1=st_sb[:, 0:1], op=ALU.mult),
                reads=[B("st0")], writes=[B("st2")], tag=pfx + f"m22{tt}")
            s.op("vector", lambda e: e.tensor_sub(
                out=st_sb[:, 2:3], in0=st_sb[:, 1:2], in1=st_sb[:, 2:3]),
                reads=[B("st1"), B("st2")], writes=[B("st2")], tag=pfx + f"var{tt}")
            s.op("scalar", lambda e: e.activation(
                out=st_sb[:, 3:4], in_=st_sb[:, 2:3], func=AF.Ln, bias=epsc_sb[:, 0:1]),
                reads=[B("st2"), B("epsc")], writes=[B("st3")], tag=pfx + f"ln2{tt}")
            s.op("scalar", lambda e: e.activation(
                out=st_sb[:, 3:4], in_=st_sb[:, 3:4], func=AF.Exp, scale=-0.5),
                reads=[B("st3")], writes=[B("st3")], tag=pfx + f"rstd{tt}")
            s.op("vector", lambda e: e.tensor_tensor(
                out=st_sb[:, 4:5], in0=st_sb[:, 0:1], in1=st_sb[:, 3:4], op=ALU.mult),
                reads=[B("st0"), B("st3")], writes=[B("st4")], tag=pfx + f"mr{tt}")
            s.op("vector", lambda e: e.tensor_scalar_mul(
                out=st_sb[:, 4:5], in0=st_sb[:, 4:5], scalar1=-1.0),
                reads=[B("st4")], writes=[B("st4")], tag=pfx + f"nmr{tt}")
            s.op("vector", lambda e: e.tensor_scalar(
                out=outr_sb, in0=pln2_sb, scalar1=st_sb[:, 3:4], scalar2=st_sb[:, 4:5],
                op0=ALU.mult, op1=ALU.add),
                reads=[B("pln2"), B("st3"), B("st4")], writes=[B("outr")], tag=pfx + f"lna{tt}")
            s.op("gpsimd", lambda e, tt=tt: e.dma_start(
                out=out_d[t0 + tt * 128:t0 + (tt + 1) * 128, :], in_=outr_sb),
                reads=[B("outr")], writes=[B("outd")], dma=True, sz=512 * 1024, tag=pfx + f"ow{tt}")

    for seg in range(NSEG):
        ffn_segment(seg)
        if _K._BARRIER_SEGS:
            s.barrier()
    s.op("gpsimd", lambda e: e.nop(), extra_waits=[B("outd").writer], tag="final")

    s.emit(nc, block, sems)
    es.close()
    return tap_d


def _host_inputs(inputs):
    x = np.asarray(inputs["L0_attn_x"])
    win = np.ascontiguousarray(np.asarray(inputs["L0_attn_in_proj_w"])[:, 2048:4384])
    convw = np.asarray(inputs["L0_attn_conv_w"])[:, 0, :]
    convw_r = np.zeros((128, 72), np.float32)
    for cc in range(CONVC):
        for k in range(4):
            convw_r[:, cc * 4 + k] = convw[k, cc * 128:(cc + 1) * 128]
    convb = np.asarray(inputs["L0_attn_conv_b"]).reshape(CONVC, 128).T.copy()
    dtb = np.asarray(inputs["L0_attn_dt_bias"])
    alog = np.asarray(inputs["L0_attn_A_log"])
    Dv = np.asarray(inputs["L0_attn_D"])
    drep = np.repeat(Dv, HD).reshape(16, 128).T.copy()
    wout = np.asarray(inputs["L0_attn_out_proj_w"])
    w1 = np.asarray(inputs["L0_ffn_w1"])
    b1 = np.asarray(inputs["L0_ffn_b1"]).reshape(1, NF)
    w2 = np.asarray(inputs["L0_ffn_w2"])
    b2 = np.asarray(inputs["L0_ffn_b2"]).reshape(1, DM)

    ident = np.eye(128, dtype=np.float32).astype(BF)
    tri64 = (np.arange(CH)[None, :] >= np.arange(CH)[:, None]).astype(np.float32)
    tri = np.concatenate([tri64, tri64], axis=0).astype(BF)
    ind2 = np.zeros((2, 128), np.float32)
    ind2[0, 0:64] = 1.0; ind2[1, 64:128] = 1.0
    ind2 = ind2.astype(BF)
    indn = np.zeros((16, 1024), np.float32)
    for h in range(16):
        indn[h, h * 64:(h + 1) * 64] = -1.0
    indn = indn.astype(BF)
    maska = np.ones((16, SEG), np.float32)
    maska[:, ::CH] = 0.0
    onesb = np.ones((1, 128), np.float32).astype(BF)

    common = dict(
        w_in=np.ascontiguousarray(win, np.float32), convw=convw_r,
        convb=np.ascontiguousarray(convb, np.float32),
        dtb0=dtb[0:16].reshape(16, 1).astype(np.float32),
        dtb1=dtb[16:32].reshape(16, 1).astype(np.float32),
        alog0=alog[0:16].reshape(16, 1).astype(np.float32),
        alog1=alog[16:32].reshape(16, 1).astype(np.float32),
        drep=np.ascontiguousarray(drep, np.float32),
        w_out=np.ascontiguousarray(wout, np.float32),
        w1=np.ascontiguousarray(w1, np.float32), b1r=np.ascontiguousarray(b1, np.float32),
        w2=np.ascontiguousarray(w2, np.float32), b2r=np.ascontiguousarray(b2, np.float32),
        identb=ident, tri=tri, ind2=ind2, indn16=indn,
        maska=maska, onesb=onesb, epsc=np.full((128, 1), EPS, np.float32), tinyc=np.full((16, 1), 1e-30, np.float32),
    )
    in_maps = []
    for c in range(8):
        b, r = c // 2, c % 2
        m = dict(common)
        m["xpre"] = np.ascontiguousarray(x[b, 0:TP], np.float32)
        m["xown"] = np.ascontiguousarray(x[b, r * TP:(r + 1) * TP], np.float32)
        m["gate"] = np.full((128, 1), float(r), np.float32)
        in_maps.append(m)
    return in_maps


def kernel(**inputs):
    global _LAST_RES
    in_maps = _host_inputs(inputs)
    nc = bass.Bass("TRN2", target_bir_lowering=False)
    build_program(nc, taps=_TAPS)
    res = run_bass_kernel_spmd(nc, in_maps, core_ids=list(range(8)))
    _LAST_RES = res
    out = np.zeros((4, 4096, DM), np.float32)
    for c in range(8):
        b, r = c // 2, c % 2
        out[b, r * TP:(r + 1) * TP] = res.results[c]["out"]
    return out


# version 03
# speedup vs baseline: 1.1115x; 1.1107x over previous
import contextlib
import numpy as np
import ml_dtypes
import concourse.bass as bass
import concourse.mybir as mybir
from concourse.bass_utils import run_bass_kernel_spmd

F32 = mybir.dt.float32
BF16 = mybir.dt.bfloat16
I32 = mybir.dt.int32
AF = mybir.ActivationFunctionType
ALU = mybir.AluOpType
AX = mybir.AxisListType
BF = ml_dtypes.bfloat16

DM = 1024
DIN = 2048
NH = 32
HD = 64
NF = 4096
CH = 64
TP = 2048
TO = 2048
SEG = 256
NPAIR = SEG // 128
NCHUNK = SEG // CH
NSEG = TP // SEG
CONVC = 18
EPS = 1e-5

ENGS = ("tensor", "vector", "scalar", "gpsimd", "sync")

_TAPS = ()
_LAST_RES = None
_BARRIER_SEGS = False
_BMODE = 0
_ABLATE = set()
_NOSILU = False
_DMASER = 0


class Op:
    __slots__ = ("eng", "fn", "waits", "waw_waits", "hard_waits", "marked",
                 "idx", "inc_amt", "skey", "tag")
    def __init__(self, eng, fn, idx, inc_amt, tag):
        self.eng = eng; self.fn = fn; self.idx = idx
        self.waits = []; self.waw_waits = []; self.hard_waits = []
        self.marked = False; self.skey = eng
        self.inc_amt = inc_amt; self.tag = tag


class Buf:
    __slots__ = ("writer", "readers", "name")
    def __init__(self, name=""):
        self.writer = None; self.readers = []; self.name = name


class Sched:
    def __init__(self):
        self.ops = []
        self.per_eng = {e: [] for e in ENGS}
        self.last_dma = {}

    def op(self, eng, fn, reads=(), writes=(), extra_waits=(), dma=False, sz=None,
           skey=None, chain=False, tag=""):
        import kernel as _K
        o = Op(eng, fn, len(self.per_eng[eng]), 16 if dma else 1, tag)
        if dma:
            o.skey = skey or eng
            full = _K._DMASER & (1 if eng == "gpsimd" else 2 if eng == "sync" else 0)
            prev = self.last_dma.get(o.skey)
            if (chain or full) and prev is not None:
                o.hard_waits.append(prev)
                prev.marked = True
            self.last_dma[o.skey] = o
        seen = set()
        def add_wait(w, lst=None):
            if w is not None and id(w) not in seen and w is not o:
                seen.add(id(w)); (o.waits if lst is None else lst).append(w)
        for b in reads:
            add_wait(b.writer)
        for b in writes:
            add_wait(b.writer, o.waw_waits)
            for r in b.readers:
                add_wait(r)
        for w in extra_waits:
            add_wait(w)
        for b in reads:
            b.readers.append(o)
        for b in writes:
            b.writer = o; b.readers = []
        self.ops.append(o)
        self.per_eng[eng].append(o)
        return o

    def barrier(self):
        lasts = [self.per_eng[e][-1] for e in ENGS if self.per_eng[e]]
        for e in ENGS:
            if self.per_eng[e]:
                self.op(e, lambda eng: eng.nop(), extra_waits=lasts, tag="barrier_" + e)

    def emit(self, nc, block, sems):
        for o in self.ops:
            if o.inc_amt == 16:
                o.marked = True
            for w in o.waits:
                w.marked = True
            for w in o.waw_waits:
                w.marked = True
            for w in o.hard_waits:
                w.marked = True
        val = {}
        cnt = {}
        for o in self.ops:
            if o.marked:
                cnt[o.skey] = cnt.get(o.skey, 0) + o.inc_amt
            val[id(o)] = (o.skey, cnt.get(o.skey, 0))
        def emit_engine(e, eng):
            highest = {}
            for o in self.per_eng[e]:
                needed = {}
                for w in o.waits:
                    we, wv = val[id(w)]
                    if w.eng == e == "tensor" and w.inc_amt == 1 and o.inc_amt == 1:
                        continue
                    if wv > needed.get(we, 0):
                        needed[we] = wv
                for w in o.waw_waits:
                    we, wv = val[id(w)]
                    if w.eng == e == "tensor" and w.inc_amt == 1 and o.inc_amt == 1:
                        continue
                    if we == o.skey and w.inc_amt == 16 and o.inc_amt == 16:
                        continue
                    if wv > needed.get(we, 0):
                        needed[we] = wv
                for w in o.hard_waits:
                    we, wv = val[id(w)]
                    if wv > needed.get(we, 0):
                        needed[we] = wv
                for we, wv in sorted(needed.items()):
                    if wv > highest.get(we, 0):
                        eng.wait_ge(sems[we], wv)
                        highest[we] = wv
                instr = o.fn(eng)
                if o.marked:
                    if instr is None:
                        raise RuntimeError(f"marked op {o.tag} returned None")
                    instr.then_inc(sems[o.skey], o.inc_amt)
        for e in ENGS:
            if self.per_eng[e]:
                getattr(block, e)(lambda eng, e=e: emit_engine(e, eng))


def build_program(nc, taps=()):
    import kernel as _K
    es = contextlib.ExitStack()
    tapset = set(taps)

    def din(name, shape, dtype=F32):
        return nc.dram_tensor(name, list(shape), dtype, kind="ExternalInput").ap()

    xpre_d = din("xpre", [TP, DM])
    xown_d = din("xown", [TO, DM])
    win_d = din("w_in", [DM, 2336])
    convw_d = din("convw", [128, 72])
    convb_d = din("convb", [128, CONVC])
    dtb_d = [din("dtb0", [16, 1]), din("dtb1", [16, 1])]
    alog_d = [din("alog0", [16, 1]), din("alog1", [16, 1])]
    drep_d = din("drep", [128, 16])
    wout_d = din("w_out", [DIN, DM])
    w1_d = din("w1", [NF, DM])
    b1_d = din("b1r", [1, NF])
    w2_d = din("w2", [DM, NF])
    b2_d = din("b2r", [1, DM])
    gate_d = din("gate", [128, 1])
    identb_d = din("identb", [128, 128], BF16)
    tri_d = din("tri", [128, CH], BF16)
    ind2_d = din("ind2", [2, 128], BF16)
    indn_d = din("indn16", [16, 1024], BF16)
    maska_d = din("maska", [16, SEG])
    onesb_d = din("onesb", [1, 128], BF16)
    epsc_d = din("epsc", [128, 1])
    tinyc_d = din("tinyc", [16, 1])

    out_d = nc.dram_tensor("out", [TO, DM], F32, kind="ExternalOutput").ap()
    tap_d = {}

    def scr(name, shape, dtype):
        return nc.dram_tensor(name, list(shape), dtype, kind="Internal").ap()

    xscr_d = [scr("xpscr", [TP, DM], BF16), scr("xoscr", [TO, DM], BF16)]
    xhscr = [scr(f"xhscr{i}", [DIN, SEG], BF16) for i in range(2)]
    yscr = [scr(f"yscr{i}", [SEG, DIN], BF16) for i in range(2)]
    xlnscr = scr("xlnscr", [TO, DM], BF16)
    xq1scr = scr("xq1scr", [TO, DM], BF16)
    wq1scr = scr("wq1scr", [NF, DM], BF16)
    wq2scr = scr("wq2scr", [DM, NF], BF16)
    xq2scr = [scr(f"xq2scr{i}", [SEG, NF], BF16) for i in range(2)]
    accscr = scr("accscr", [64, SEG], BF16)
    ecscr = scr("ecscr", [32, NCHUNK], BF16)

    def TT_(stack, name, shape, dtype):
        return stack.enter_context(nc.sbuf_tensor(name, list(shape), dtype))[:]
    T = lambda n, s, d: TT_(es, n, s, d)

    ident_sb = T("ident_sb", [128, 128], BF16)
    ones128_sb = T("ones128_sb", [128, 128], BF16)
    tri_sb = T("tri_sb", [128, CH], BF16)
    ind2_sb = T("ind2_sb", [2, 128], BF16)
    indn_sb = T("indn_sb", [16, 1024], BF16)
    maska_sb = T("maska_sb", [16, SEG], F32)
    onesb_sb = T("onesb_sb", [1, 128], BF16)
    convb_sb = T("convb_sb", [128, CONVC], F32)
    drep_sb = T("drep_sb", [128, 16], F32)
    gate_sb = T("gate_sb", [128, 1], F32)
    epsc_sb = T("epsc_sb", [128, 1], F32)
    tinyc_sb = T("tinyc_sb", [16, 1], F32)
    convw_sb = T("convw_sb", [128, 72], F32)
    acol_sb = [T(f"acol{g}", [16, 1], F32) for g in range(2)]
    dtb_sb = [T(f"dtbs{g}", [16, 1], F32) for g in range(2)]
    S_sb = [T(f"S{g}", [128, 1024], BF16) for g in range(2)]
    halo_sb = T("halo_sb", [128, CONVC * 3], BF16)
    sw_scale = {k: T(k + "_scale", [128, 1], F32) for k in ("w1", "w2")}
    sw_mean = {k: T(k + "_mean", [128, 1], F32) for k in ("w1", "w2")}
    sw_acc = {k: T(k + "_acc", [128, 32], F32) for k in ("w1", "w2")}
    sw_accb = T("sw_accb", [128, 32], BF16)
    b1row_sb = T("b1row_sb", [1, NF], BF16)
    b2row_sb = T("b2row_sb", [1, DM], BF16)
    st_sb = T("st_sb", [128, 16], F32)
    pln1_sb = T("pln1_sb", [128, DM], F32)
    xln1_sb = T("xln1_sb", [128, DM], F32)
    xln1b_sb = T("xln1b_sb", [128, DM], BF16)

    PSY = es.enter_context(nc.psum_tensor("psy", [128, 1024], F32))[:]
    PSO = es.enter_context(nc.psum_tensor("pso", [128, 1024], F32))[:]
    PST = es.enter_context(nc.psum_tensor("pst", [128, 1024], F32))[:]
    PG = [es.enter_context(nc.psum_tensor(f"pg{i}", [128, 512], F32))[:] for i in range(2)]
    PT = PSY.bitcast(BF16)

    SKEYS = ["const", "wld", "slab", "xfm0", "xfm1", "xT", "yf", "kt0", "kt1", "ec",
             "ecw", "xhw", "yw", "xscrw", "xrl", "ktw", "lnw", "q1w", "q2w0",
             "q2w1", "ow", "wqs_w1", "wqs_w2", "xq1f", "xq2f", "wq1t", "wq2t",
             "xlnt"]
    sems = {e: es.enter_context(nc.semaphore("sem_" + e))
            for e in list(ENGS) + SKEYS}
    block = es.enter_context(nc.Block())

    s = Sched()
    bufs = {}
    def B(name):
        if name not in bufs:
            bufs[name] = Buf(name)
        return bufs[name]

    def tap(name, src_ap, shape, dtype, rbufs):
        if name in tapset:
            tap_d[name] = nc.dram_tensor("tap_" + name, list(shape), dtype,
                                         kind="ExternalOutput").ap()
            s.op("gpsimd", lambda e: e.dma_start(out=tap_d[name], in_=src_ap),
                 reads=[B(r) for r in rbufs], dma=True, tag="tap_" + name)

    aes = contextlib.ExitStack()
    A = lambda n, s_, d: TT_(aes, n, s_, d)
    win_sb = A("win_sb", [128, 8 * 2336], BF16)
    wout_sb = A("wout_sb", [128, 16 * DM], BF16)
    diagw_sb = A("diagw_sb", [128, 72 * 128], BF16)
    xtm_sb = A("xtm_sb", [128, DM], BF16)
    xrl_sb = A("xrl_sb", [128, DM], BF16)
    xfm_sb = [A(f"xfm_sb{i}", [128, 8 * SEG], BF16) for i in range(2)]
    xbc_sb = A("xbc_sb", [128, 2 * (SEG + 3)], BF16)
    xhs_sb = A("xhs_sb", [128, 16 * SEG], BF16)
    bfm_sb = A("bfm_sb", [128, SEG], BF16)
    cfm_sb = A("cfm_sb", [128, SEG], BF16)
    xT_sb = A("xT_sb", [128, NPAIR * DIN], BF16)
    xddt_sb = A("xddt_sb", [128, NPAIR * DIN], BF16)
    bt_sb = A("bt_sb", [128, NPAIR * 128], BF16)
    dtT_sb = [A(f"dtT{g}", [16, SEG], F32) for g in range(2)]
    accp_sb = [A(f"accp{g}", [16, SEG], F32) for g in range(2)]
    acchi_sb = [A(f"acchi{g}", [16, SEG], BF16) for g in range(2)]
    acclo_sb = [A(f"acclo{g}", [16, SEG], BF16) for g in range(2)]
    ea_sb = [A(f"ea{g}", [16, SEG], BF16) for g in range(2)]
    ddt_sb = [A(f"ddt{g}", [16, SEG], BF16) for g in range(2)]
    dtb16_sb = [A(f"dtb16_{g}", [16, SEG], BF16) for g in range(2)]
    sp1_sb = A("sp1_sb", [16, SEG], F32)
    sp2_sb = A("sp2_sb", [16, SEG], F32)
    kt_sb = [A(f"kt{g}", [2, 2 * 1024], BF16) for g in range(2)]
    eat_sb = A("eat_sb", [128, NPAIR * NH], BF16)
    dttm_sb = A("dttm_sb", [128, NPAIR * NH], BF16)
    ddttm_sb = A("ddttm_sb", [128, NPAIR * NH], BF16)
    eb_sb = A("eb_sb", [128, NCHUNK * NH], BF16)
    ecrow_sb = A("ecrow_sb", [1, NCHUNK * NH], BF16)
    ecc_sb = A("ecc_sb", [16, 2 * NCHUNK], BF16)
    lt_sb = A("lt_sb", [128, 1024], BF16)
    m2t_sb = A("m2t_sb", [128, 1024], BF16)
    bcm_sb = A("bcm_sb", [128, CH], BF16)
    ytm_sb = A("ytm_sb", [128, NPAIR * DIN], BF16)
    yfm_sb = A("yfm_sb", [128, 16 * SEG], BF16)
    wslab_sb = [A("wslab0", [128, DM], F32)] * 2
    WSLN = ["wsl0", "wsl0"]
    wqb_sb = A("wqb_sb", [128, DM], BF16)
    wint_sb = A("wint_sb", [128, DM], I32)
    q1o_sb = wqb_sb

    def ld(dst, src, bufname, q="sync"):
        return s.op(q, lambda e: e.dma_start(out=dst, in_=src),
                    writes=[B(bufname)], dma=True, skey="const", chain=True,
                    tag="ld_" + bufname)

    ld(ident_sb, identb_d, "ident")
    ld(tri_sb, tri_d, "tri")
    ld(ind2_sb, ind2_d, "ind2")
    ld(indn_sb, indn_d, "indn")
    ld(maska_sb, maska_d, "maska")
    ld(onesb_sb, onesb_d, "onesb")
    ld(convb_sb, convb_d, "convb")
    ld(drep_sb, drep_d, "drep")
    ld(gate_sb, gate_d, "gate")
    ld(epsc_sb, epsc_d, "epsc")
    ld(tinyc_sb, tinyc_d, "tinyc")
    ld(convw_sb, convw_d, "convw")
    ld(b1row_sb, b1_d, "b1row", q="gpsimd")
    ld(b2row_sb, b2_d, "b2row", q="gpsimd")
    s.op("gpsimd", lambda e: e.dma_start(
        out=win_sb.rearrange("p (k c) -> p k c", k=8),
        in_=win_d.rearrange("(k p) c -> p k c", p=128)),
        writes=[B("win")], dma=True, skey="wld", chain=True, tag="ld_win")
    s.op("gpsimd", lambda e: e.dma_start(
        out=wout_sb.rearrange("p (k c) -> p k c", k=16),
        in_=wout_d.rearrange("(k p) c -> p k c", p=128)),
        writes=[B("wout")], dma=True, skey="wld", chain=True, tag="ld_wout")
    s.op("vector", lambda e: e.memset(ones128_sb, 1.0), writes=[B("ones128")], tag="ones128")
    for wf in ((AF.Exp,) if _K._NOSILU else (AF.Exp, AF.Silu, AF.Gelu, AF.Exp)):
        s.op("scalar", lambda e, wf=wf: e.activation(out=sp1_sb[0:16, 0:16],
                                                     in_=maska_sb[0:16, 0:16], func=wf),
             reads=[B("maska")], writes=[B("sp1")], tag=f"warm_{wf.name}")
    for j in range(72):
        s.op("vector", lambda e, j=j: e.tensor_scalar_mul(
            out=diagw_sb.rearrange("p (j c) -> p j c", j=72)[:, j, :],
            in0=ident_sb, scalar1=convw_sb[:, j:j + 1]),
            reads=[B("ident"), B("convw")], writes=[B("diagw")], tag=f"dw{j}")
    s.op("vector", lambda e: e.memset(halo_sb, 0.0), writes=[B("halo")], tag="halo0")
    for g in range(2):
        s.op("vector", lambda e, g=g: e.memset(S_sb[g], 0.0), writes=[B(f"S{g}")], tag="S0")
    for g in range(2):
        ld(acol_sb[g], alog_d[g], f"acol{g}")
        ld(dtb_sb[g], dtb_d[g], f"dtb{g}")

    def bcast_row(row_ap, rowbuf, n, dst, dstbuf):
        nb = (n + 511) // 512
        for i in range(nb):
            w = min(512, n - i * 512)
            s.op("tensor", lambda e, i=i, w=w: e.matmul(
                PG[0][:, 0:w], onesb_sb[0:1, :], row_ap[0:1, i * 512:i * 512 + w],
                start=True, stop=True),
                reads=[B("onesb"), B(rowbuf)], writes=[B("pg0")], tag="bcmm")
            s.op("vector", lambda e, i=i, w=w: e.tensor_copy(
                out=dst[:, i * 512:i * 512 + w], in_=PG[0][:, 0:w]),
                reads=[B("pg0")], writes=[B(dstbuf)], tag="bcev")

    def wq_prep(wd, rows, cols, scrd, key, scrbuf):
        tiles = [(it, jc) for it in range(rows // 128) for jc in range(cols // DM)]
        for idx, (it, jc) in enumerate(tiles):
            si = idx % 2
            sl = wslab_sb[si]
            s.op("sync" if si == 0 else "gpsimd", lambda e, it=it, jc=jc, sl=sl: e.dma_start(
                out=sl, in_=wd[it * 128:(it + 1) * 128, jc * DM:(jc + 1) * DM]),
                writes=[B(WSLN[si])], dma=True, skey="slab", tag=f"{key}l1_{idx}")
            s.op("scalar", lambda e, idx=idx, sl=sl: e.activation(
                out=sl, in_=sl, func=AF.Abs, accum_out=sw_acc[key][:, idx:idx + 1]),
                reads=[B(WSLN[si])], writes=[B(WSLN[si]), B(key + "acc")], tag=f"{key}a{idx}")
        ntot = len(tiles)
        s.op("vector", lambda e, ntot=ntot: e.tensor_copy(
            out=sw_accb[:, 0:ntot], in_=sw_acc[key][:, 0:ntot]),
            reads=[B(key + "acc")], writes=[B("accb")], tag=f"{key}cb")
        s.op("tensor", lambda e, ntot=ntot: e.matmul(
            PG[0][:, 0:ntot], ones128_sb, sw_accb[:, 0:ntot], start=True, stop=True),
            reads=[B("ones128"), B("accb")], writes=[B("pg0")], tag=f"{key}rm")
        s.op("vector", lambda e, ntot=ntot: e.tensor_reduce(
            out=sw_mean[key], in_=PG[0][:, 0:ntot], axis=AX.X, op=ALU.add),
            reads=[B("pg0")], writes=[B(key + "mean")], tag=f"{key}rd")
        N = float(rows * cols)
        s.op("vector", lambda e: e.tensor_scalar_mul(out=sw_mean[key], in0=sw_mean[key],
                                                     scalar1=1.0 / N),
             reads=[B(key + "mean")], writes=[B(key + "mean")], tag=f"{key}mn")
        s.op("vector", lambda e: e.tensor_scalar_max(out=sw_scale[key], in0=sw_mean[key],
                                                     scalar1=1e-5),
             reads=[B(key + "mean")], writes=[B(key + "sc")], tag=f"{key}mx")
        s.op("vector", lambda e: e.reciprocal(out=sw_scale[key], in_=sw_scale[key]),
             reads=[B(key + "sc")], writes=[B(key + "sc")], tag=f"{key}rc")
        for idx, (it, jc) in enumerate(tiles):
            si = idx % 2
            sl = wslab_sb[si]
            s.op("sync" if si == 0 else "gpsimd", lambda e, it=it, jc=jc, sl=sl: e.dma_start(
                out=sl, in_=wd[it * 128:(it + 1) * 128, jc * DM:(jc + 1) * DM]),
                writes=[B(WSLN[si])], dma=True, skey="slab", tag=f"{key}l2_{idx}")
            s.op("scalar", lambda e, sl=sl: e.activation(
                out=wint_sb, in_=sl, func=AF.Copy, scale=sw_scale[key][:, 0:1]),
                reads=[B(WSLN[si]), B(key + "sc")], writes=[B("wint")], tag=f"{key}r{idx}")
            s.op("vector", lambda e: e.tensor_scalar(
                out=wqb_sb, in0=wint_sb, scalar1=1.0, scalar2=-1.0,
                op0=ALU.min, op1=ALU.max),
                reads=[B("wint")], writes=[B("wqb")], tag=f"{key}c{idx}")
            s.op("gpsimd", lambda e, it=it, jc=jc: e.dma_start(
                out=scrd[it * 128:(it + 1) * 128, jc * DM:(jc + 1) * DM], in_=wqb_sb),
                reads=[B("wqb")], writes=[B(scrbuf)], dma=True, skey="wqs_" + key, tag=f"{key}s{idx}")

    def attn_segment(seg, phase):
        full = (phase == 2)
        t0 = seg * SEG
        xd = xown_d if full else xpre_d
        xscrd = xscr_d[1] if full else xscr_d[0]
        si = seg % 2
        pfx = f"p{phase}s{seg}_"
        istap = full and seg == 0

        for pp in range(NPAIR):
            s.op("gpsimd", lambda e, pp=pp: e.dma_start(
                out=xtm_sb,
                in_=xd[t0 + pp * 128:t0 + (pp + 1) * 128, :]),
                writes=[B("xtm")], dma=True, tag=pfx + f"xtm{pp}")
            s.op("gpsimd", lambda e, pp=pp: e.dma_start(
                out=xscrd[t0 + pp * 128:t0 + (pp + 1) * 128, :], in_=xtm_sb),
                reads=[B("xtm")], writes=[B("xscr")], dma=True, skey="xscrw", tag=pfx + f"xscrw{pp}")
        fs = seg % 2
        xfm = xfm_sb[fs]
        for k in range(8):
            s.op("sync", lambda e, k=k: e.dma_start(
                out=xfm[:, k * SEG:(k + 1) * SEG],
                in_=xscrd[t0:t0 + SEG, k * 128:(k + 1) * 128], transpose=True),
                reads=[B("xscr")], writes=[B(f"xfm{fs}")], dma=True, skey=f"xfm{fs}",
                tag=pfx + f"xfm{k}")

        mtiles = list(range(16)) + [16] + ([17] if (full or seg == NSEG - 1) else [])
        xbcv = xbc_sb.rearrange("p (c q) -> p c q", c=2)
        halov = halo_sb.rearrange("p (c q) -> p c q", q=3)
        for cc in mtiles:
            halo_only = (cc == 17 and not full)
            pid = cc % 2
            for k in range(8):
                s.op("tensor", lambda e, cc=cc, k=k, pid=pid: e.matmul(
                    PG[pid][:, 0:SEG],
                    win_sb.rearrange("p (k c) -> p k c", k=8)[:, k, cc * 128:(cc + 1) * 128],
                    xfm[:, k * SEG:(k + 1) * SEG],
                    start=(k == 0), stop=(k == 7)),
                    reads=[B("win"), B(f"xfm{fs}")], writes=[B(f"pg{pid}")],
                    tag=pfx + f"ip{cc}_{k}")
            s.op("vector", lambda e, cc=cc, pid=pid: e.tensor_copy(
                out=xbcv[:, pid, 3:3 + SEG], in_=PG[pid][:, 0:SEG]),
                reads=[B(f"pg{pid}")], writes=[B(f"xbcr{pid}")], tag=pfx + f"ipe{cc}")
            if not halo_only:
                s.op("vector", lambda e, cc=cc, pid=pid: e.tensor_copy(
                    out=xbcv[:, pid, 0:3], in_=halov[:, cc, :]),
                    reads=[B("halo")], writes=[B(f"xbcr{pid}")], tag=pfx + f"hin{cc}")
            s.op("vector", lambda e, cc=cc, pid=pid: e.tensor_copy(
                out=halov[:, cc, :], in_=xbcv[:, pid, SEG:SEG + 3]),
                reads=[B(f"xbcr{pid}")], writes=[B("halo")], tag=pfx + f"hsv{cc}")
            if halo_only:
                continue
            for k in range(4):
                s.op("tensor", lambda e, cc=cc, k=k, pid=pid: e.matmul(
                    PG[pid][:, 0:SEG],
                    diagw_sb.rearrange("p (j c) -> p j c", j=72)[:, cc * 4 + k, :],
                    xbcv[:, pid, k:k + SEG],
                    start=(k == 0), stop=(k == 3)),
                    reads=[B("diagw"), B(f"xbcr{pid}")],
                    writes=[B(f"pg{pid}")], tag=pfx + f"cv{cc}_{k}")
            if cc < 16:
                dst = xhs_sb[:, cc * SEG:(cc + 1) * SEG]; dbuf = "xhs"
            elif cc == 16:
                dst = bfm_sb; dbuf = "bfm"
            else:
                dst = cfm_sb; dbuf = "cfm"
            s.op("scalar", lambda e, cc=cc, pid=pid, dst=dst: e.activation(
                out=dst, in_=PG[pid][:, 0:SEG], func=(AF.Relu if _K._NOSILU else AF.Silu), bias=convb_sb[:, cc:cc + 1]),
                reads=[B(f"pg{pid}"), B("convb")], writes=[B(dbuf)], tag=pfx + f"si{cc}")
        for g in range(2):
            for k in range(8):
                s.op("tensor", lambda e, g=g, k=k: e.matmul(
                    PG[0][0:16, 0:SEG],
                    win_sb.rearrange("p (k c) -> p k c", k=8)[:, k, 2304 + 16 * g:2304 + 16 * (g + 1)],
                    xfm[:, k * SEG:(k + 1) * SEG],
                    start=(k == 0), stop=(k == 7)),
                    reads=[B("win"), B(f"xfm{fs}")], writes=[B("pg0")],
                    tag=pfx + f"dt{g}_{k}")
            s.op("scalar", lambda e, g=g: e.activation(
                out=sp1_sb, in_=PG[0][0:16, 0:SEG], func=AF.Abs, bias=dtb_sb[g][:, 0:1]),
                reads=[B("pg0"), B(f"dtb{g}")], writes=[B("sp1")], tag=pfx + f"spa{g}")
            s.op("scalar", lambda e: e.activation(out=sp2_sb, in_=sp1_sb, func=AF.Exp, scale=-1.0),
                 reads=[B("sp1")], writes=[B("sp2")], tag=pfx + f"spe{g}")
            s.op("scalar", lambda e: e.activation(out=sp2_sb, in_=sp2_sb, func=AF.Ln, bias=1.0),
                 reads=[B("sp2")], writes=[B("sp2")], tag=pfx + f"spl{g}")
            s.op("scalar", lambda e, g=g: e.activation(
                out=sp1_sb, in_=PG[0][0:16, 0:SEG], func=AF.Relu, bias=dtb_sb[g][:, 0:1]),
                reads=[B("pg0"), B(f"dtb{g}")], writes=[B("sp1")], tag=pfx + f"spr{g}")
            s.op("vector", lambda e, g=g: e.tensor_add(out=dtT_sb[g], in0=sp1_sb, in1=sp2_sb),
                 reads=[B("sp1"), B("sp2")], writes=[B(f"dtT{g}")], tag=pfx + f"spadd{g}")
            s.op("scalar", lambda e, g=g: e.activation(out=sp1_sb, in_=dtT_sb[g], func=AF.Ln,
                                                        bias=tinyc_sb[:, 0:1]),
                reads=[B(f"dtT{g}"), B("tinyc")], writes=[B("sp1")], tag=pfx + f"adtln{g}")
            s.op("scalar", lambda e, g=g: e.activation(
                out=sp2_sb, in_=sp1_sb, func=AF.Exp, bias=acol_sb[g][:, 0:1]),
                reads=[B("sp1"), B(f"acol{g}")], writes=[B("sp2")], tag=pfx + f"adtex{g}")
            s.op("vector", lambda e, g=g: e.tensor_tensor_scan(
                out=accp_sb[g], data0=maska_sb, data1=sp2_sb,
                initial=0.0, op0=ALU.mult, op1=ALU.add),
                reads=[B("maska"), B("sp2")], writes=[B(f"accp{g}")], tag=pfx + f"scan{g}")
            s.op("vector", lambda e, g=g: e.tensor_scalar_mul(
                out=acchi_sb[g], in0=accp_sb[g], scalar1=-1.0),
                reads=[B(f"accp{g}")], writes=[B(f"acchi{g}")], tag=pfx + f"hi{g}")
            s.op("vector", lambda e, g=g: e.scalar_tensor_tensor(
                out=acclo_sb[g], in0=accp_sb[g], scalar=-1.0, in1=acchi_sb[g],
                op0=ALU.mult, op1=ALU.subtract),
                reads=[B(f"accp{g}"), B(f"acchi{g}")], writes=[B(f"acclo{g}")], tag=pfx + f"lo{g}")
            s.op("scalar", lambda e, g=g: e.activation(out=ea_sb[g], in_=accp_sb[g], func=AF.Exp,
                                                        scale=-1.0),
                 reads=[B(f"accp{g}")], writes=[B(f"ea{g}")], tag=pfx + f"ea{g}")
            s.op("vector", lambda e, g=g: e.tensor_tensor(
                out=sp1_sb.rearrange("h (c l) -> h c l", l=CH),
                in0=accp_sb[g].rearrange("h (c l) -> h c l", l=CH),
                in1=accp_sb[g].rearrange("h (c l) -> h c l", l=CH)[:, :, CH - 1:CH]
                    .broadcast_to([16, NCHUNK, CH]), op=ALU.subtract),
                reads=[B(f"accp{g}")], writes=[B("sp1")], tag=pfx + f"dda{g}")
            s.op("scalar", lambda e: e.activation(out=sp2_sb, in_=sp1_sb, func=AF.Exp),
                 reads=[B("sp1")], writes=[B("sp2")], tag=pfx + f"dde{g}")
            s.op("vector", lambda e, g=g: e.tensor_tensor(
                out=ddt_sb[g], in0=sp2_sb, in1=dtT_sb[g], op=ALU.mult),
                reads=[B("sp2"), B(f"dtT{g}")], writes=[B(f"ddt{g}")], tag=pfx + f"ddm{g}")
            if full:
                s.op("vector", lambda e, g=g: e.tensor_copy(out=dtb16_sb[g], in_=dtT_sb[g]),
                     reads=[B(f"dtT{g}")], writes=[B(f"dtb16{g}")], tag=pfx + f"d16{g}")

        if istap:
            tap("xfm", xfm_sb[fs], [128, 8 * SEG], BF16, [f"xfm{fs}"])
            tap("xhs", xhs_sb, [128, 16 * SEG], BF16, ["xhs"])
            tap("bfm", bfm_sb, [128, SEG], BF16, ["bfm"])
            tap("cfm", cfm_sb, [128, SEG], BF16, ["cfm"])
            tap("dtT0", dtT_sb[0], [16, SEG], F32, ["dtT0"])
            tap("accp0", accp_sb[0], [16, SEG], F32, ["accp0"])
            tap("ea0", ea_sb[0], [16, SEG], BF16, ["ea0"])
            tap("ddt0", ddt_sb[0], [16, SEG], BF16, ["ddt0"])
            tap("ddt1", ddt_sb[1], [16, SEG], BF16, ["ddt1"])
            tap("dtT1", dtT_sb[1], [16, SEG], F32, ["dtT1"])
            tap("accp1", accp_sb[1], [16, SEG], F32, ["accp1"])

        tp_jobs = [(ddt_sb, ddttm_sb, "ddt", "ddttm")]
        if full:
            tp_jobs += [(ea_sb, eat_sb, "ea", "eat"), (dtb16_sb, dttm_sb, "dtb16", "dttm")]
        for (srcs, dst, sname, dname) in tp_jobs:
            for p in range(NPAIR):
                for g in range(2):
                    s.op("tensor", lambda e, srcs=srcs, p=p, g=g: e.transpose(
                        PT[:, (p * NH + g * 16):(p * NH + g * 16) + 16],
                        srcs[g][0:16, p * 128:(p + 1) * 128], ident_sb[0:16, 0:16]),
                        reads=[B(f"{sname}{g}"), B("ident")], writes=[B("psy")],
                        tag=pfx + f"t{sname}{p}{g}")
            s.op("vector", lambda e, dst=dst: e.tensor_copy(out=dst, in_=PT[:, 0:NPAIR * NH]),
                 reads=[B("psy")], writes=[B(dname)], tag=pfx + f"te{sname}")
        for p in range(NPAIR):
            s.op("tensor", lambda e, p=p: e.transpose(
                PT[:, p * 128:(p + 1) * 128], bfm_sb[:, p * 128:(p + 1) * 128], ident_sb),
                reads=[B("bfm"), B("ident")], writes=[B("psy")], tag=pfx + f"tb{p}")
        s.op("vector", lambda e: e.tensor_copy(out=bt_sb, in_=PT[:, 0:NPAIR * 128]),
             reads=[B("psy")], writes=[B("bt")], tag=pfx + "tbe")

        for g in range(2):
            s.op("vector", lambda e, g=g: e.tensor_copy(
                out=ecc_sb[:, g * NCHUNK:(g + 1) * NCHUNK],
                in_=ea_sb[g].rearrange("h (c l) -> h c l", l=CH)[:, :, CH - 1]),
                reads=[B(f"ea{g}")], writes=[B("ecc")], tag=pfx + f"ecc{g}")
        s.op("gpsimd", lambda e: e.dma_start(
            out=ecscr.rearrange("(g h) c -> h g c", g=2),
            in_=ecc_sb.rearrange("h (g c) -> h g c", g=2)),
            reads=[B("ecc")], writes=[B("ecscr")], dma=True, skey="ecw", tag=pfx + "ecw")
        s.op("sync", lambda e: e.dma_start(
            out=ecrow_sb,
            in_=ecscr.rearrange("h c -> (h c)")[None, :]),
            reads=[B("ecscr")], writes=[B("ecrow")], dma=True, skey="ec", tag=pfx + "ec")
        s.op("tensor", lambda e: e.matmul(
            PG[0][:, 0:NCHUNK * NH], onesb_sb[0:1, :], ecrow_sb[0:1, :],
            start=True, stop=True),
            reads=[B("ecrow"), B("onesb")], writes=[B("pg0")], tag=pfx + "ebm")
        s.op("vector", lambda e: e.tensor_copy(out=eb_sb, in_=PG[0][:, 0:NCHUNK * NH]),
             reads=[B("pg0")], writes=[B("eb")], tag=pfx + "ebe")

        s.op("gpsimd", lambda e, si=si: e.dma_start(
            out=xhscr[si].rearrange("(c p) t -> p c t", p=128),
            in_=xhs_sb.rearrange("p (c t) -> p c t", c=16)),
            reads=[B("xhs")], writes=[B(f"xhscr{si}")], dma=True, skey="xhw", tag=pfx + "xhw")
        for p in range(NPAIR):
            s.op("sync", lambda e, p=p, si=si: e.dma_start(
                out=xT_sb[:, p * DIN:(p + 1) * DIN],
                in_=xhscr[si][:, p * 128:(p + 1) * 128], transpose=True),
                reads=[B(f"xhscr{si}")], writes=[B("xT")], dma=True, skey="xT", tag=pfx + f"xT{p}")
        for p in range(NPAIR):
            s.op("vector", lambda e, p=p: e.tensor_tensor(
                out=xddt_sb[:, p * DIN:(p + 1) * DIN].rearrange("p (h q) -> p h q", h=NH),
                in0=xT_sb[:, p * DIN:(p + 1) * DIN].rearrange("p (h q) -> p h q", h=NH),
                in1=ddttm_sb.rearrange("p (r h) -> p r h", r=NPAIR)[:, p, :, None]
                    .broadcast_to([128, NH, HD]),
                op=ALU.mult),
                reads=[B("xT"), B("ddttm")], writes=[B("xddt")], tag=pfx + f"xd{p}")
        if istap:
            tap("eat", eat_sb, [128, NPAIR * NH], BF16, ["eat"])
            tap("dttm", dttm_sb, [128, NPAIR * NH], BF16, ["dttm"])
            tap("ddttm", ddttm_sb, [128, NPAIR * NH], BF16, ["ddttm"])
            tap("xT", xT_sb, [128, NPAIR * DIN], BF16, ["xT"])
            tap("xddt", xddt_sb, [128, NPAIR * DIN], BF16, ["xddt"])
            tap("bt", bt_sb, [128, NPAIR * 128], BF16, ["bt"])
            tap("eb", eb_sb, [128, NCHUNK * NH], BF16, ["eb"])

        if full:
            for g in range(2):
                for hl, src, sname in ((0, acchi_sb[g], f"acchi{g}"), (1, acclo_sb[g], f"acclo{g}")):
                    s.op("gpsimd", lambda e, g=g, hl=hl, src=src: e.dma_start(
                        out=accscr[g * 32 + hl * 16:g * 32 + hl * 16 + 16, :], in_=src),
                        reads=[B(sname)], writes=[B("accscr")], dma=True, skey="ktw", tag=pfx + f"ktw{g}{hl}")

        ablate_pair = (f"pair{phase}" in _K._ABLATE)
        for p in range(NPAIR) if not ablate_pair else []:
            if full:
                for g in range(2):
                    for hl in range(2):
                        s.op("sync", lambda e, g=g, hl=hl, p=p: e.dma_start(
                            out=kt_sb[g].rearrange("r (x h l) -> r x h l", x=2, h=16)
                                [:, hl, :, :],
                            in_=accscr[g * 32 + hl * 16:g * 32 + hl * 16 + 16, :]
                                .rearrange("h (p r l) -> r p h l", p=NPAIR, r=2)[:, p, :, :]),
                            reads=[B("accscr")], writes=[B(f"kt{g}")], dma=True,
                            skey=f"kt{g}", tag=pfx + f"kt{g}{hl}{p}")
                for ci in range(2):
                    s.op("tensor", lambda e, p=p, ci=ci: e.matmul(
                        PST[ci * 64:(ci + 1) * 64, 0:CH],
                        bfm_sb[:, (p * 2 + ci) * CH:(p * 2 + ci + 1) * CH],
                        cfm_sb[:, (p * 2 + ci) * CH:(p * 2 + ci + 1) * CH],
                        start=True, stop=True, skip_group_check=True),
                        reads=[B("bfm"), B("cfm")], writes=[B("pst")], tag=pfx + f"cb{p}{ci}")
                s.op("vector", lambda e: e.tensor_tensor(
                    out=bcm_sb, in0=PST[:, 0:CH], in1=tri_sb, op=ALU.mult),
                    reads=[B("pst"), B("tri")], writes=[B("bcm")], tag=pfx + f"bcm{p}")
            for g in range(2):
                sg = f"{p}{g}"
                if full:
                    ktv = kt_sb[g].rearrange("r (x q) -> r x q", x=2)
                    for half in range(2):
                        cs = slice(half * 512, (half + 1) * 512)
                        s.op("tensor", lambda e, cs=cs, ktv=ktv: e.matmul(
                            PSY[:, cs], ind2_sb, ktv[:, 0, cs],
                            start=True, stop=False, skip_group_check=True),
                            reads=[B(f"kt{g}"), B("ind2")], writes=[B("psy")], tag=pfx + "d1" + sg)
                        s.op("tensor", lambda e, cs=cs, ktv=ktv: e.matmul(
                            PSY[:, cs], ind2_sb, ktv[:, 1, cs],
                            start=False, stop=False, skip_group_check=True),
                            reads=[B(f"kt{g}")], writes=[B("psy")], tag=pfx + "d2" + sg)
                        s.op("tensor", lambda e, p=p, g=g, cs=cs: e.matmul(
                            PSY[:, cs], acchi_sb[g][0:16, p * 128:(p + 1) * 128],
                            indn_sb[:, cs],
                            start=False, stop=False, skip_group_check=True),
                            reads=[B(f"acchi{g}"), B("indn")], writes=[B("psy")], tag=pfx + "d3" + sg)
                        s.op("tensor", lambda e, p=p, g=g, cs=cs: e.matmul(
                            PSY[:, cs], acclo_sb[g][0:16, p * 128:(p + 1) * 128],
                            indn_sb[:, cs],
                            start=False, stop=True, skip_group_check=True),
                            reads=[B(f"acclo{g}"), B("indn")], writes=[B("psy")], tag=pfx + "d4" + sg)
                    s.op("vector", lambda e: e.tensor_scalar_min(out=m2t_sb, in0=PSY, scalar1=0.0),
                         reads=[B("psy")], writes=[B("m2t")], tag=pfx + "dm" + sg)
                    s.op("scalar", lambda e: e.activation(out=lt_sb, in_=m2t_sb, func=AF.Exp),
                         reads=[B("m2t")], writes=[B("lt")], tag=pfx + "lx" + sg)
                    s.op("vector", lambda e, p=p, g=g: e.tensor_tensor(
                        out=lt_sb.rearrange("p (h l) -> p h l", h=16),
                        in0=lt_sb.rearrange("p (h l) -> p h l", h=16),
                        in1=dttm_sb.rearrange("p (r h) -> p r h", r=NPAIR)
                            [:, p, g * 16:(g + 1) * 16, None].broadcast_to([128, 16, CH]),
                        op=ALU.mult),
                        reads=[B("lt"), B("dttm")], writes=[B("lt")], tag=pfx + "ld" + sg)
                    s.op("vector", lambda e: e.tensor_tensor(
                        out=m2t_sb.rearrange("p (h l) -> p h l", h=16),
                        in0=lt_sb.rearrange("p (h l) -> p h l", h=16),
                        in1=bcm_sb[:, None, :].broadcast_to([128, 16, CH]),
                        op=ALU.mult),
                        reads=[B("lt"), B("bcm")], writes=[B("m2t")], tag=pfx + "m2" + sg)
                    if istap and p == 0 and g == 0:
                        tap("m2t", m2t_sb, [128, 1024], BF16, ["m2t"])
                    if istap and p == 0 and g == 1:
                        tap("lt1", lt_sb, [128, 1024], BF16, ["lt"])
                        tap("m2t1", m2t_sb, [128, 1024], BF16, ["m2t"])
                    for h in range(16):
                        hh = g * 16 + h
                        for ci in range(2):
                            s.op("tensor", lambda e, p=p, h=h, hh=hh, ci=ci: e.matmul(
                                PSY[ci * 64:(ci + 1) * 64, h * 64:(h + 1) * 64],
                                m2t_sb.rearrange("p (h l) -> p h l", h=16)[ci * 64:(ci + 1) * 64, h, :],
                                xT_sb[ci * 64:(ci + 1) * 64,
                                      p * DIN + hh * 64:p * DIN + (hh + 1) * 64],
                                start=True, stop=True, skip_group_check=True),
                                reads=[B("m2t"), B("xT")], writes=[B("psy")],
                                tag=pfx + f"yd{h}{ci}" + sg)
                for ci in range(2):
                    cloc = p * 2 + ci
                    if full:
                        for half in range(2):
                            s.op("tensor", lambda e, cloc=cloc, ci=ci, g=g, half=half: e.matmul(
                                PSO[ci * 64:(ci + 1) * 64, half * 512:(half + 1) * 512],
                                cfm_sb[:, cloc * CH:(cloc + 1) * CH],
                                S_sb[g][:, half * 512:(half + 1) * 512],
                                start=True, stop=True, skip_group_check=True),
                                reads=[B("cfm"), B(f"S{g}")], writes=[B("pso")],
                                tag=pfx + f"yo{ci}{half}" + sg)
                    for half in range(2):
                        s.op("tensor", lambda e, p=p, ci=ci, g=g, half=half: e.matmul(
                            PST[:, half * 512:(half + 1) * 512],
                            bt_sb[ci * 64:(ci + 1) * 64, p * 128:(p + 1) * 128],
                            xddt_sb[ci * 64:(ci + 1) * 64,
                                    p * DIN + g * 1024 + half * 512:
                                    p * DIN + g * 1024 + (half + 1) * 512],
                            start=True, stop=True, skip_group_check=True),
                            reads=[B("bt"), B("xddt")], writes=[B("pst")],
                            tag=pfx + f"st{ci}{half}" + sg)
                    s.op("vector", lambda e, cloc=cloc, g=g: e.tensor_tensor(
                        out=S_sb[g].rearrange("p (h q) -> p h q", h=16),
                        in0=S_sb[g].rearrange("p (h q) -> p h q", h=16),
                        in1=eb_sb.rearrange("p (g h c) -> p g h c", g=2, h=16)
                            [:, g, :, cloc, None].broadcast_to([128, 16, HD]),
                        op=ALU.mult),
                        reads=[B("eb")], writes=[B(f"S{g}")], tag=pfx + f"sm{ci}" + sg)
                    s.op("vector", lambda e, g=g: e.tensor_tensor(
                        out=S_sb[g], in0=S_sb[g], in1=PST, op=ALU.add),
                        reads=[B("pst")], writes=[B(f"S{g}")], tag=pfx + f"sa{ci}" + sg)
                if full:
                    s.op("vector", lambda e, p=p, g=g: e.tensor_tensor(
                        out=lt_sb.rearrange("p (h q) -> p h q", h=16),
                        in0=PSO.rearrange("p (h q) -> p h q", h=16),
                        in1=eat_sb.rearrange("p (r h) -> p r h", r=NPAIR)
                            [:, p, g * 16:(g + 1) * 16, None].broadcast_to([128, 16, HD]),
                        op=ALU.mult),
                        reads=[B("pso"), B("eat")], writes=[B("lt")], tag=pfx + "ym" + sg)
                    s.op("vector", lambda e, p=p, g=g: e.tensor_tensor(
                        out=ytm_sb[:, p * DIN + g * 1024:p * DIN + (g + 1) * 1024],
                        in0=lt_sb, in1=PSY, op=ALU.add),
                        reads=[B("lt"), B("psy")], writes=[B("ytm")], tag=pfx + "ya" + sg)
                if _K._BMODE >= 3:
                    s.barrier()

        if not full:
            return
        if istap:
            tap("ytm", ytm_sb, [128, NPAIR * DIN], BF16, ["ytm"])

        s.op("gpsimd", lambda e, si=si: e.dma_start(
            out=yscr[si].rearrange("(r p) c -> p r c", p=128),
            in_=ytm_sb.rearrange("p (r c) -> p r c", r=NPAIR)),
            reads=[B("ytm")], writes=[B(f"yscr{si}")], dma=True, skey="yw", tag=pfx + "yw")
        for cc in range(16):
            s.op("sync", lambda e, cc=cc, si=si: e.dma_start(
                out=yfm_sb[:, cc * SEG:(cc + 1) * SEG],
                in_=yscr[si][:, cc * 128:(cc + 1) * 128], transpose=True),
                reads=[B(f"yscr{si}")], writes=[B("yfmr")], dma=True, skey="yf",
                tag=pfx + f"yf{cc}")
        for cc in range(16):
            s.op("vector", lambda e, cc=cc: e.scalar_tensor_tensor(
                out=yfm_sb[:, cc * SEG:(cc + 1) * SEG],
                in0=xhs_sb[:, cc * SEG:(cc + 1) * SEG],
                scalar=drep_sb[:, cc:cc + 1],
                in1=yfm_sb[:, cc * SEG:(cc + 1) * SEG],
                op0=ALU.mult, op1=ALU.add),
                reads=[B("yfmr"), B("xhs"), B("drep")], writes=[B(f"yfm{cc}")],
                tag=pfx + f"yD{cc}")
        if istap:
            tap("yfm", yfm_sb, [128, 16 * SEG], BF16, [f"yfm{c}" for c in range(16)])

        for tt in range(NPAIR):
            for cc in range(16):
                for half in range(2):
                    s.op("tensor", lambda e, tt=tt, cc=cc, half=half: e.matmul(
                        PSY[:, half * 512:(half + 1) * 512],
                        yfm_sb[:, cc * SEG + tt * 128:cc * SEG + (tt + 1) * 128],
                        wout_sb.rearrange("p (c q) -> p c q", c=16)[:, cc, half * 512:(half + 1) * 512],
                        start=(cc == 0), stop=(cc == 15), skip_group_check=True),
                        reads=[B(f"yfm{cc}"), B("wout")], writes=[B("psy")],
                        tag=pfx + f"op{tt}{cc}{half}")
            s.op("gpsimd", lambda e, tt=tt: e.dma_start(
                out=xrl_sb, in_=xscrd[t0 + tt * 128:t0 + (tt + 1) * 128, :]),
                reads=[B("xscr")], writes=[B("xrl")], dma=True, skey="xrl", tag=pfx + f"xrl{tt}")
            s.op("vector", lambda e, tt=tt: e.tensor_tensor(
                out=pln1_sb, in0=PSY, in1=xrl_sb, op=ALU.add),
                reads=[B("psy"), B("xrl")], writes=[B("pln1")], tag=pfx + f"res{tt}")
            _ln_quant(pfx, tt, t0, istap)

    def _ln_quant(pfx, tt, t0, istap):
        s.op("vector", lambda e: e.tensor_reduce(
            out=st_sb[:, 0:1], in_=pln1_sb, axis=AX.X, op=ALU.add),
            reads=[B("pln1")], writes=[B("st0")], tag=pfx + f"mu{tt}")
        s.op("vector", lambda e: e.tensor_scalar_mul(out=st_sb[:, 0:1], in0=st_sb[:, 0:1],
                                                     scalar1=1.0 / DM),
             reads=[B("st0")], writes=[B("st0")], tag=pfx + f"mu2{tt}")
        s.op("scalar", lambda e: e.activation(
            out=xln1b_sb, in_=pln1_sb, func=AF.Square, accum_out=st_sb[:, 1:2]),
            reads=[B("pln1")], writes=[B("st1"), B("xlnb")], tag=pfx + f"sq{tt}")
        s.op("vector", lambda e: e.tensor_scalar_mul(out=st_sb[:, 1:2], in0=st_sb[:, 1:2],
                                                     scalar1=1.0 / DM),
             reads=[B("st1")], writes=[B("st1")], tag=pfx + f"ex2{tt}")
        s.op("vector", lambda e: e.tensor_tensor(
            out=st_sb[:, 2:3], in0=st_sb[:, 0:1], in1=st_sb[:, 0:1], op=ALU.mult),
            reads=[B("st0")], writes=[B("st2")], tag=pfx + f"m2{tt}")
        s.op("vector", lambda e: e.tensor_sub(
            out=st_sb[:, 2:3], in0=st_sb[:, 1:2], in1=st_sb[:, 2:3]),
            reads=[B("st1"), B("st2")], writes=[B("st2")], tag=pfx + f"var{tt}")
        s.op("scalar", lambda e: e.activation(
            out=st_sb[:, 3:4], in_=st_sb[:, 2:3], func=AF.Ln, bias=epsc_sb[:, 0:1]),
            reads=[B("st2"), B("epsc")], writes=[B("st3")], tag=pfx + f"lns{tt}")
        s.op("scalar", lambda e: e.activation(
            out=st_sb[:, 3:4], in_=st_sb[:, 3:4], func=AF.Exp, scale=-0.5),
            reads=[B("st3")], writes=[B("st3")], tag=pfx + f"rstd{tt}")
        s.op("vector", lambda e: e.tensor_tensor(
            out=st_sb[:, 4:5], in0=st_sb[:, 0:1], in1=st_sb[:, 3:4], op=ALU.mult),
            reads=[B("st0"), B("st3")], writes=[B("st4")], tag=pfx + f"mr{tt}")
        s.op("vector", lambda e: e.tensor_scalar_mul(out=st_sb[:, 4:5], in0=st_sb[:, 4:5],
                                                     scalar1=-1.0),
             reads=[B("st4")], writes=[B("st4")], tag=pfx + f"nmr{tt}")
        s.op("vector", lambda e: e.tensor_scalar(
            out=xln1_sb, in0=pln1_sb, scalar1=st_sb[:, 3:4], scalar2=st_sb[:, 4:5],
            op0=ALU.mult, op1=ALU.add),
            reads=[B("pln1"), B("st3"), B("st4")], writes=[B("xln1")], tag=pfx + f"lna{tt}")
        s.op("vector", lambda e: e.tensor_copy(out=xln1b_sb, in_=xln1_sb),
             reads=[B("xln1")], writes=[B("xlnb")], tag=pfx + f"lnb{tt}")
        s.op("gpsimd", lambda e: e.dma_start(
            out=xlnscr[t0 + tt * 128:t0 + (tt + 1) * 128, :], in_=xln1b_sb),
            reads=[B("xlnb")], writes=[B("xlnscr")], dma=True, skey="lnw", tag=pfx + f"lnw{tt}")
        s.op("vector", lambda e: e.tensor_tensor(
            out=st_sb[:, 5:6], in0=st_sb[:, 3:4], in1=st_sb[:, 3:4], op=ALU.mult),
            reads=[B("st3")], writes=[B("st5")], tag=pfx + f"r2a{tt}")
        s.op("vector", lambda e: e.tensor_tensor(
            out=st_sb[:, 5:6], in0=st_sb[:, 5:6], in1=st_sb[:, 2:3], op=ALU.mult),
            reads=[B("st5"), B("st2")], writes=[B("st5")], tag=pfx + f"r2b{tt}")
        s.op("scalar", lambda e: e.activation(
            out=st_sb[:, 5:6], in_=st_sb[:, 5:6], func=AF.Ln, bias=epsc_sb[:, 0:1]),
            reads=[B("st5"), B("epsc")], writes=[B("st5")], tag=pfx + f"r2c{tt}")
        s.op("scalar", lambda e: e.activation(
            out=st_sb[:, 5:6], in_=st_sb[:, 5:6], func=AF.Exp, scale=-0.5),
            reads=[B("st5")], writes=[B("st5")], tag=pfx + f"r2d{tt}")
        s.op("vector", lambda e: e.reduce_max(
            out=st_sb[:, 6:7], in_=xln1_sb, axis=AX.X, apply_absolute_value=True),
            reads=[B("xln1")], writes=[B("st6")], tag=pfx + f"am{tt}")
        s.op("vector", lambda e: e.tensor_tensor(
            out=st_sb[:, 6:7], in0=st_sb[:, 6:7], in1=st_sb[:, 5:6], op=ALU.mult),
            reads=[B("st6"), B("st5")], writes=[B("st6")], tag=pfx + f"am2{tt}")
        s.op("vector", lambda e: e.tensor_scalar_max(
            out=st_sb[:, 6:7], in0=st_sb[:, 6:7], scalar1=1e-5),
            reads=[B("st6")], writes=[B("st6")], tag=pfx + f"am3{tt}")
        s.op("vector", lambda e: e.reciprocal(out=st_sb[:, 7:8], in_=st_sb[:, 6:7]),
             reads=[B("st6")], writes=[B("st7")], tag=pfx + f"rs{tt}")
        s.op("vector", lambda e: e.tensor_scalar_mul(
            out=st_sb[:, 7:8], in0=st_sb[:, 7:8], scalar1=127.0),
            reads=[B("st7")], writes=[B("st7")], tag=pfx + f"rs2{tt}")
        s.op("vector", lambda e: e.tensor_tensor(
            out=st_sb[:, 8:9], in0=st_sb[:, 7:8], in1=st_sb[:, 5:6], op=ALU.mult),
            reads=[B("st7"), B("st5")], writes=[B("st8")], tag=pfx + f"rs3{tt}")
        s.op("vector", lambda e: e.tensor_scalar_mul(
            out=st_sb[:, 9:10], in0=st_sb[:, 6:7], scalar1=1.0 / 127.0),
            reads=[B("st6")], writes=[B("st9")], tag=pfx + f"inv{tt}")
        s.op("scalar", lambda e: e.activation(
            out=wint_sb, in_=xln1_sb, func=AF.Copy, scale=st_sb[:, 8:9]),
            reads=[B("xln1"), B("st8")], writes=[B("wint")], tag=pfx + f"q1i{tt}")
        s.op("vector", lambda e: e.tensor_scalar_mul(
            out=q1o_sb, in0=wint_sb, scalar1=st_sb[:, 9:10]),
            reads=[B("wint"), B("st9")], writes=[B("wqb")], tag=pfx + f"q1o{tt}")
        s.op("gpsimd", lambda e: e.dma_start(
            out=xq1scr[t0 + tt * 128:t0 + (tt + 1) * 128, :], in_=q1o_sb),
            reads=[B("wqb")], writes=[B("xq1scr")], dma=True, skey="q1w", tag=pfx + f"q1w{tt}")
        if istap and tt == 0:
            tap("st", st_sb, [128, 16], F32, ["st0", "st1", "st2", "st3", "st4", "st5", "st6", "st7", "st8", "st9"])
            tap("pln1", pln1_sb, [128, DM], F32, ["pln1"])
            tap("xln1", xln1_sb, [128, DM], F32, ["xln1"])
            tap("q1o", q1o_sb, [128, DM], BF16, ["wqb"])

    for seg in range(NSEG):
        attn_segment(seg, 1)
        if _K._BARRIER_SEGS:
            s.barrier()
    wq_prep(w1_d, NF, DM, wq1scr, "w1", "wq1scr")
    for g in range(2):
        s.op("vector", lambda e, g=g: e.tensor_scalar_mul(
            out=S_sb[g], in0=S_sb[g], scalar1=gate_sb[:, 0:1]),
            reads=[B(f"S{g}"), B("gate")], writes=[B(f"S{g}")], tag=f"gateS{g}")
    s.op("vector", lambda e: e.tensor_scalar_mul(
        out=halo_sb, in0=halo_sb, scalar1=gate_sb[:, 0:1]),
        reads=[B("halo"), B("gate")], writes=[B("halo")], tag="gateH")
    tap("Sg0", S_sb[0], [128, 1024], BF16, ["S0"])
    tap("acol0", acol_sb[0], [16, 1], F32, ["acol0"])
    tap("acol1", acol_sb[1], [16, 1], F32, ["acol1"])
    for seg in range(NSEG):
        attn_segment(seg, 2)
        if _K._BARRIER_SEGS:
            s.barrier()
    wq_prep(w2_d, DM, NF, wq2scr, "w2", "wq2scr")

    aes.close()
    s.barrier()
    P3 = lambda n, s_, d: TT_(es, n, s_, d)
    wq1t_sb = P3("wq1t_sb", [128, 8 * NF], BF16)
    b1bc_sb = P3("b1bc_sb", [128, NF], BF16)
    b2bc_sb = P3("b2bc_sb", [128, DM], BF16)
    bcast_row(b1row_sb, "b1row", NF, b1bc_sb, "b1bc")
    bcast_row(b2row_sb, "b2row", DM, b2bc_sb, "b2bc")
    wq2t_sb = P3("wq2t_sb", [128, 32 * DM], BF16)
    xq1f_sb = P3("xq1f_sb", [128, 8 * SEG], BF16)
    xlnt_sb = P3("xlnt_sb", [128, NPAIR * DM], BF16)
    hpre_sb = P3("hpre_sb", [128, NF], BF16)
    q2i_sb = P3("q2i_sb", [128, 1024], I32)
    q2o_sb = P3("q2o_sb", [128, 1024], BF16)
    xq2f_sb = P3("xq2f_sb", [128, 32 * SEG], BF16)
    pln2_sb = pln1_sb
    rb2_sb = xln1b_sb
    outr_sb = xln1_sb

    tap("wq1scr", wq1scr, [NF, DM], BF16, ["wq1scr"])
    for k in range(8):
        s.op("sync", lambda e, k=k: e.dma_start(
            out=wq1t_sb.rearrange("p (k c) -> p k c", k=8)[:, k, :],
            in_=wq1scr[:, k * 128:(k + 1) * 128], transpose=True),
            reads=[B("wq1scr")], writes=[B("wq1t")], dma=True, skey="wq1t",
            tag=f"wq1t{k}")
    for k in range(32):
        s.op("sync", lambda e, k=k: e.dma_start(
            out=wq2t_sb.rearrange("p (k c) -> p k c", k=32)[:, k, :],
            in_=wq2scr[:, k * 128:(k + 1) * 128], transpose=True),
            reads=[B("wq2scr")], writes=[B("wq2t")], dma=True, skey="wq2t",
            tag=f"wq2t{k}")

    def ffn_segment(seg):
        t0 = seg * SEG
        si = seg % 2
        pfx = f"p3s{seg}_"
        istap = seg == 0
        for k in range(8):
            s.op("sync", lambda e, k=k: e.dma_start(
                out=xq1f_sb[:, k * SEG:(k + 1) * SEG],
                in_=xq1scr[t0:t0 + SEG, k * 128:(k + 1) * 128], transpose=True),
                reads=[B("xq1scr")], writes=[B("xq1f")], dma=True, skey="xq1f",
                tag=pfx + f"q1f{k}")
        s.op("gpsimd", lambda e: e.dma_start(
            out=xlnt_sb.rearrange("p (r c) -> p r c", r=NPAIR),
            in_=xlnscr.rearrange("(r p) c -> p r c", p=128)[:, NPAIR * seg:NPAIR * (seg + 1), :]),
            reads=[B("xlnscr")], writes=[B("xlnt")], dma=True, skey="xlnt", tag=pfx + "xlnt")
        for tt in range(NPAIR):
            for fc in range(8):
                pid = fc % 2
                for k in range(8):
                    s.op("tensor", lambda e, tt=tt, fc=fc, k=k, pid=pid: e.matmul(
                        PG[pid][:, 0:512],
                        xq1f_sb[:, k * SEG + tt * 128:k * SEG + (tt + 1) * 128],
                        wq1t_sb.rearrange("p (k c) -> p k c", k=8)[:, k, fc * 512:(fc + 1) * 512],
                        start=(k == 0), stop=(k == 7)),
                        reads=[B("xq1f"), B("wq1t")], writes=[B(f"pg{pid}")],
                        tag=pfx + f"m1_{tt}{fc}{k}")
                s.op("vector", lambda e, fc=fc, pid=pid: e.scalar_tensor_tensor(
                    out=hpre_sb[:, fc * 512:(fc + 1) * 512],
                    in0=PG[pid][:, 0:512], scalar=sw_mean["w1"][:, 0:1],
                    in1=b1bc_sb[:, fc * 512:(fc + 1) * 512],
                    op0=ALU.mult, op1=ALU.add),
                    reads=[B(f"pg{pid}"), B("w1mean"), B("b1bc")], writes=[B("hpre")],
                    tag=pfx + f"e1_{tt}{fc}")
            if istap and tt == 0:
                tap("hpre", hpre_sb, [128, NF], BF16, ["hpre"])
                tap("xq1f", xq1f_sb, [128, 8 * SEG], BF16, ["xq1f"])
                tap("b1bc", b1bc_sb, [128, NF], BF16, ["b1bc"])
                tap("swm1", sw_mean["w1"], [128, 1], F32, ["w1mean"])
                tap("sws1", sw_scale["w1"], [128, 1], F32, ["w1sc"])
                tap("wq1t", wq1t_sb, [128, 8 * NF], BF16, ["wq1t"])
            s.op("scalar", lambda e: e.activation(out=hpre_sb, in_=hpre_sb, func=(AF.Relu if _K._NOSILU else AF.Gelu)),
                 reads=[B("hpre")], writes=[B("hpre")], tag=pfx + f"gel{tt}")
            s.op("scalar", lambda e: e.activation(
                out=xq2f_sb[:, 0:NF], in_=hpre_sb, func=AF.Square, accum_out=st_sb[:, 1:2]),
                reads=[B("hpre")], writes=[B("st1"), B("xq2f")], tag=pfx + f"sq{tt}")
            s.op("vector", lambda e: e.reduce_max(
                out=st_sb[:, 6:7], in_=hpre_sb, axis=AX.X, apply_absolute_value=True),
                reads=[B("hpre")], writes=[B("st6")], tag=pfx + f"am{tt}")
            s.op("vector", lambda e: e.tensor_scalar_mul(
                out=st_sb[:, 1:2], in0=st_sb[:, 1:2], scalar1=1.0 / NF),
                reads=[B("st1")], writes=[B("st1")], tag=pfx + f"ms{tt}")
            s.op("scalar", lambda e: e.activation(
                out=st_sb[:, 5:6], in_=st_sb[:, 1:2], func=AF.Ln, bias=epsc_sb[:, 0:1]),
                reads=[B("st1"), B("epsc")], writes=[B("st5")], tag=pfx + f"r2a{tt}")
            s.op("scalar", lambda e: e.activation(
                out=st_sb[:, 5:6], in_=st_sb[:, 5:6], func=AF.Exp, scale=-0.5),
                reads=[B("st5")], writes=[B("st5")], tag=pfx + f"r2b{tt}")
            s.op("vector", lambda e: e.tensor_tensor(
                out=st_sb[:, 6:7], in0=st_sb[:, 6:7], in1=st_sb[:, 5:6], op=ALU.mult),
                reads=[B("st6"), B("st5")], writes=[B("st6")], tag=pfx + f"am2{tt}")
            s.op("vector", lambda e: e.tensor_scalar_max(
                out=st_sb[:, 6:7], in0=st_sb[:, 6:7], scalar1=1e-5),
                reads=[B("st6")], writes=[B("st6")], tag=pfx + f"am3{tt}")
            s.op("vector", lambda e: e.reciprocal(out=st_sb[:, 7:8], in_=st_sb[:, 6:7]),
                 reads=[B("st6")], writes=[B("st7")], tag=pfx + f"rs{tt}")
            s.op("vector", lambda e: e.tensor_scalar_mul(
                out=st_sb[:, 7:8], in0=st_sb[:, 7:8], scalar1=127.0),
                reads=[B("st7")], writes=[B("st7")], tag=pfx + f"rs2{tt}")
            s.op("vector", lambda e: e.tensor_tensor(
                out=st_sb[:, 8:9], in0=st_sb[:, 7:8], in1=st_sb[:, 5:6], op=ALU.mult),
                reads=[B("st7"), B("st5")], writes=[B("st8")], tag=pfx + f"rs3{tt}")
            s.op("vector", lambda e: e.tensor_scalar_mul(
                out=st_sb[:, 9:10], in0=st_sb[:, 6:7], scalar1=1.0 / 127.0),
                reads=[B("st6")], writes=[B("st9")], tag=pfx + f"inv{tt}")
            for fh in range(4):
                s.op("scalar", lambda e, fh=fh: e.activation(
                    out=q2i_sb, in_=hpre_sb[:, fh * 1024:(fh + 1) * 1024],
                    func=AF.Copy, scale=st_sb[:, 8:9]),
                    reads=[B("hpre"), B("st8")], writes=[B("q2i")], tag=pfx + f"q2i{tt}{fh}")
                s.op("vector", lambda e, fh=fh: e.tensor_scalar_mul(
                    out=q2o_sb, in0=q2i_sb, scalar1=st_sb[:, 9:10]),
                    reads=[B("q2i"), B("st9")], writes=[B("q2o")], tag=pfx + f"q2o{tt}{fh}")
                s.op("gpsimd", lambda e, tt=tt, si=si, fh=fh: e.dma_start(
                    out=xq2scr[si][tt * 128:(tt + 1) * 128, fh * 1024:(fh + 1) * 1024],
                    in_=q2o_sb),
                    reads=[B("q2o")], writes=[B(f"xq2scr{si}")], dma=True, skey=f"q2w{si}", tag=pfx + f"q2w{tt}{fh}")
            if istap and tt == 0:
                tap("h", hpre_sb, [128, NF], BF16, ["hpre"])
                tap("q2o", q2o_sb, [128, 1024], BF16, ["q2o"])
        for fc in range(32):
            s.op("sync", lambda e, fc=fc, si=si: e.dma_start(
                out=xq2f_sb[:, fc * SEG:(fc + 1) * SEG],
                in_=xq2scr[si][:, fc * 128:(fc + 1) * 128], transpose=True),
                reads=[B(f"xq2scr{si}")], writes=[B("xq2f")], dma=True, skey="xq2f",
                tag=pfx + f"q2f{fc}")
        for tt in range(NPAIR):
            for half in range(2):
                for k in range(32):
                    s.op("tensor", lambda e, tt=tt, half=half, k=k: e.matmul(
                        PSY[:, half * 512:(half + 1) * 512],
                        xq2f_sb[:, k * SEG + tt * 128:k * SEG + (tt + 1) * 128],
                        wq2t_sb.rearrange("p (k c) -> p k c", k=32)[:, k, half * 512:(half + 1) * 512],
                        start=(k == 0), stop=(k == 31), skip_group_check=True),
                        reads=[B("xq2f"), B("wq2t")], writes=[B("psy")],
                        tag=pfx + f"m2_{tt}{half}{k}")
            s.op("vector", lambda e, tt=tt: e.tensor_tensor(
                out=rb2_sb, in0=xlnt_sb[:, tt * DM:(tt + 1) * DM], in1=b2bc_sb, op=ALU.add),
                reads=[B("xlnt"), B("b2bc")], writes=[B("rb2")], tag=pfx + f"rb{tt}")
            s.op("vector", lambda e: e.scalar_tensor_tensor(
                out=pln2_sb, in0=PSY, scalar=sw_mean["w2"][:, 0:1], in1=rb2_sb,
                op0=ALU.mult, op1=ALU.add),
                reads=[B("psy"), B("w2mean"), B("rb2")], writes=[B("pln2")], tag=pfx + f"e2{tt}")
            s.op("vector", lambda e: e.tensor_reduce(
                out=st_sb[:, 0:1], in_=pln2_sb, axis=AX.X, op=ALU.add),
                reads=[B("pln2")], writes=[B("st0")], tag=pfx + f"mu{tt}")
            s.op("vector", lambda e: e.tensor_scalar_mul(
                out=st_sb[:, 0:1], in0=st_sb[:, 0:1], scalar1=1.0 / DM),
                reads=[B("st0")], writes=[B("st0")], tag=pfx + f"mu2{tt}")
            s.op("scalar", lambda e: e.activation(
                out=rb2_sb, in_=pln2_sb, func=AF.Square, accum_out=st_sb[:, 1:2]),
                reads=[B("pln2")], writes=[B("st1"), B("rb2")], tag=pfx + f"sq2{tt}")
            s.op("vector", lambda e: e.tensor_scalar_mul(
                out=st_sb[:, 1:2], in0=st_sb[:, 1:2], scalar1=1.0 / DM),
                reads=[B("st1")], writes=[B("st1")], tag=pfx + f"ex2{tt}")
            s.op("vector", lambda e: e.tensor_tensor(
                out=st_sb[:, 2:3], in0=st_sb[:, 0:1], in1=st_sb[:, 0:1], op=ALU.mult),
                reads=[B("st0")], writes=[B("st2")], tag=pfx + f"m22{tt}")
            s.op("vector", lambda e: e.tensor_sub(
                out=st_sb[:, 2:3], in0=st_sb[:, 1:2], in1=st_sb[:, 2:3]),
                reads=[B("st1"), B("st2")], writes=[B("st2")], tag=pfx + f"var{tt}")
            s.op("scalar", lambda e: e.activation(
                out=st_sb[:, 3:4], in_=st_sb[:, 2:3], func=AF.Ln, bias=epsc_sb[:, 0:1]),
                reads=[B("st2"), B("epsc")], writes=[B("st3")], tag=pfx + f"ln2{tt}")
            s.op("scalar", lambda e: e.activation(
                out=st_sb[:, 3:4], in_=st_sb[:, 3:4], func=AF.Exp, scale=-0.5),
                reads=[B("st3")], writes=[B("st3")], tag=pfx + f"rstd{tt}")
            s.op("vector", lambda e: e.tensor_tensor(
                out=st_sb[:, 4:5], in0=st_sb[:, 0:1], in1=st_sb[:, 3:4], op=ALU.mult),
                reads=[B("st0"), B("st3")], writes=[B("st4")], tag=pfx + f"mr{tt}")
            s.op("vector", lambda e: e.tensor_scalar_mul(
                out=st_sb[:, 4:5], in0=st_sb[:, 4:5], scalar1=-1.0),
                reads=[B("st4")], writes=[B("st4")], tag=pfx + f"nmr{tt}")
            s.op("vector", lambda e: e.tensor_scalar(
                out=outr_sb, in0=pln2_sb, scalar1=st_sb[:, 3:4], scalar2=st_sb[:, 4:5],
                op0=ALU.mult, op1=ALU.add),
                reads=[B("pln2"), B("st3"), B("st4")], writes=[B("outr")], tag=pfx + f"lna{tt}")
            s.op("gpsimd", lambda e, tt=tt: e.dma_start(
                out=out_d[t0 + tt * 128:t0 + (tt + 1) * 128, :], in_=outr_sb),
                reads=[B("outr")], writes=[B("outd")], dma=True, skey="ow", tag=pfx + f"ow{tt}")

    for seg in range(NSEG):
        ffn_segment(seg)
        if _K._BARRIER_SEGS:
            s.barrier()
    s.op("gpsimd", lambda e: e.nop(), extra_waits=[B("outd").writer], tag="final")

    s.emit(nc, block, sems)
    es.close()
    return tap_d


def _host_inputs(inputs):
    x = np.asarray(inputs["L0_attn_x"])
    win = np.ascontiguousarray(np.asarray(inputs["L0_attn_in_proj_w"])[:, 2048:4384])
    convw = np.asarray(inputs["L0_attn_conv_w"])[:, 0, :]
    convw_r = np.zeros((128, 72), np.float32)
    for cc in range(CONVC):
        for k in range(4):
            convw_r[:, cc * 4 + k] = convw[k, cc * 128:(cc + 1) * 128]
    convb = np.asarray(inputs["L0_attn_conv_b"]).reshape(CONVC, 128).T.copy()
    dtb = np.asarray(inputs["L0_attn_dt_bias"])
    alog = np.asarray(inputs["L0_attn_A_log"])
    Dv = np.asarray(inputs["L0_attn_D"])
    drep = np.repeat(Dv, HD).reshape(16, 128).T.copy()
    wout = np.asarray(inputs["L0_attn_out_proj_w"])
    w1 = np.asarray(inputs["L0_ffn_w1"])
    b1 = np.asarray(inputs["L0_ffn_b1"]).reshape(1, NF)
    w2 = np.asarray(inputs["L0_ffn_w2"])
    b2 = np.asarray(inputs["L0_ffn_b2"]).reshape(1, DM)

    ident = np.eye(128, dtype=np.float32).astype(BF)
    tri64 = (np.arange(CH)[None, :] >= np.arange(CH)[:, None]).astype(np.float32)
    tri = np.concatenate([tri64, tri64], axis=0).astype(BF)
    ind2 = np.zeros((2, 128), np.float32)
    ind2[0, 0:64] = 1.0; ind2[1, 64:128] = 1.0
    ind2 = ind2.astype(BF)
    indn = np.zeros((16, 1024), np.float32)
    for h in range(16):
        indn[h, h * 64:(h + 1) * 64] = -1.0
    indn = indn.astype(BF)
    maska = np.ones((16, SEG), np.float32)
    maska[:, ::CH] = 0.0
    onesb = np.ones((1, 128), np.float32).astype(BF)

    common = dict(
        w_in=np.ascontiguousarray(win, np.float32), convw=convw_r,
        convb=np.ascontiguousarray(convb, np.float32),
        dtb0=dtb[0:16].reshape(16, 1).astype(np.float32),
        dtb1=dtb[16:32].reshape(16, 1).astype(np.float32),
        alog0=alog[0:16].reshape(16, 1).astype(np.float32),
        alog1=alog[16:32].reshape(16, 1).astype(np.float32),
        drep=np.ascontiguousarray(drep, np.float32),
        w_out=np.ascontiguousarray(wout, np.float32),
        w1=np.ascontiguousarray(w1, np.float32), b1r=np.ascontiguousarray(b1, np.float32),
        w2=np.ascontiguousarray(w2, np.float32), b2r=np.ascontiguousarray(b2, np.float32),
        identb=ident, tri=tri, ind2=ind2, indn16=indn,
        maska=maska, onesb=onesb, epsc=np.full((128, 1), EPS, np.float32), tinyc=np.full((16, 1), 1e-30, np.float32),
    )
    in_maps = []
    for c in range(8):
        b, r = c // 2, c % 2
        m = dict(common)
        m["xpre"] = np.ascontiguousarray(x[b, 0:TP], np.float32)
        m["xown"] = np.ascontiguousarray(x[b, r * TP:(r + 1) * TP], np.float32)
        m["gate"] = np.full((128, 1), float(r), np.float32)
        in_maps.append(m)
    return in_maps


def kernel(**inputs):
    global _LAST_RES
    in_maps = _host_inputs(inputs)
    nc = bass.Bass("TRN2", target_bir_lowering=False)
    build_program(nc, taps=_TAPS)
    res = run_bass_kernel_spmd(nc, in_maps, core_ids=list(range(8)))
    _LAST_RES = res
    out = np.zeros((4, 4096, DM), np.float32)
    for c in range(8):
        b, r = c // 2, c % 2
        out[b, r * TP:(r + 1) * TP] = res.results[c]["out"]
    return out


# version 04
# speedup vs baseline: 1.1430x; 1.0283x over previous
import contextlib
import numpy as np
import ml_dtypes
import concourse.bass as bass
import concourse.mybir as mybir
from concourse.bass_utils import run_bass_kernel_spmd

F32 = mybir.dt.float32
BF16 = mybir.dt.bfloat16
I32 = mybir.dt.int32
AF = mybir.ActivationFunctionType
ALU = mybir.AluOpType
AX = mybir.AxisListType
BF = ml_dtypes.bfloat16

DM = 1024
DIN = 2048
NH = 32
HD = 64
NF = 4096
CH = 64
TP = 2048
TO = 2048
SEG = 256
NPAIR = SEG // 128
NCHUNK = SEG // CH
NSEG = TP // SEG
CONVC = 18
EPS = 1e-5

ENGS = ("tensor", "vector", "scalar", "gpsimd", "sync")

_TAPS = ()
_LAST_RES = None
_BARRIER_SEGS = False
_BMODE = 0
_ABLATE = set()
_NOSILU = False
_DMASER = 0


class Op:
    __slots__ = ("eng", "fn", "waits", "waw_waits", "hard_waits", "marked",
                 "idx", "inc_amt", "skey", "tag")
    def __init__(self, eng, fn, idx, inc_amt, tag):
        self.eng = eng; self.fn = fn; self.idx = idx
        self.waits = []; self.waw_waits = []; self.hard_waits = []
        self.marked = False; self.skey = eng
        self.inc_amt = inc_amt; self.tag = tag


class Buf:
    __slots__ = ("writer", "readers", "name")
    def __init__(self, name=""):
        self.writer = None; self.readers = []; self.name = name


class Sched:
    def __init__(self):
        self.ops = []
        self.per_eng = {e: [] for e in ENGS}
        self.last_dma = {}

    def op(self, eng, fn, reads=(), writes=(), extra_waits=(), dma=False, sz=None,
           skey=None, chain=False, tag=""):
        import kernel as _K
        o = Op(eng, fn, len(self.per_eng[eng]), 16 if dma else 1, tag)
        if dma:
            o.skey = skey or eng
            full = _K._DMASER & (1 if eng == "gpsimd" else 2 if eng == "sync" else 0)
            prev = self.last_dma.get(o.skey)
            if (chain or full) and prev is not None:
                o.hard_waits.append(prev)
                prev.marked = True
            self.last_dma[o.skey] = o
        seen = set()
        def add_wait(w, lst=None):
            if w is not None and id(w) not in seen and w is not o:
                seen.add(id(w)); (o.waits if lst is None else lst).append(w)
        for b in reads:
            add_wait(b.writer)
        for b in writes:
            add_wait(b.writer, o.waw_waits)
            for r in b.readers:
                add_wait(r)
        for w in extra_waits:
            add_wait(w)
        for b in reads:
            b.readers.append(o)
        for b in writes:
            b.writer = o; b.readers = []
        self.ops.append(o)
        self.per_eng[eng].append(o)
        return o

    def barrier(self):
        lasts = [self.per_eng[e][-1] for e in ENGS if self.per_eng[e]]
        for e in ENGS:
            if self.per_eng[e]:
                self.op(e, lambda eng: eng.nop(), extra_waits=lasts, tag="barrier_" + e)

    def emit(self, nc, block, sems):
        for o in self.ops:
            if o.inc_amt == 16:
                o.marked = True
            for w in o.waits:
                w.marked = True
            for w in o.waw_waits:
                w.marked = True
            for w in o.hard_waits:
                w.marked = True
        val = {}
        cnt = {}
        for o in self.ops:
            if o.marked:
                cnt[o.skey] = cnt.get(o.skey, 0) + o.inc_amt
            val[id(o)] = (o.skey, cnt.get(o.skey, 0))
        def emit_engine(e, eng):
            highest = {}
            for o in self.per_eng[e]:
                needed = {}
                for w in o.waits:
                    we, wv = val[id(w)]
                    if w.eng == e == "tensor" and w.inc_amt == 1 and o.inc_amt == 1:
                        continue
                    if wv > needed.get(we, 0):
                        needed[we] = wv
                for w in o.waw_waits:
                    we, wv = val[id(w)]
                    if w.eng == e == "tensor" and w.inc_amt == 1 and o.inc_amt == 1:
                        continue
                    if we == o.skey and w.inc_amt == 16 and o.inc_amt == 16:
                        continue
                    if wv > needed.get(we, 0):
                        needed[we] = wv
                for w in o.hard_waits:
                    we, wv = val[id(w)]
                    if wv > needed.get(we, 0):
                        needed[we] = wv
                for we, wv in sorted(needed.items()):
                    if wv > highest.get(we, 0):
                        eng.wait_ge(sems[we], wv)
                        highest[we] = wv
                instr = o.fn(eng)
                if o.marked:
                    if instr is None:
                        raise RuntimeError(f"marked op {o.tag} returned None")
                    instr.then_inc(sems[o.skey], o.inc_amt)
        for e in ENGS:
            if self.per_eng[e]:
                getattr(block, e)(lambda eng, e=e: emit_engine(e, eng))


def build_program(nc, taps=()):
    import kernel as _K
    es = contextlib.ExitStack()
    tapset = set(taps)

    def din(name, shape, dtype=F32):
        return nc.dram_tensor(name, list(shape), dtype, kind="ExternalInput").ap()

    xpre_d = din("xpre", [TP, DM])
    xown_d = din("xown", [TO, DM])
    win_d = din("w_in", [DM, 2336])
    convw_d = din("convw", [128, 72])
    convb_d = din("convb", [128, CONVC])
    dtb_d = [din("dtb0", [16, 1]), din("dtb1", [16, 1])]
    alog_d = [din("alog0", [16, 1]), din("alog1", [16, 1])]
    drep_d = din("drep", [128, 16])
    wout_d = din("w_out", [DIN, DM])
    w1_d = din("w1", [NF, DM])
    b1_d = din("b1r", [1, NF])
    w2_d = din("w2", [DM, NF])
    b2_d = din("b2r", [1, DM])
    gate_d = din("gate", [128, 1])
    identb_d = din("identb", [128, 128], BF16)
    tri_d = din("tri", [128, CH], BF16)
    ind2_d = din("ind2", [2, 128], BF16)
    indn_d = din("indn16", [16, 1024], BF16)
    maska_d = din("maska", [16, SEG])
    onesb_d = din("onesb", [1, 128], BF16)
    epsc_d = din("epsc", [128, 1])
    tinyc_d = din("tinyc", [16, 1])

    out_d = nc.dram_tensor("out", [TO, DM], F32, kind="ExternalOutput").ap()
    tap_d = {}

    def scr(name, shape, dtype):
        return nc.dram_tensor(name, list(shape), dtype, kind="Internal").ap()

    xscr_d = [scr("xpscr", [TP, DM], BF16), scr("xoscr", [TO, DM], BF16)]
    xhscr = [scr(f"xhscr{i}", [DIN, SEG], BF16) for i in range(2)]
    yscr = [scr(f"yscr{i}", [SEG, DIN], BF16) for i in range(2)]
    xlnscr = scr("xlnscr", [TO, DM], BF16)
    xq1scr = scr("xq1scr", [TO, DM], BF16)
    wq1scr = scr("wq1scr", [NF, DM], BF16)
    wq2scr = scr("wq2scr", [DM, NF], BF16)
    xq2scr = [scr(f"xq2scr{i}", [SEG, NF], BF16) for i in range(2)]
    accscr = scr("accscr", [64, SEG], BF16)
    ecscr = scr("ecscr", [32, NCHUNK], BF16)

    def TT_(stack, name, shape, dtype):
        return stack.enter_context(nc.sbuf_tensor(name, list(shape), dtype))[:]
    T = lambda n, s, d: TT_(es, n, s, d)

    ident_sb = T("ident_sb", [128, 128], BF16)
    ones128_sb = T("ones128_sb", [128, 128], BF16)
    tri_sb = T("tri_sb", [128, CH], BF16)
    ind2_sb = T("ind2_sb", [2, 128], BF16)
    indn_sb = T("indn_sb", [16, 1024], BF16)
    maska_sb = T("maska_sb", [16, SEG], F32)
    onesb_sb = T("onesb_sb", [1, 128], BF16)
    convb_sb = T("convb_sb", [128, CONVC], F32)
    drep_sb = T("drep_sb", [128, 16], F32)
    gate_sb = T("gate_sb", [128, 1], F32)
    epsc_sb = T("epsc_sb", [128, 1], F32)
    tinyc_sb = T("tinyc_sb", [16, 1], F32)
    convw_sb = T("convw_sb", [128, 72], F32)
    acol_sb = [T(f"acol{g}", [16, 1], F32) for g in range(2)]
    dtb_sb = [T(f"dtbs{g}", [16, 1], F32) for g in range(2)]
    S_sb = [T(f"S{g}", [128, 1024], BF16) for g in range(2)]
    halo_sb = T("halo_sb", [128, CONVC * 3], BF16)
    sw_scale = {k: T(k + "_scale", [128, 1], F32) for k in ("w1", "w2")}
    sw_mean = {k: T(k + "_mean", [128, 1], F32) for k in ("w1", "w2")}
    sw_acc = {k: T(k + "_acc", [128, 32], F32) for k in ("w1", "w2")}
    sw_accb = T("sw_accb", [128, 32], BF16)
    b1row_sb = T("b1row_sb", [1, NF], BF16)
    b2row_sb = T("b2row_sb", [1, DM], BF16)
    st_sb = T("st_sb", [128, 16], F32)
    pln1_sb = T("pln1_sb", [128, DM], F32)
    xln1_sb = T("xln1_sb", [128, DM], F32)
    xln1b_sb = T("xln1b_sb", [128, DM], BF16)

    PSY = es.enter_context(nc.psum_tensor("psy", [128, 1024], F32))[:]
    PSO = es.enter_context(nc.psum_tensor("pso", [128, 1024], F32))[:]
    PST = es.enter_context(nc.psum_tensor("pst", [128, 1024], F32))[:]
    PG = [es.enter_context(nc.psum_tensor(f"pg{i}", [128, 512], F32))[:] for i in range(2)]
    PT = PSY.bitcast(BF16)

    SKEYS = ["const", "wld", "slab", "xfm0", "xfm1", "xT", "yf", "kt0", "kt1", "ec",
             "ecw", "xhw", "yw", "xscrw", "xrl", "ktw", "lnw", "q1w", "q2w0",
             "q2w1", "ow", "wqs_w1", "wqs_w2", "xq1f", "xq2f", "wq1t", "wq2t",
             "xlnt"]
    sems = {e: es.enter_context(nc.semaphore("sem_" + e))
            for e in list(ENGS) + SKEYS}
    block = es.enter_context(nc.Block())

    s = Sched()
    bufs = {}
    def B(name):
        if name not in bufs:
            bufs[name] = Buf(name)
        return bufs[name]

    def tap(name, src_ap, shape, dtype, rbufs):
        if name in tapset:
            tap_d[name] = nc.dram_tensor("tap_" + name, list(shape), dtype,
                                         kind="ExternalOutput").ap()
            s.op("gpsimd", lambda e: e.dma_start(out=tap_d[name], in_=src_ap),
                 reads=[B(r) for r in rbufs], dma=True, tag="tap_" + name)

    aes = contextlib.ExitStack()
    A = lambda n, s_, d: TT_(aes, n, s_, d)
    win_sb = A("win_sb", [128, 8 * 2336], BF16)
    wout_sb = A("wout_sb", [128, 16 * DM], BF16)
    diagw_sb = A("diagw_sb", [128, 72 * 128], BF16)
    xtm_sb = A("xtm_sb", [128, DM], BF16)
    xrl_sb = A("xrl_sb", [128, DM], BF16)
    xfm_sb = [A(f"xfm_sb{i}", [128, 8 * SEG], BF16) for i in range(2)]
    xbc_sb = A("xbc_sb", [128, 2 * (SEG + 3)], BF16)
    xhs_sb = A("xhs_sb", [128, 16 * SEG], BF16)
    bfm_sb = A("bfm_sb", [128, SEG], BF16)
    cfm_sb = A("cfm_sb", [128, SEG], BF16)
    xT_sb = A("xT_sb", [128, NPAIR * DIN], BF16)
    xddt_sb = A("xddt_sb", [128, NPAIR * DIN], BF16)
    bt_sb = A("bt_sb", [128, NPAIR * 128], BF16)
    dtT_sb = [A(f"dtT{g}", [16, SEG], F32) for g in range(2)]
    accp_sb = [A(f"accp{g}", [16, SEG], F32) for g in range(2)]
    acchi_sb = [A(f"acchi{g}", [16, SEG], BF16) for g in range(2)]
    acclo_sb = [A(f"acclo{g}", [16, SEG], BF16) for g in range(2)]
    ea_sb = [A(f"ea{g}", [16, SEG], BF16) for g in range(2)]
    ddt_sb = [A(f"ddt{g}", [16, SEG], BF16) for g in range(2)]
    dtb16_sb = [A(f"dtb16_{g}", [16, SEG], BF16) for g in range(2)]
    sp1_sb = A("sp1_sb", [16, SEG], F32)
    sp2_sb = A("sp2_sb", [16, SEG], F32)
    kt_sb = [A(f"kt{g}", [2, 2 * 1024], BF16) for g in range(2)]
    eat_sb = A("eat_sb", [128, NPAIR * NH], BF16)
    dttm_sb = A("dttm_sb", [128, NPAIR * NH], BF16)
    ddttm_sb = A("ddttm_sb", [128, NPAIR * NH], BF16)
    eb_sb = A("eb_sb", [128, NCHUNK * NH], BF16)
    ecrow_sb = A("ecrow_sb", [1, NCHUNK * NH], BF16)
    ecc_sb = A("ecc_sb", [16, 2 * NCHUNK], BF16)
    lt_sb = A("lt_sb", [128, 1024], BF16)
    m2t_sb = A("m2t_sb", [128, 1024], BF16)
    bcm_sb = A("bcm_sb", [128, CH], BF16)
    ytm_sb = A("ytm_sb", [128, NPAIR * DIN], BF16)
    yfm_sb = A("yfm_sb", [128, 16 * SEG], BF16)
    wslab_sb = [A("wslab0", [128, DM], F32)] * 2
    WSLN = ["wsl0", "wsl0"]
    wqb_sb = A("wqb_sb", [128, DM], BF16)
    wint_sb = A("wint_sb", [128, DM], I32)
    q1o_sb = wqb_sb

    def ld(dst, src, bufname, q="sync"):
        return s.op(q, lambda e: e.dma_start(out=dst, in_=src),
                    writes=[B(bufname)], dma=True, skey="const", chain=True,
                    tag="ld_" + bufname)

    ld(ident_sb, identb_d, "ident")
    ld(tri_sb, tri_d, "tri")
    ld(ind2_sb, ind2_d, "ind2")
    ld(indn_sb, indn_d, "indn")
    ld(maska_sb, maska_d, "maska")
    ld(onesb_sb, onesb_d, "onesb")
    ld(convb_sb, convb_d, "convb")
    ld(drep_sb, drep_d, "drep")
    ld(gate_sb, gate_d, "gate")
    ld(epsc_sb, epsc_d, "epsc")
    ld(tinyc_sb, tinyc_d, "tinyc")
    ld(convw_sb, convw_d, "convw")
    ld(b1row_sb, b1_d, "b1row", q="gpsimd")
    ld(b2row_sb, b2_d, "b2row", q="gpsimd")
    s.op("gpsimd", lambda e: e.dma_start(
        out=win_sb.rearrange("p (k c) -> p k c", k=8),
        in_=win_d.rearrange("(k p) c -> p k c", p=128)),
        writes=[B("win")], dma=True, skey="wld", chain=True, tag="ld_win")
    s.op("gpsimd", lambda e: e.dma_start(
        out=wout_sb.rearrange("p (k c) -> p k c", k=16),
        in_=wout_d.rearrange("(k p) c -> p k c", p=128)),
        writes=[B("wout")], dma=True, skey="wld", chain=True, tag="ld_wout")
    s.op("vector", lambda e: e.memset(ones128_sb, 1.0), writes=[B("ones128")], tag="ones128")
    for wf in ((AF.Exp,) if _K._NOSILU else (AF.Exp, AF.Silu, AF.Gelu, AF.Exp)):
        s.op("scalar", lambda e, wf=wf: e.activation(out=sp1_sb[0:16, 0:16],
                                                     in_=maska_sb[0:16, 0:16], func=wf),
             reads=[B("maska")], writes=[B("sp1")], tag=f"warm_{wf.name}")
    for j in range(72):
        s.op("vector", lambda e, j=j: e.tensor_scalar_mul(
            out=diagw_sb.rearrange("p (j c) -> p j c", j=72)[:, j, :],
            in0=ident_sb, scalar1=convw_sb[:, j:j + 1]),
            reads=[B("ident"), B("convw")], writes=[B("diagw")], tag=f"dw{j}")
    s.op("vector", lambda e: e.memset(halo_sb, 0.0), writes=[B("halo")], tag="halo0")
    for g in range(2):
        s.op("vector", lambda e, g=g: e.memset(S_sb[g], 0.0), writes=[B(f"S{g}")], tag="S0")
    for g in range(2):
        ld(acol_sb[g], alog_d[g], f"acol{g}")
        ld(dtb_sb[g], dtb_d[g], f"dtb{g}")

    def bcast_row(row_ap, rowbuf, n, dst, dstbuf):
        nb = (n + 511) // 512
        for i in range(nb):
            w = min(512, n - i * 512)
            s.op("tensor", lambda e, i=i, w=w: e.matmul(
                PG[0][:, 0:w], onesb_sb[0:1, :], row_ap[0:1, i * 512:i * 512 + w],
                start=True, stop=True),
                reads=[B("onesb"), B(rowbuf)], writes=[B("pg0")], tag="bcmm")
            s.op("vector", lambda e, i=i, w=w: e.tensor_copy(
                out=dst[:, i * 512:i * 512 + w], in_=PG[0][:, 0:w]),
                reads=[B("pg0")], writes=[B(dstbuf)], tag="bcev")

    def wq_tiles(rows, cols):
        return [(it, jc) for it in range(rows // 128) for jc in range(cols // DM)]

    def wq_p1(key, wd, tiles, rng):
        for idx in rng:
            it, jc = tiles[idx]
            si = idx % 2
            sl = wslab_sb[si]
            s.op("sync" if si == 0 else "gpsimd", lambda e, it=it, jc=jc, sl=sl: e.dma_start(
                out=sl, in_=wd[it * 128:(it + 1) * 128, jc * DM:(jc + 1) * DM]),
                writes=[B(WSLN[si])], dma=True, skey="slab", tag=f"{key}l1_{idx}")
            s.op("scalar", lambda e, idx=idx, sl=sl: e.activation(
                out=sl, in_=sl, func=AF.Abs, accum_out=sw_acc[key][:, idx:idx + 1]),
                reads=[B(WSLN[si])], writes=[B(WSLN[si]), B(key + "acc")], tag=f"{key}a{idx}")

    def wq_mean(key, rows, cols, ntot):
        s.op("vector", lambda e, ntot=ntot: e.tensor_copy(
            out=sw_accb[:, 0:ntot], in_=sw_acc[key][:, 0:ntot]),
            reads=[B(key + "acc")], writes=[B("accb")], tag=f"{key}cb")
        s.op("tensor", lambda e, ntot=ntot: e.matmul(
            PG[0][:, 0:ntot], ones128_sb, sw_accb[:, 0:ntot], start=True, stop=True),
            reads=[B("ones128"), B("accb")], writes=[B("pg0")], tag=f"{key}rm")
        s.op("vector", lambda e, ntot=ntot: e.tensor_reduce(
            out=sw_mean[key], in_=PG[0][:, 0:ntot], axis=AX.X, op=ALU.add),
            reads=[B("pg0")], writes=[B(key + "mean")], tag=f"{key}rd")
        N = float(rows * cols)
        s.op("vector", lambda e: e.tensor_scalar_mul(out=sw_mean[key], in0=sw_mean[key],
                                                     scalar1=1.0 / N),
             reads=[B(key + "mean")], writes=[B(key + "mean")], tag=f"{key}mn")
        s.op("vector", lambda e: e.tensor_scalar_max(out=sw_scale[key], in0=sw_mean[key],
                                                     scalar1=1e-5),
             reads=[B(key + "mean")], writes=[B(key + "sc")], tag=f"{key}mx")
        s.op("vector", lambda e: e.reciprocal(out=sw_scale[key], in_=sw_scale[key]),
             reads=[B(key + "sc")], writes=[B(key + "sc")], tag=f"{key}rc")

    def wq_p2(key, wd, scrd, scrbuf, tiles, rng):
        for idx in rng:
            it, jc = tiles[idx]
            si = idx % 2
            sl = wslab_sb[si]
            s.op("sync" if si == 0 else "gpsimd", lambda e, it=it, jc=jc, sl=sl: e.dma_start(
                out=sl, in_=wd[it * 128:(it + 1) * 128, jc * DM:(jc + 1) * DM]),
                writes=[B(WSLN[si])], dma=True, skey="slab", tag=f"{key}l2_{idx}")
            s.op("scalar", lambda e, sl=sl: e.activation(
                out=wint_sb, in_=sl, func=AF.Copy, scale=sw_scale[key][:, 0:1]),
                reads=[B(WSLN[si]), B(key + "sc")], writes=[B("wint")], tag=f"{key}r{idx}")
            s.op("vector", lambda e: e.tensor_scalar(
                out=wqb_sb, in0=wint_sb, scalar1=1.0, scalar2=-1.0,
                op0=ALU.min, op1=ALU.max),
                reads=[B("wint")], writes=[B("wqb")], tag=f"{key}c{idx}")
            s.op("gpsimd", lambda e, it=it, jc=jc: e.dma_start(
                out=scrd[it * 128:(it + 1) * 128, jc * DM:(jc + 1) * DM], in_=wqb_sb),
                reads=[B("wqb")], writes=[B(scrbuf)], dma=True, skey="wqs_" + key, tag=f"{key}s{idx}")

    def attn_segment(seg, phase):
        full = (phase == 2)
        t0 = seg * SEG
        xd = xown_d if full else xpre_d
        xscrd = xscr_d[1] if full else xscr_d[0]
        si = seg % 2
        pfx = f"p{phase}s{seg}_"
        istap = full and seg == 0

        for pp in range(NPAIR):
            s.op("gpsimd", lambda e, pp=pp: e.dma_start(
                out=xtm_sb,
                in_=xd[t0 + pp * 128:t0 + (pp + 1) * 128, :]),
                writes=[B("xtm")], dma=True, tag=pfx + f"xtm{pp}")
            s.op("gpsimd", lambda e, pp=pp: e.dma_start(
                out=xscrd[t0 + pp * 128:t0 + (pp + 1) * 128, :], in_=xtm_sb),
                reads=[B("xtm")], writes=[B("xscr")], dma=True, skey="xscrw", tag=pfx + f"xscrw{pp}")
        fs = seg % 2
        xfm = xfm_sb[fs]
        for k in range(8):
            s.op("sync", lambda e, k=k: e.dma_start(
                out=xfm[:, k * SEG:(k + 1) * SEG],
                in_=xscrd[t0:t0 + SEG, k * 128:(k + 1) * 128], transpose=True),
                reads=[B("xscr")], writes=[B(f"xfm{fs}")], dma=True, skey=f"xfm{fs}",
                tag=pfx + f"xfm{k}")

        mtiles = list(range(16)) + [16] + ([17] if (full or seg == NSEG - 1) else [])
        xbcv = xbc_sb.rearrange("p (c q) -> p c q", c=2)
        halov = halo_sb.rearrange("p (c q) -> p c q", q=3)
        for cc in mtiles:
            halo_only = (cc == 17 and not full)
            pid = cc % 2
            for k in range(8):
                s.op("tensor", lambda e, cc=cc, k=k, pid=pid: e.matmul(
                    PG[pid][:, 0:SEG],
                    win_sb.rearrange("p (k c) -> p k c", k=8)[:, k, cc * 128:(cc + 1) * 128],
                    xfm[:, k * SEG:(k + 1) * SEG],
                    start=(k == 0), stop=(k == 7)),
                    reads=[B("win"), B(f"xfm{fs}")], writes=[B(f"pg{pid}")],
                    tag=pfx + f"ip{cc}_{k}")
            s.op("vector", lambda e, cc=cc, pid=pid: e.tensor_copy(
                out=xbcv[:, pid, 3:3 + SEG], in_=PG[pid][:, 0:SEG]),
                reads=[B(f"pg{pid}")], writes=[B(f"xbcr{pid}")], tag=pfx + f"ipe{cc}")
            if not halo_only:
                s.op("vector", lambda e, cc=cc, pid=pid: e.tensor_copy(
                    out=xbcv[:, pid, 0:3], in_=halov[:, cc, :]),
                    reads=[B("halo")], writes=[B(f"xbcr{pid}")], tag=pfx + f"hin{cc}")
            s.op("vector", lambda e, cc=cc, pid=pid: e.tensor_copy(
                out=halov[:, cc, :], in_=xbcv[:, pid, SEG:SEG + 3]),
                reads=[B(f"xbcr{pid}")], writes=[B("halo")], tag=pfx + f"hsv{cc}")
            if halo_only:
                continue
            for k in range(4):
                s.op("tensor", lambda e, cc=cc, k=k, pid=pid: e.matmul(
                    PG[pid][:, 0:SEG],
                    diagw_sb.rearrange("p (j c) -> p j c", j=72)[:, cc * 4 + k, :],
                    xbcv[:, pid, k:k + SEG],
                    start=(k == 0), stop=(k == 3)),
                    reads=[B("diagw"), B(f"xbcr{pid}")],
                    writes=[B(f"pg{pid}")], tag=pfx + f"cv{cc}_{k}")
            if cc < 16:
                dst = xhs_sb[:, cc * SEG:(cc + 1) * SEG]; dbuf = "xhs"
            elif cc == 16:
                dst = bfm_sb; dbuf = "bfm"
            else:
                dst = cfm_sb; dbuf = "cfm"
            s.op("scalar", lambda e, cc=cc, pid=pid, dst=dst: e.activation(
                out=dst, in_=PG[pid][:, 0:SEG], func=(AF.Relu if _K._NOSILU else AF.Silu), bias=convb_sb[:, cc:cc + 1]),
                reads=[B(f"pg{pid}"), B("convb")], writes=[B(dbuf)], tag=pfx + f"si{cc}")
        for g in range(2):
            for k in range(8):
                s.op("tensor", lambda e, g=g, k=k: e.matmul(
                    PG[0][0:16, 0:SEG],
                    win_sb.rearrange("p (k c) -> p k c", k=8)[:, k, 2304 + 16 * g:2304 + 16 * (g + 1)],
                    xfm[:, k * SEG:(k + 1) * SEG],
                    start=(k == 0), stop=(k == 7)),
                    reads=[B("win"), B(f"xfm{fs}")], writes=[B("pg0")],
                    tag=pfx + f"dt{g}_{k}")
            s.op("scalar", lambda e, g=g: e.activation(
                out=sp1_sb, in_=PG[0][0:16, 0:SEG], func=AF.Abs, bias=dtb_sb[g][:, 0:1]),
                reads=[B("pg0"), B(f"dtb{g}")], writes=[B("sp1")], tag=pfx + f"spa{g}")
            s.op("scalar", lambda e: e.activation(out=sp2_sb, in_=sp1_sb, func=AF.Exp, scale=-1.0),
                 reads=[B("sp1")], writes=[B("sp2")], tag=pfx + f"spe{g}")
            s.op("scalar", lambda e: e.activation(out=sp2_sb, in_=sp2_sb, func=AF.Ln, bias=1.0),
                 reads=[B("sp2")], writes=[B("sp2")], tag=pfx + f"spl{g}")
            s.op("scalar", lambda e, g=g: e.activation(
                out=sp1_sb, in_=PG[0][0:16, 0:SEG], func=AF.Relu, bias=dtb_sb[g][:, 0:1]),
                reads=[B("pg0"), B(f"dtb{g}")], writes=[B("sp1")], tag=pfx + f"spr{g}")
            s.op("vector", lambda e, g=g: e.tensor_add(out=dtT_sb[g], in0=sp1_sb, in1=sp2_sb),
                 reads=[B("sp1"), B("sp2")], writes=[B(f"dtT{g}")], tag=pfx + f"spadd{g}")
            s.op("scalar", lambda e, g=g: e.activation(out=sp1_sb, in_=dtT_sb[g], func=AF.Ln,
                                                        bias=tinyc_sb[:, 0:1]),
                reads=[B(f"dtT{g}"), B("tinyc")], writes=[B("sp1")], tag=pfx + f"adtln{g}")
            s.op("scalar", lambda e, g=g: e.activation(
                out=sp2_sb, in_=sp1_sb, func=AF.Exp, bias=acol_sb[g][:, 0:1]),
                reads=[B("sp1"), B(f"acol{g}")], writes=[B("sp2")], tag=pfx + f"adtex{g}")
            s.op("vector", lambda e, g=g: e.tensor_tensor_scan(
                out=accp_sb[g], data0=maska_sb, data1=sp2_sb,
                initial=0.0, op0=ALU.mult, op1=ALU.add),
                reads=[B("maska"), B("sp2")], writes=[B(f"accp{g}")], tag=pfx + f"scan{g}")
            s.op("vector", lambda e, g=g: e.tensor_scalar_mul(
                out=acchi_sb[g], in0=accp_sb[g], scalar1=-1.0),
                reads=[B(f"accp{g}")], writes=[B(f"acchi{g}")], tag=pfx + f"hi{g}")
            s.op("vector", lambda e, g=g: e.scalar_tensor_tensor(
                out=acclo_sb[g], in0=accp_sb[g], scalar=-1.0, in1=acchi_sb[g],
                op0=ALU.mult, op1=ALU.subtract),
                reads=[B(f"accp{g}"), B(f"acchi{g}")], writes=[B(f"acclo{g}")], tag=pfx + f"lo{g}")
            s.op("scalar", lambda e, g=g: e.activation(out=ea_sb[g], in_=accp_sb[g], func=AF.Exp,
                                                        scale=-1.0),
                 reads=[B(f"accp{g}")], writes=[B(f"ea{g}")], tag=pfx + f"ea{g}")
            s.op("vector", lambda e, g=g: e.tensor_tensor(
                out=sp1_sb.rearrange("h (c l) -> h c l", l=CH),
                in0=accp_sb[g].rearrange("h (c l) -> h c l", l=CH),
                in1=accp_sb[g].rearrange("h (c l) -> h c l", l=CH)[:, :, CH - 1:CH]
                    .broadcast_to([16, NCHUNK, CH]), op=ALU.subtract),
                reads=[B(f"accp{g}")], writes=[B("sp1")], tag=pfx + f"dda{g}")
            s.op("scalar", lambda e: e.activation(out=sp2_sb, in_=sp1_sb, func=AF.Exp),
                 reads=[B("sp1")], writes=[B("sp2")], tag=pfx + f"dde{g}")
            s.op("vector", lambda e, g=g: e.tensor_tensor(
                out=ddt_sb[g], in0=sp2_sb, in1=dtT_sb[g], op=ALU.mult),
                reads=[B("sp2"), B(f"dtT{g}")], writes=[B(f"ddt{g}")], tag=pfx + f"ddm{g}")
            if full:
                s.op("vector", lambda e, g=g: e.tensor_copy(out=dtb16_sb[g], in_=dtT_sb[g]),
                     reads=[B(f"dtT{g}")], writes=[B(f"dtb16{g}")], tag=pfx + f"d16{g}")

        if istap:
            tap("xfm", xfm_sb[fs], [128, 8 * SEG], BF16, [f"xfm{fs}"])
            tap("xhs", xhs_sb, [128, 16 * SEG], BF16, ["xhs"])
            tap("bfm", bfm_sb, [128, SEG], BF16, ["bfm"])
            tap("cfm", cfm_sb, [128, SEG], BF16, ["cfm"])
            tap("dtT0", dtT_sb[0], [16, SEG], F32, ["dtT0"])
            tap("accp0", accp_sb[0], [16, SEG], F32, ["accp0"])
            tap("ea0", ea_sb[0], [16, SEG], BF16, ["ea0"])
            tap("ddt0", ddt_sb[0], [16, SEG], BF16, ["ddt0"])
            tap("ddt1", ddt_sb[1], [16, SEG], BF16, ["ddt1"])
            tap("dtT1", dtT_sb[1], [16, SEG], F32, ["dtT1"])
            tap("accp1", accp_sb[1], [16, SEG], F32, ["accp1"])

        tp_jobs = [(ddt_sb, ddttm_sb, "ddt", "ddttm")]
        if full:
            tp_jobs += [(ea_sb, eat_sb, "ea", "eat"), (dtb16_sb, dttm_sb, "dtb16", "dttm")]
        for (srcs, dst, sname, dname) in tp_jobs:
            for p in range(NPAIR):
                for g in range(2):
                    s.op("tensor", lambda e, srcs=srcs, p=p, g=g: e.transpose(
                        PT[:, (p * NH + g * 16):(p * NH + g * 16) + 16],
                        srcs[g][0:16, p * 128:(p + 1) * 128], ident_sb[0:16, 0:16]),
                        reads=[B(f"{sname}{g}"), B("ident")], writes=[B("psy")],
                        tag=pfx + f"t{sname}{p}{g}")
            s.op("vector", lambda e, dst=dst: e.tensor_copy(out=dst, in_=PT[:, 0:NPAIR * NH]),
                 reads=[B("psy")], writes=[B(dname)], tag=pfx + f"te{sname}")
        for p in range(NPAIR):
            s.op("tensor", lambda e, p=p: e.transpose(
                PT[:, p * 128:(p + 1) * 128], bfm_sb[:, p * 128:(p + 1) * 128], ident_sb),
                reads=[B("bfm"), B("ident")], writes=[B("psy")], tag=pfx + f"tb{p}")
        s.op("vector", lambda e: e.tensor_copy(out=bt_sb, in_=PT[:, 0:NPAIR * 128]),
             reads=[B("psy")], writes=[B("bt")], tag=pfx + "tbe")

        for g in range(2):
            s.op("vector", lambda e, g=g: e.tensor_copy(
                out=ecc_sb[:, g * NCHUNK:(g + 1) * NCHUNK],
                in_=ea_sb[g].rearrange("h (c l) -> h c l", l=CH)[:, :, CH - 1]),
                reads=[B(f"ea{g}")], writes=[B("ecc")], tag=pfx + f"ecc{g}")
        s.op("gpsimd", lambda e: e.dma_start(
            out=ecscr.rearrange("(g h) c -> h g c", g=2),
            in_=ecc_sb.rearrange("h (g c) -> h g c", g=2)),
            reads=[B("ecc")], writes=[B("ecscr")], dma=True, skey="ecw", tag=pfx + "ecw")
        s.op("sync", lambda e: e.dma_start(
            out=ecrow_sb,
            in_=ecscr.rearrange("h c -> (h c)")[None, :]),
            reads=[B("ecscr")], writes=[B("ecrow")], dma=True, skey="ec", tag=pfx + "ec")
        s.op("tensor", lambda e: e.matmul(
            PG[0][:, 0:NCHUNK * NH], onesb_sb[0:1, :], ecrow_sb[0:1, :],
            start=True, stop=True),
            reads=[B("ecrow"), B("onesb")], writes=[B("pg0")], tag=pfx + "ebm")
        s.op("vector", lambda e: e.tensor_copy(out=eb_sb, in_=PG[0][:, 0:NCHUNK * NH]),
             reads=[B("pg0")], writes=[B("eb")], tag=pfx + "ebe")

        s.op("gpsimd", lambda e, si=si: e.dma_start(
            out=xhscr[si].rearrange("(c p) t -> p c t", p=128),
            in_=xhs_sb.rearrange("p (c t) -> p c t", c=16)),
            reads=[B("xhs")], writes=[B(f"xhscr{si}")], dma=True, skey="xhw", tag=pfx + "xhw")
        for p in range(NPAIR):
            s.op("sync", lambda e, p=p, si=si: e.dma_start(
                out=xT_sb[:, p * DIN:(p + 1) * DIN],
                in_=xhscr[si][:, p * 128:(p + 1) * 128], transpose=True),
                reads=[B(f"xhscr{si}")], writes=[B("xT")], dma=True, skey="xT", tag=pfx + f"xT{p}")
        for p in range(NPAIR):
            s.op("vector", lambda e, p=p: e.tensor_tensor(
                out=xddt_sb[:, p * DIN:(p + 1) * DIN].rearrange("p (h q) -> p h q", h=NH),
                in0=xT_sb[:, p * DIN:(p + 1) * DIN].rearrange("p (h q) -> p h q", h=NH),
                in1=ddttm_sb.rearrange("p (r h) -> p r h", r=NPAIR)[:, p, :, None]
                    .broadcast_to([128, NH, HD]),
                op=ALU.mult),
                reads=[B("xT"), B("ddttm")], writes=[B("xddt")], tag=pfx + f"xd{p}")
        if istap:
            tap("eat", eat_sb, [128, NPAIR * NH], BF16, ["eat"])
            tap("dttm", dttm_sb, [128, NPAIR * NH], BF16, ["dttm"])
            tap("ddttm", ddttm_sb, [128, NPAIR * NH], BF16, ["ddttm"])
            tap("xT", xT_sb, [128, NPAIR * DIN], BF16, ["xT"])
            tap("xddt", xddt_sb, [128, NPAIR * DIN], BF16, ["xddt"])
            tap("bt", bt_sb, [128, NPAIR * 128], BF16, ["bt"])
            tap("eb", eb_sb, [128, NCHUNK * NH], BF16, ["eb"])

        if full:
            for g in range(2):
                for hl, src, sname in ((0, acchi_sb[g], f"acchi{g}"), (1, acclo_sb[g], f"acclo{g}")):
                    s.op("gpsimd", lambda e, g=g, hl=hl, src=src: e.dma_start(
                        out=accscr[g * 32 + hl * 16:g * 32 + hl * 16 + 16, :], in_=src),
                        reads=[B(sname)], writes=[B("accscr")], dma=True, skey="ktw", tag=pfx + f"ktw{g}{hl}")

        ablate_pair = (f"pair{phase}" in _K._ABLATE)
        for p in range(NPAIR) if not ablate_pair else []:
            if full:
                for g in range(2):
                    for hl in range(2):
                        s.op("sync", lambda e, g=g, hl=hl, p=p: e.dma_start(
                            out=kt_sb[g].rearrange("r (x h l) -> r x h l", x=2, h=16)
                                [:, hl, :, :],
                            in_=accscr[g * 32 + hl * 16:g * 32 + hl * 16 + 16, :]
                                .rearrange("h (p r l) -> r p h l", p=NPAIR, r=2)[:, p, :, :]),
                            reads=[B("accscr")], writes=[B(f"kt{g}")], dma=True,
                            skey=f"kt{g}", tag=pfx + f"kt{g}{hl}{p}")
                for ci in range(2):
                    s.op("tensor", lambda e, p=p, ci=ci: e.matmul(
                        PST[ci * 64:(ci + 1) * 64, 0:CH],
                        bfm_sb[:, (p * 2 + ci) * CH:(p * 2 + ci + 1) * CH],
                        cfm_sb[:, (p * 2 + ci) * CH:(p * 2 + ci + 1) * CH],
                        start=True, stop=True, skip_group_check=True),
                        reads=[B("bfm"), B("cfm")], writes=[B("pst")], tag=pfx + f"cb{p}{ci}")
                s.op("vector", lambda e: e.tensor_tensor(
                    out=bcm_sb, in0=PST[:, 0:CH], in1=tri_sb, op=ALU.mult),
                    reads=[B("pst"), B("tri")], writes=[B("bcm")], tag=pfx + f"bcm{p}")
            for g in range(2):
                sg = f"{p}{g}"
                if full:
                    ktv = kt_sb[g].rearrange("r (x q) -> r x q", x=2)
                    for half in range(2):
                        cs = slice(half * 512, (half + 1) * 512)
                        s.op("tensor", lambda e, cs=cs, ktv=ktv: e.matmul(
                            PSY[:, cs], ind2_sb, ktv[:, 0, cs],
                            start=True, stop=False, skip_group_check=True),
                            reads=[B(f"kt{g}"), B("ind2")], writes=[B("psy")], tag=pfx + "d1" + sg)
                        s.op("tensor", lambda e, cs=cs, ktv=ktv: e.matmul(
                            PSY[:, cs], ind2_sb, ktv[:, 1, cs],
                            start=False, stop=False, skip_group_check=True),
                            reads=[B(f"kt{g}")], writes=[B("psy")], tag=pfx + "d2" + sg)
                        s.op("tensor", lambda e, p=p, g=g, cs=cs: e.matmul(
                            PSY[:, cs], acchi_sb[g][0:16, p * 128:(p + 1) * 128],
                            indn_sb[:, cs],
                            start=False, stop=False, skip_group_check=True),
                            reads=[B(f"acchi{g}"), B("indn")], writes=[B("psy")], tag=pfx + "d3" + sg)
                        s.op("tensor", lambda e, p=p, g=g, cs=cs: e.matmul(
                            PSY[:, cs], acclo_sb[g][0:16, p * 128:(p + 1) * 128],
                            indn_sb[:, cs],
                            start=False, stop=True, skip_group_check=True),
                            reads=[B(f"acclo{g}"), B("indn")], writes=[B("psy")], tag=pfx + "d4" + sg)
                    s.op("vector", lambda e: e.tensor_scalar_min(out=m2t_sb, in0=PSY, scalar1=0.0),
                         reads=[B("psy")], writes=[B("m2t")], tag=pfx + "dm" + sg)
                    s.op("scalar", lambda e: e.activation(out=lt_sb, in_=m2t_sb, func=AF.Exp),
                         reads=[B("m2t")], writes=[B("lt")], tag=pfx + "lx" + sg)
                    s.op("vector", lambda e, p=p, g=g: e.tensor_tensor(
                        out=lt_sb.rearrange("p (h l) -> p h l", h=16),
                        in0=lt_sb.rearrange("p (h l) -> p h l", h=16),
                        in1=dttm_sb.rearrange("p (r h) -> p r h", r=NPAIR)
                            [:, p, g * 16:(g + 1) * 16, None].broadcast_to([128, 16, CH]),
                        op=ALU.mult),
                        reads=[B("lt"), B("dttm")], writes=[B("lt")], tag=pfx + "ld" + sg)
                    s.op("vector", lambda e: e.tensor_tensor(
                        out=m2t_sb.rearrange("p (h l) -> p h l", h=16),
                        in0=lt_sb.rearrange("p (h l) -> p h l", h=16),
                        in1=bcm_sb[:, None, :].broadcast_to([128, 16, CH]),
                        op=ALU.mult),
                        reads=[B("lt"), B("bcm")], writes=[B("m2t")], tag=pfx + "m2" + sg)
                    if istap and p == 0 and g == 0:
                        tap("m2t", m2t_sb, [128, 1024], BF16, ["m2t"])
                    if istap and p == 0 and g == 1:
                        tap("lt1", lt_sb, [128, 1024], BF16, ["lt"])
                        tap("m2t1", m2t_sb, [128, 1024], BF16, ["m2t"])
                    for h in range(16):
                        hh = g * 16 + h
                        for ci in range(2):
                            s.op("tensor", lambda e, p=p, h=h, hh=hh, ci=ci: e.matmul(
                                PSY[ci * 64:(ci + 1) * 64, h * 64:(h + 1) * 64],
                                m2t_sb.rearrange("p (h l) -> p h l", h=16)[ci * 64:(ci + 1) * 64, h, :],
                                xT_sb[ci * 64:(ci + 1) * 64,
                                      p * DIN + hh * 64:p * DIN + (hh + 1) * 64],
                                start=True, stop=True, skip_group_check=True),
                                reads=[B("m2t"), B("xT")], writes=[B("psy")],
                                tag=pfx + f"yd{h}{ci}" + sg)
                for ci in range(2):
                    cloc = p * 2 + ci
                    if full:
                        for half in range(2):
                            s.op("tensor", lambda e, cloc=cloc, ci=ci, g=g, half=half: e.matmul(
                                PSO[ci * 64:(ci + 1) * 64, half * 512:(half + 1) * 512],
                                cfm_sb[:, cloc * CH:(cloc + 1) * CH],
                                S_sb[g][:, half * 512:(half + 1) * 512],
                                start=True, stop=True, skip_group_check=True),
                                reads=[B("cfm"), B(f"S{g}")], writes=[B("pso")],
                                tag=pfx + f"yo{ci}{half}" + sg)
                    for half in range(2):
                        s.op("tensor", lambda e, p=p, ci=ci, g=g, half=half: e.matmul(
                            PST[:, half * 512:(half + 1) * 512],
                            bt_sb[ci * 64:(ci + 1) * 64, p * 128:(p + 1) * 128],
                            xddt_sb[ci * 64:(ci + 1) * 64,
                                    p * DIN + g * 1024 + half * 512:
                                    p * DIN + g * 1024 + (half + 1) * 512],
                            start=True, stop=True, skip_group_check=True),
                            reads=[B("bt"), B("xddt")], writes=[B("pst")],
                            tag=pfx + f"st{ci}{half}" + sg)
                    s.op("vector", lambda e, cloc=cloc, g=g: e.tensor_tensor(
                        out=S_sb[g].rearrange("p (h q) -> p h q", h=16),
                        in0=S_sb[g].rearrange("p (h q) -> p h q", h=16),
                        in1=eb_sb.rearrange("p (g h c) -> p g h c", g=2, h=16)
                            [:, g, :, cloc, None].broadcast_to([128, 16, HD]),
                        op=ALU.mult),
                        reads=[B("eb")], writes=[B(f"S{g}")], tag=pfx + f"sm{ci}" + sg)
                    s.op("vector", lambda e, g=g: e.tensor_tensor(
                        out=S_sb[g], in0=S_sb[g], in1=PST, op=ALU.add),
                        reads=[B("pst")], writes=[B(f"S{g}")], tag=pfx + f"sa{ci}" + sg)
                if full:
                    s.op("vector", lambda e, p=p, g=g: e.tensor_tensor(
                        out=lt_sb.rearrange("p (h q) -> p h q", h=16),
                        in0=PSO.rearrange("p (h q) -> p h q", h=16),
                        in1=eat_sb.rearrange("p (r h) -> p r h", r=NPAIR)
                            [:, p, g * 16:(g + 1) * 16, None].broadcast_to([128, 16, HD]),
                        op=ALU.mult),
                        reads=[B("pso"), B("eat")], writes=[B("lt")], tag=pfx + "ym" + sg)
                    s.op("vector", lambda e, p=p, g=g: e.tensor_tensor(
                        out=ytm_sb[:, p * DIN + g * 1024:p * DIN + (g + 1) * 1024],
                        in0=lt_sb, in1=PSY, op=ALU.add),
                        reads=[B("lt"), B("psy")], writes=[B("ytm")], tag=pfx + "ya" + sg)
                if _K._BMODE >= 3:
                    s.barrier()

        if not full:
            return
        if istap:
            tap("ytm", ytm_sb, [128, NPAIR * DIN], BF16, ["ytm"])

        s.op("gpsimd", lambda e, si=si: e.dma_start(
            out=yscr[si].rearrange("(r p) c -> p r c", p=128),
            in_=ytm_sb.rearrange("p (r c) -> p r c", r=NPAIR)),
            reads=[B("ytm")], writes=[B(f"yscr{si}")], dma=True, skey="yw", tag=pfx + "yw")
        for cc in range(16):
            s.op("sync", lambda e, cc=cc, si=si: e.dma_start(
                out=yfm_sb[:, cc * SEG:(cc + 1) * SEG],
                in_=yscr[si][:, cc * 128:(cc + 1) * 128], transpose=True),
                reads=[B(f"yscr{si}")], writes=[B("yfmr")], dma=True, skey="yf",
                tag=pfx + f"yf{cc}")
        for cc in range(16):
            s.op("vector", lambda e, cc=cc: e.scalar_tensor_tensor(
                out=yfm_sb[:, cc * SEG:(cc + 1) * SEG],
                in0=xhs_sb[:, cc * SEG:(cc + 1) * SEG],
                scalar=drep_sb[:, cc:cc + 1],
                in1=yfm_sb[:, cc * SEG:(cc + 1) * SEG],
                op0=ALU.mult, op1=ALU.add),
                reads=[B("yfmr"), B("xhs"), B("drep")], writes=[B(f"yfm{cc}")],
                tag=pfx + f"yD{cc}")
        if istap:
            tap("yfm", yfm_sb, [128, 16 * SEG], BF16, [f"yfm{c}" for c in range(16)])

        for tt in range(NPAIR):
            for cc in range(16):
                for half in range(2):
                    s.op("tensor", lambda e, tt=tt, cc=cc, half=half: e.matmul(
                        PSY[:, half * 512:(half + 1) * 512],
                        yfm_sb[:, cc * SEG + tt * 128:cc * SEG + (tt + 1) * 128],
                        wout_sb.rearrange("p (c q) -> p c q", c=16)[:, cc, half * 512:(half + 1) * 512],
                        start=(cc == 0), stop=(cc == 15), skip_group_check=True),
                        reads=[B(f"yfm{cc}"), B("wout")], writes=[B("psy")],
                        tag=pfx + f"op{tt}{cc}{half}")
            s.op("gpsimd", lambda e, tt=tt: e.dma_start(
                out=xrl_sb, in_=xscrd[t0 + tt * 128:t0 + (tt + 1) * 128, :]),
                reads=[B("xscr")], writes=[B("xrl")], dma=True, skey="xrl", tag=pfx + f"xrl{tt}")
            s.op("vector", lambda e, tt=tt: e.tensor_tensor(
                out=pln1_sb, in0=PSY, in1=xrl_sb, op=ALU.add),
                reads=[B("psy"), B("xrl")], writes=[B("pln1")], tag=pfx + f"res{tt}")
            _ln_quant(pfx, tt, t0, istap)

    def _ln_quant(pfx, tt, t0, istap):
        s.op("vector", lambda e: e.tensor_reduce(
            out=st_sb[:, 0:1], in_=pln1_sb, axis=AX.X, op=ALU.add),
            reads=[B("pln1")], writes=[B("st0")], tag=pfx + f"mu{tt}")
        s.op("vector", lambda e: e.tensor_scalar_mul(out=st_sb[:, 0:1], in0=st_sb[:, 0:1],
                                                     scalar1=1.0 / DM),
             reads=[B("st0")], writes=[B("st0")], tag=pfx + f"mu2{tt}")
        s.op("scalar", lambda e: e.activation(
            out=xln1b_sb, in_=pln1_sb, func=AF.Square, accum_out=st_sb[:, 1:2]),
            reads=[B("pln1")], writes=[B("st1"), B("xlnb")], tag=pfx + f"sq{tt}")
        s.op("vector", lambda e: e.tensor_scalar_mul(out=st_sb[:, 1:2], in0=st_sb[:, 1:2],
                                                     scalar1=1.0 / DM),
             reads=[B("st1")], writes=[B("st1")], tag=pfx + f"ex2{tt}")
        s.op("vector", lambda e: e.tensor_tensor(
            out=st_sb[:, 2:3], in0=st_sb[:, 0:1], in1=st_sb[:, 0:1], op=ALU.mult),
            reads=[B("st0")], writes=[B("st2")], tag=pfx + f"m2{tt}")
        s.op("vector", lambda e: e.tensor_sub(
            out=st_sb[:, 2:3], in0=st_sb[:, 1:2], in1=st_sb[:, 2:3]),
            reads=[B("st1"), B("st2")], writes=[B("st2")], tag=pfx + f"var{tt}")
        s.op("scalar", lambda e: e.activation(
            out=st_sb[:, 3:4], in_=st_sb[:, 2:3], func=AF.Ln, bias=epsc_sb[:, 0:1]),
            reads=[B("st2"), B("epsc")], writes=[B("st3")], tag=pfx + f"lns{tt}")
        s.op("scalar", lambda e: e.activation(
            out=st_sb[:, 3:4], in_=st_sb[:, 3:4], func=AF.Exp, scale=-0.5),
            reads=[B("st3")], writes=[B("st3")], tag=pfx + f"rstd{tt}")
        s.op("vector", lambda e: e.tensor_tensor(
            out=st_sb[:, 4:5], in0=st_sb[:, 0:1], in1=st_sb[:, 3:4], op=ALU.mult),
            reads=[B("st0"), B("st3")], writes=[B("st4")], tag=pfx + f"mr{tt}")
        s.op("vector", lambda e: e.tensor_scalar_mul(out=st_sb[:, 4:5], in0=st_sb[:, 4:5],
                                                     scalar1=-1.0),
             reads=[B("st4")], writes=[B("st4")], tag=pfx + f"nmr{tt}")
        s.op("vector", lambda e: e.tensor_scalar(
            out=xln1_sb, in0=pln1_sb, scalar1=st_sb[:, 3:4], scalar2=st_sb[:, 4:5],
            op0=ALU.mult, op1=ALU.add),
            reads=[B("pln1"), B("st3"), B("st4")], writes=[B("xln1")], tag=pfx + f"lna{tt}")
        s.op("vector", lambda e: e.tensor_copy(out=xln1b_sb, in_=xln1_sb),
             reads=[B("xln1")], writes=[B("xlnb")], tag=pfx + f"lnb{tt}")
        s.op("gpsimd", lambda e: e.dma_start(
            out=xlnscr[t0 + tt * 128:t0 + (tt + 1) * 128, :], in_=xln1b_sb),
            reads=[B("xlnb")], writes=[B("xlnscr")], dma=True, skey="lnw", tag=pfx + f"lnw{tt}")
        s.op("vector", lambda e: e.tensor_tensor(
            out=st_sb[:, 5:6], in0=st_sb[:, 3:4], in1=st_sb[:, 3:4], op=ALU.mult),
            reads=[B("st3")], writes=[B("st5")], tag=pfx + f"r2a{tt}")
        s.op("vector", lambda e: e.tensor_tensor(
            out=st_sb[:, 5:6], in0=st_sb[:, 5:6], in1=st_sb[:, 2:3], op=ALU.mult),
            reads=[B("st5"), B("st2")], writes=[B("st5")], tag=pfx + f"r2b{tt}")
        s.op("scalar", lambda e: e.activation(
            out=st_sb[:, 5:6], in_=st_sb[:, 5:6], func=AF.Ln, bias=epsc_sb[:, 0:1]),
            reads=[B("st5"), B("epsc")], writes=[B("st5")], tag=pfx + f"r2c{tt}")
        s.op("scalar", lambda e: e.activation(
            out=st_sb[:, 5:6], in_=st_sb[:, 5:6], func=AF.Exp, scale=-0.5),
            reads=[B("st5")], writes=[B("st5")], tag=pfx + f"r2d{tt}")
        s.op("vector", lambda e: e.reduce_max(
            out=st_sb[:, 6:7], in_=xln1_sb, axis=AX.X, apply_absolute_value=True),
            reads=[B("xln1")], writes=[B("st6")], tag=pfx + f"am{tt}")
        s.op("vector", lambda e: e.tensor_tensor(
            out=st_sb[:, 6:7], in0=st_sb[:, 6:7], in1=st_sb[:, 5:6], op=ALU.mult),
            reads=[B("st6"), B("st5")], writes=[B("st6")], tag=pfx + f"am2{tt}")
        s.op("vector", lambda e: e.tensor_scalar_max(
            out=st_sb[:, 6:7], in0=st_sb[:, 6:7], scalar1=1e-5),
            reads=[B("st6")], writes=[B("st6")], tag=pfx + f"am3{tt}")
        s.op("vector", lambda e: e.reciprocal(out=st_sb[:, 7:8], in_=st_sb[:, 6:7]),
             reads=[B("st6")], writes=[B("st7")], tag=pfx + f"rs{tt}")
        s.op("vector", lambda e: e.tensor_scalar_mul(
            out=st_sb[:, 7:8], in0=st_sb[:, 7:8], scalar1=127.0),
            reads=[B("st7")], writes=[B("st7")], tag=pfx + f"rs2{tt}")
        s.op("vector", lambda e: e.tensor_tensor(
            out=st_sb[:, 8:9], in0=st_sb[:, 7:8], in1=st_sb[:, 5:6], op=ALU.mult),
            reads=[B("st7"), B("st5")], writes=[B("st8")], tag=pfx + f"rs3{tt}")
        s.op("vector", lambda e: e.tensor_scalar_mul(
            out=st_sb[:, 9:10], in0=st_sb[:, 6:7], scalar1=1.0 / 127.0),
            reads=[B("st6")], writes=[B("st9")], tag=pfx + f"inv{tt}")
        s.op("scalar", lambda e: e.activation(
            out=wint_sb, in_=xln1_sb, func=AF.Copy, scale=st_sb[:, 8:9]),
            reads=[B("xln1"), B("st8")], writes=[B("wint")], tag=pfx + f"q1i{tt}")
        s.op("vector", lambda e: e.tensor_scalar_mul(
            out=q1o_sb, in0=wint_sb, scalar1=st_sb[:, 9:10]),
            reads=[B("wint"), B("st9")], writes=[B("wqb")], tag=pfx + f"q1o{tt}")
        s.op("gpsimd", lambda e: e.dma_start(
            out=xq1scr[t0 + tt * 128:t0 + (tt + 1) * 128, :], in_=q1o_sb),
            reads=[B("wqb")], writes=[B("xq1scr")], dma=True, skey="q1w", tag=pfx + f"q1w{tt}")
        if istap and tt == 0:
            tap("st", st_sb, [128, 16], F32, ["st0", "st1", "st2", "st3", "st4", "st5", "st6", "st7", "st8", "st9"])
            tap("pln1", pln1_sb, [128, DM], F32, ["pln1"])
            tap("xln1", xln1_sb, [128, DM], F32, ["xln1"])
            tap("q1o", q1o_sb, [128, DM], BF16, ["wqb"])

    t1_tiles = wq_tiles(NF, DM)
    t2_tiles = wq_tiles(DM, NF)
    for seg in range(NSEG):
        attn_segment(seg, 1)
        wq_p1("w1", w1_d, t1_tiles, range(seg * 4, seg * 4 + 4))
        if _K._BARRIER_SEGS:
            s.barrier()
    wq_mean("w1", NF, DM, len(t1_tiles))
    for g in range(2):
        s.op("vector", lambda e, g=g: e.tensor_scalar_mul(
            out=S_sb[g], in0=S_sb[g], scalar1=gate_sb[:, 0:1]),
            reads=[B(f"S{g}"), B("gate")], writes=[B(f"S{g}")], tag=f"gateS{g}")
    s.op("vector", lambda e: e.tensor_scalar_mul(
        out=halo_sb, in0=halo_sb, scalar1=gate_sb[:, 0:1]),
        reads=[B("halo"), B("gate")], writes=[B("halo")], tag="gateH")
    tap("Sg0", S_sb[0], [128, 1024], BF16, ["S0"])
    tap("acol0", acol_sb[0], [16, 1], F32, ["acol0"])
    tap("acol1", acol_sb[1], [16, 1], F32, ["acol1"])
    for seg in range(NSEG):
        attn_segment(seg, 2)
        wq_p2("w1", w1_d, wq1scr, "wq1scr", t1_tiles, range(seg * 4, seg * 4 + 4))
        if seg < 4:
            wq_p1("w2", w2_d, t2_tiles, range(seg * 8, seg * 8 + 8))
            if seg == 3:
                wq_mean("w2", DM, NF, len(t2_tiles))
        else:
            wq_p2("w2", w2_d, wq2scr, "wq2scr", t2_tiles,
                  range((seg - 4) * 8, (seg - 4) * 8 + 8))
        if _K._BARRIER_SEGS:
            s.barrier()

    aes.close()
    s.barrier()
    P3 = lambda n, s_, d: TT_(es, n, s_, d)
    wq1t_sb = P3("wq1t_sb", [128, 8 * NF], BF16)
    b1bc_sb = P3("b1bc_sb", [128, NF], BF16)
    b2bc_sb = P3("b2bc_sb", [128, DM], BF16)
    bcast_row(b1row_sb, "b1row", NF, b1bc_sb, "b1bc")
    bcast_row(b2row_sb, "b2row", DM, b2bc_sb, "b2bc")
    wq2t_sb = P3("wq2t_sb", [128, 32 * DM], BF16)
    xq1f_sb = P3("xq1f_sb", [128, 8 * SEG], BF16)
    xlnt_sb = P3("xlnt_sb", [128, NPAIR * DM], BF16)
    hpre_sb = P3("hpre_sb", [128, NF], BF16)
    q2i_sb = P3("q2i_sb", [128, 1024], I32)
    q2o_sb = P3("q2o_sb", [128, 1024], BF16)
    xq2f_sb = P3("xq2f_sb", [128, 32 * SEG], BF16)
    pln2_sb = pln1_sb
    rb2_sb = xln1b_sb
    outr_sb = xln1_sb

    tap("wq1scr", wq1scr, [NF, DM], BF16, ["wq1scr"])
    for k in range(8):
        s.op("sync", lambda e, k=k: e.dma_start(
            out=wq1t_sb.rearrange("p (k c) -> p k c", k=8)[:, k, :],
            in_=wq1scr[:, k * 128:(k + 1) * 128], transpose=True),
            reads=[B("wq1scr")], writes=[B("wq1t")], dma=True, skey="wq1t",
            tag=f"wq1t{k}")
    for k in range(32):
        s.op("sync", lambda e, k=k: e.dma_start(
            out=wq2t_sb.rearrange("p (k c) -> p k c", k=32)[:, k, :],
            in_=wq2scr[:, k * 128:(k + 1) * 128], transpose=True),
            reads=[B("wq2scr")], writes=[B("wq2t")], dma=True, skey="wq2t",
            tag=f"wq2t{k}")

    def ffn_segment(seg):
        t0 = seg * SEG
        si = seg % 2
        pfx = f"p3s{seg}_"
        istap = seg == 0
        for k in range(8):
            s.op("sync", lambda e, k=k: e.dma_start(
                out=xq1f_sb[:, k * SEG:(k + 1) * SEG],
                in_=xq1scr[t0:t0 + SEG, k * 128:(k + 1) * 128], transpose=True),
                reads=[B("xq1scr")], writes=[B("xq1f")], dma=True, skey="xq1f",
                tag=pfx + f"q1f{k}")
        s.op("gpsimd", lambda e: e.dma_start(
            out=xlnt_sb.rearrange("p (r c) -> p r c", r=NPAIR),
            in_=xlnscr.rearrange("(r p) c -> p r c", p=128)[:, NPAIR * seg:NPAIR * (seg + 1), :]),
            reads=[B("xlnscr")], writes=[B("xlnt")], dma=True, skey="xlnt", tag=pfx + "xlnt")
        for tt in range(NPAIR):
            for fc in range(8):
                pid = fc % 2
                for k in range(8):
                    s.op("tensor", lambda e, tt=tt, fc=fc, k=k, pid=pid: e.matmul(
                        PG[pid][:, 0:512],
                        xq1f_sb[:, k * SEG + tt * 128:k * SEG + (tt + 1) * 128],
                        wq1t_sb.rearrange("p (k c) -> p k c", k=8)[:, k, fc * 512:(fc + 1) * 512],
                        start=(k == 0), stop=(k == 7)),
                        reads=[B("xq1f"), B("wq1t")], writes=[B(f"pg{pid}")],
                        tag=pfx + f"m1_{tt}{fc}{k}")
                s.op("vector", lambda e, fc=fc, pid=pid: e.scalar_tensor_tensor(
                    out=hpre_sb[:, fc * 512:(fc + 1) * 512],
                    in0=PG[pid][:, 0:512], scalar=sw_mean["w1"][:, 0:1],
                    in1=b1bc_sb[:, fc * 512:(fc + 1) * 512],
                    op0=ALU.mult, op1=ALU.add),
                    reads=[B(f"pg{pid}"), B("w1mean"), B("b1bc")], writes=[B("hpre")],
                    tag=pfx + f"e1_{tt}{fc}")
            if istap and tt == 0:
                tap("hpre", hpre_sb, [128, NF], BF16, ["hpre"])
                tap("xq1f", xq1f_sb, [128, 8 * SEG], BF16, ["xq1f"])
                tap("b1bc", b1bc_sb, [128, NF], BF16, ["b1bc"])
                tap("swm1", sw_mean["w1"], [128, 1], F32, ["w1mean"])
                tap("sws1", sw_scale["w1"], [128, 1], F32, ["w1sc"])
                tap("wq1t", wq1t_sb, [128, 8 * NF], BF16, ["wq1t"])
            s.op("scalar", lambda e: e.activation(out=hpre_sb, in_=hpre_sb, func=(AF.Relu if _K._NOSILU else AF.Gelu)),
                 reads=[B("hpre")], writes=[B("hpre")], tag=pfx + f"gel{tt}")
            s.op("scalar", lambda e: e.activation(
                out=xq2f_sb[:, 0:NF], in_=hpre_sb, func=AF.Square, accum_out=st_sb[:, 1:2]),
                reads=[B("hpre")], writes=[B("st1"), B("xq2f")], tag=pfx + f"sq{tt}")
            s.op("vector", lambda e: e.reduce_max(
                out=st_sb[:, 6:7], in_=hpre_sb, axis=AX.X, apply_absolute_value=True),
                reads=[B("hpre")], writes=[B("st6")], tag=pfx + f"am{tt}")
            s.op("vector", lambda e: e.tensor_scalar_mul(
                out=st_sb[:, 1:2], in0=st_sb[:, 1:2], scalar1=1.0 / NF),
                reads=[B("st1")], writes=[B("st1")], tag=pfx + f"ms{tt}")
            s.op("scalar", lambda e: e.activation(
                out=st_sb[:, 5:6], in_=st_sb[:, 1:2], func=AF.Ln, bias=epsc_sb[:, 0:1]),
                reads=[B("st1"), B("epsc")], writes=[B("st5")], tag=pfx + f"r2a{tt}")
            s.op("scalar", lambda e: e.activation(
                out=st_sb[:, 5:6], in_=st_sb[:, 5:6], func=AF.Exp, scale=-0.5),
                reads=[B("st5")], writes=[B("st5")], tag=pfx + f"r2b{tt}")
            s.op("vector", lambda e: e.tensor_tensor(
                out=st_sb[:, 6:7], in0=st_sb[:, 6:7], in1=st_sb[:, 5:6], op=ALU.mult),
                reads=[B("st6"), B("st5")], writes=[B("st6")], tag=pfx + f"am2{tt}")
            s.op("vector", lambda e: e.tensor_scalar_max(
                out=st_sb[:, 6:7], in0=st_sb[:, 6:7], scalar1=1e-5),
                reads=[B("st6")], writes=[B("st6")], tag=pfx + f"am3{tt}")
            s.op("vector", lambda e: e.reciprocal(out=st_sb[:, 7:8], in_=st_sb[:, 6:7]),
                 reads=[B("st6")], writes=[B("st7")], tag=pfx + f"rs{tt}")
            s.op("vector", lambda e: e.tensor_scalar_mul(
                out=st_sb[:, 7:8], in0=st_sb[:, 7:8], scalar1=127.0),
                reads=[B("st7")], writes=[B("st7")], tag=pfx + f"rs2{tt}")
            s.op("vector", lambda e: e.tensor_tensor(
                out=st_sb[:, 8:9], in0=st_sb[:, 7:8], in1=st_sb[:, 5:6], op=ALU.mult),
                reads=[B("st7"), B("st5")], writes=[B("st8")], tag=pfx + f"rs3{tt}")
            s.op("vector", lambda e: e.tensor_scalar_mul(
                out=st_sb[:, 9:10], in0=st_sb[:, 6:7], scalar1=1.0 / 127.0),
                reads=[B("st6")], writes=[B("st9")], tag=pfx + f"inv{tt}")
            for fh in range(4):
                s.op("scalar", lambda e, fh=fh: e.activation(
                    out=q2i_sb, in_=hpre_sb[:, fh * 1024:(fh + 1) * 1024],
                    func=AF.Copy, scale=st_sb[:, 8:9]),
                    reads=[B("hpre"), B("st8")], writes=[B("q2i")], tag=pfx + f"q2i{tt}{fh}")
                s.op("vector", lambda e, fh=fh: e.tensor_scalar_mul(
                    out=q2o_sb, in0=q2i_sb, scalar1=st_sb[:, 9:10]),
                    reads=[B("q2i"), B("st9")], writes=[B("q2o")], tag=pfx + f"q2o{tt}{fh}")
                s.op("gpsimd", lambda e, tt=tt, si=si, fh=fh: e.dma_start(
                    out=xq2scr[si][tt * 128:(tt + 1) * 128, fh * 1024:(fh + 1) * 1024],
                    in_=q2o_sb),
                    reads=[B("q2o")], writes=[B(f"xq2scr{si}")], dma=True, skey=f"q2w{si}", tag=pfx + f"q2w{tt}{fh}")
            if istap and tt == 0:
                tap("h", hpre_sb, [128, NF], BF16, ["hpre"])
                tap("q2o", q2o_sb, [128, 1024], BF16, ["q2o"])
        for fc in range(32):
            s.op("sync", lambda e, fc=fc, si=si: e.dma_start(
                out=xq2f_sb[:, fc * SEG:(fc + 1) * SEG],
                in_=xq2scr[si][:, fc * 128:(fc + 1) * 128], transpose=True),
                reads=[B(f"xq2scr{si}")], writes=[B("xq2f")], dma=True, skey="xq2f",
                tag=pfx + f"q2f{fc}")
        for tt in range(NPAIR):
            for half in range(2):
                for k in range(32):
                    s.op("tensor", lambda e, tt=tt, half=half, k=k: e.matmul(
                        PSY[:, half * 512:(half + 1) * 512],
                        xq2f_sb[:, k * SEG + tt * 128:k * SEG + (tt + 1) * 128],
                        wq2t_sb.rearrange("p (k c) -> p k c", k=32)[:, k, half * 512:(half + 1) * 512],
                        start=(k == 0), stop=(k == 31), skip_group_check=True),
                        reads=[B("xq2f"), B("wq2t")], writes=[B("psy")],
                        tag=pfx + f"m2_{tt}{half}{k}")
            s.op("vector", lambda e, tt=tt: e.tensor_tensor(
                out=rb2_sb, in0=xlnt_sb[:, tt * DM:(tt + 1) * DM], in1=b2bc_sb, op=ALU.add),
                reads=[B("xlnt"), B("b2bc")], writes=[B("rb2")], tag=pfx + f"rb{tt}")
            s.op("vector", lambda e: e.scalar_tensor_tensor(
                out=pln2_sb, in0=PSY, scalar=sw_mean["w2"][:, 0:1], in1=rb2_sb,
                op0=ALU.mult, op1=ALU.add),
                reads=[B("psy"), B("w2mean"), B("rb2")], writes=[B("pln2")], tag=pfx + f"e2{tt}")
            s.op("vector", lambda e: e.tensor_reduce(
                out=st_sb[:, 0:1], in_=pln2_sb, axis=AX.X, op=ALU.add),
                reads=[B("pln2")], writes=[B("st0")], tag=pfx + f"mu{tt}")
            s.op("vector", lambda e: e.tensor_scalar_mul(
                out=st_sb[:, 0:1], in0=st_sb[:, 0:1], scalar1=1.0 / DM),
                reads=[B("st0")], writes=[B("st0")], tag=pfx + f"mu2{tt}")
            s.op("scalar", lambda e: e.activation(
                out=rb2_sb, in_=pln2_sb, func=AF.Square, accum_out=st_sb[:, 1:2]),
                reads=[B("pln2")], writes=[B("st1"), B("rb2")], tag=pfx + f"sq2{tt}")
            s.op("vector", lambda e: e.tensor_scalar_mul(
                out=st_sb[:, 1:2], in0=st_sb[:, 1:2], scalar1=1.0 / DM),
                reads=[B("st1")], writes=[B("st1")], tag=pfx + f"ex2{tt}")
            s.op("vector", lambda e: e.tensor_tensor(
                out=st_sb[:, 2:3], in0=st_sb[:, 0:1], in1=st_sb[:, 0:1], op=ALU.mult),
                reads=[B("st0")], writes=[B("st2")], tag=pfx + f"m22{tt}")
            s.op("vector", lambda e: e.tensor_sub(
                out=st_sb[:, 2:3], in0=st_sb[:, 1:2], in1=st_sb[:, 2:3]),
                reads=[B("st1"), B("st2")], writes=[B("st2")], tag=pfx + f"var{tt}")
            s.op("scalar", lambda e: e.activation(
                out=st_sb[:, 3:4], in_=st_sb[:, 2:3], func=AF.Ln, bias=epsc_sb[:, 0:1]),
                reads=[B("st2"), B("epsc")], writes=[B("st3")], tag=pfx + f"ln2{tt}")
            s.op("scalar", lambda e: e.activation(
                out=st_sb[:, 3:4], in_=st_sb[:, 3:4], func=AF.Exp, scale=-0.5),
                reads=[B("st3")], writes=[B("st3")], tag=pfx + f"rstd{tt}")
            s.op("vector", lambda e: e.tensor_tensor(
                out=st_sb[:, 4:5], in0=st_sb[:, 0:1], in1=st_sb[:, 3:4], op=ALU.mult),
                reads=[B("st0"), B("st3")], writes=[B("st4")], tag=pfx + f"mr{tt}")
            s.op("vector", lambda e: e.tensor_scalar_mul(
                out=st_sb[:, 4:5], in0=st_sb[:, 4:5], scalar1=-1.0),
                reads=[B("st4")], writes=[B("st4")], tag=pfx + f"nmr{tt}")
            s.op("vector", lambda e: e.tensor_scalar(
                out=outr_sb, in0=pln2_sb, scalar1=st_sb[:, 3:4], scalar2=st_sb[:, 4:5],
                op0=ALU.mult, op1=ALU.add),
                reads=[B("pln2"), B("st3"), B("st4")], writes=[B("outr")], tag=pfx + f"lna{tt}")
            s.op("gpsimd", lambda e, tt=tt: e.dma_start(
                out=out_d[t0 + tt * 128:t0 + (tt + 1) * 128, :], in_=outr_sb),
                reads=[B("outr")], writes=[B("outd")], dma=True, skey="ow", tag=pfx + f"ow{tt}")

    for seg in range(NSEG):
        ffn_segment(seg)
        if _K._BARRIER_SEGS:
            s.barrier()
    s.op("gpsimd", lambda e: e.nop(), extra_waits=[B("outd").writer], tag="final")

    s.emit(nc, block, sems)
    es.close()
    return tap_d


def _host_inputs(inputs):
    x = np.asarray(inputs["L0_attn_x"])
    win = np.ascontiguousarray(np.asarray(inputs["L0_attn_in_proj_w"])[:, 2048:4384])
    convw = np.asarray(inputs["L0_attn_conv_w"])[:, 0, :]
    convw_r = np.zeros((128, 72), np.float32)
    for cc in range(CONVC):
        for k in range(4):
            convw_r[:, cc * 4 + k] = convw[k, cc * 128:(cc + 1) * 128]
    convb = np.asarray(inputs["L0_attn_conv_b"]).reshape(CONVC, 128).T.copy()
    dtb = np.asarray(inputs["L0_attn_dt_bias"])
    alog = np.asarray(inputs["L0_attn_A_log"])
    Dv = np.asarray(inputs["L0_attn_D"])
    drep = np.repeat(Dv, HD).reshape(16, 128).T.copy()
    wout = np.asarray(inputs["L0_attn_out_proj_w"])
    w1 = np.asarray(inputs["L0_ffn_w1"])
    b1 = np.asarray(inputs["L0_ffn_b1"]).reshape(1, NF)
    w2 = np.asarray(inputs["L0_ffn_w2"])
    b2 = np.asarray(inputs["L0_ffn_b2"]).reshape(1, DM)

    ident = np.eye(128, dtype=np.float32).astype(BF)
    tri64 = (np.arange(CH)[None, :] >= np.arange(CH)[:, None]).astype(np.float32)
    tri = np.concatenate([tri64, tri64], axis=0).astype(BF)
    ind2 = np.zeros((2, 128), np.float32)
    ind2[0, 0:64] = 1.0; ind2[1, 64:128] = 1.0
    ind2 = ind2.astype(BF)
    indn = np.zeros((16, 1024), np.float32)
    for h in range(16):
        indn[h, h * 64:(h + 1) * 64] = -1.0
    indn = indn.astype(BF)
    maska = np.ones((16, SEG), np.float32)
    maska[:, ::CH] = 0.0
    onesb = np.ones((1, 128), np.float32).astype(BF)

    common = dict(
        w_in=np.ascontiguousarray(win, np.float32), convw=convw_r,
        convb=np.ascontiguousarray(convb, np.float32),
        dtb0=dtb[0:16].reshape(16, 1).astype(np.float32),
        dtb1=dtb[16:32].reshape(16, 1).astype(np.float32),
        alog0=alog[0:16].reshape(16, 1).astype(np.float32),
        alog1=alog[16:32].reshape(16, 1).astype(np.float32),
        drep=np.ascontiguousarray(drep, np.float32),
        w_out=np.ascontiguousarray(wout, np.float32),
        w1=np.ascontiguousarray(w1, np.float32), b1r=np.ascontiguousarray(b1, np.float32),
        w2=np.ascontiguousarray(w2, np.float32), b2r=np.ascontiguousarray(b2, np.float32),
        identb=ident, tri=tri, ind2=ind2, indn16=indn,
        maska=maska, onesb=onesb, epsc=np.full((128, 1), EPS, np.float32), tinyc=np.full((16, 1), 1e-30, np.float32),
    )
    in_maps = []
    for c in range(8):
        b, r = c // 2, c % 2
        m = dict(common)
        m["xpre"] = np.ascontiguousarray(x[b, 0:TP], np.float32)
        m["xown"] = np.ascontiguousarray(x[b, r * TP:(r + 1) * TP], np.float32)
        m["gate"] = np.full((128, 1), float(r), np.float32)
        in_maps.append(m)
    return in_maps


def kernel(**inputs):
    global _LAST_RES
    in_maps = _host_inputs(inputs)
    nc = bass.Bass("TRN2", target_bir_lowering=False)
    build_program(nc, taps=_TAPS)
    res = run_bass_kernel_spmd(nc, in_maps, core_ids=list(range(8)))
    _LAST_RES = res
    out = np.zeros((4, 4096, DM), np.float32)
    for c in range(8):
        b, r = c // 2, c % 2
        out[b, r * TP:(r + 1) * TP] = res.results[c]["out"]
    return out
